# version 04
# baseline (speedup 1.0000x reference)
import numpy as np
import ml_dtypes
import concourse.bass as bass
import concourse.mybir as mybir
from concourse.bass_utils import run_bass_kernel_spmd
from contextlib import ExitStack

F32 = mybir.dt.float32
BF16 = mybir.dt.bfloat16
AF = mybir.ActivationFunctionType
ALU = mybir.AluOpType
AX = mybir.AxisListType

NCORES = 8
ALV = 99
ASUB = 99
BC = 2
L = 2048
T = BC * L
D = 1024
H = 16
DH = 64
MLP = 4096
KC = 8
HC = 32
NBLK = 8
BLK = 512
NSUB = 4
NT = 32
EPS_LN = 1e-5
EPS_RMS = float(np.finfo(np.float32).eps)
EPS_A = 1e-6
CNORM = float(1.0 / np.sqrt(np.sqrt(DH)))

ENGINES = ("sync", "tensor", "vector", "scalar", "gpsimd")
DEBUG = set()


class Prog:

    NPOOL = 14
    GPOOL = 4
    APOOL = 2

    def __init__(self, nc):
        self.nc = nc
        self.ops = {e: [] for e in ENGINES}
        self.sems = {}
        self.cnt = {}
        self.ctx = ExitStack()
        self.gates = {e: [] for e in ENGINES}
        for s in ("PE", "AC", "DV", "PL"):
            self.sem(s)
        for i in range(self.NPOOL):
            self.sem(f"Q{i}")
        for i in range(self.GPOOL):
            self.sem(f"G{i}")
        for i in range(self.APOOL):
            self.sem(f"A{i}")
        self.dma_keys = {}
        self.phase = "init"

    def dma(self, queue, key, out, in_, w=(), **kw):
        pool, npool = {"gpsimd": ("G", self.GPOOL), "sync": ("Q", self.NPOOL),
                       "scalar": ("A", self.APOOL)}[queue]
        k = (queue, key)
        if k not in self.dma_keys:
            used = sum(1 for (q, _) in self.dma_keys if q == queue)
            assert used < npool, f"dma sem pool exhausted for {key}"
            self.dma_keys[k] = f"{pool}{used}"
        s = self.dma_keys[k]
        v = self.op(queue, "dma_start", w=w, inc=(s, 16), out=out, in_=in_, **kw)
        return (s, v)

    def sem(self, name):
        if name not in self.sems:
            self.sems[name] = self.ctx.enter_context(self.nc.semaphore(name))
            self.cnt[name] = 0
        return self.sems[name]

    _ENG_SEM = {"vector": "DV", "scalar": "AC", "gpsimd": "PL"}

    def op(self, eng, meth, *args, w=(), inc=None, **kw):
        for x in w:
            if x is not None:
                assert x[1] <= self.cnt[x[0]], \
                    f"future wait {x} > {self.cnt[x[0]]} ({eng} {meth})"
        waits = self.gates[eng] + [x for x in w if x is not None and x[1] > 0]
        es = self._ENG_SEM.get(eng)
        if es is not None and meth != "dma_start":
            if self.cnt[es] > 0:
                waits.append((es, self.cnt[es]))
            if inc is None:
                inc = (es, 1)
        self.gates[eng] = []
        rec = [meth, args, kw, waits, [], self.phase]
        self.ops[eng].append(rec)
        if inc is not None:
            s, n = inc
            self.cnt[s] += n
            rec[4].append((s, n))
            return self.cnt[s]
        return None

    def wait_only(self, eng, w):
        waits = self.gates[eng] + [x for x in w if x is not None and x[1] > 0]
        self.gates[eng] = []
        if waits:
            self.ops[eng].append([None, (), {}, waits, [], self.phase])

    def close(self, eng, semname):
        for rec in reversed(self.ops[eng]):
            if rec[0] is None:
                continue
            if rec[4]:
                return self.cnt[semname]
            self.cnt[semname] += 1
            rec[4].append((semname, 1))
            return self.cnt[semname]
        return self.cnt[semname]

    def val(self, name):
        return self.cnt[name]

    def gate_all(self):
        for e in ("tensor", "vector", "scalar", "gpsimd"):
            self.close(e, {"tensor": "PE", "vector": "DV", "scalar": "AC",
                           "gpsimd": "PL"}[e])
        snap = [(s, v) for s, v in self.cnt.items() if v > 0]
        for e in ENGINES:
            self.gates[e] = list(snap)
        self.dma_keys = {}

    def finish(self):
        sems = self.sems

        nc = self.nc

        def runner(recs):
            def f(engobj):
                cur = [None, None]

                def setph(ph):
                    if ph == cur[0]:
                        return
                    if cur[0] is not None:
                        nc.leave_named_scope(cur[0], cur[1], False)
                    sid, _ = nc.enter_named_scope(ph, False)
                    cur[0], cur[1] = ph, sid
                for meth, args, kw, waits, incs, ph in recs:
                    setph(ph)
                    for s, v in waits:
                        engobj.wait_ge(sems[s], v)
                    if meth is None:
                        continue
                    ins = getattr(engobj, meth)(*args, **kw)
                    for s, n in incs:
                        ins.then_inc(sems[s], n)
                if cur[0] is not None:
                    nc.leave_named_scope(cur[0], cur[1], False)
            return f

        blk = self.ctx.enter_context(self.nc.Block())
        blk.sync(runner(self.ops["sync"]))
        blk.tensor(runner(self.ops["tensor"]))
        blk.vector(runner(self.ops["vector"]))
        blk.scalar(runner(self.ops["scalar"]))
        blk.gpsimd(runner(self.ops["gpsimd"]))


class Ring:

    def __init__(self, n):
        self.n = n
        self.free = [[] for _ in range(n)]

    def waits(self, i):
        return list(self.free[i % self.n])

    def done(self, i, *sv):
        self.free[i % self.n] = [x for x in sv if x is not None]


def build_nc(seq=("attn_a", "ffn", "attn_b", "ffn")):
    nc = bass.Bass("TRN2", target_bir_lowering=False)
    p = Prog(nc)

    def din(name, shape):
        return nc.dram_tensor(name, list(shape), F32, kind="ExternalInput").ap()

    x_in = din("x", (T, D))
    y_out = nc.dram_tensor("y", [T, D], F32, kind="ExternalOutput").ap()

    aWq = din("aWq", (D, D)); aWk = din("aWk", (D, D)); aWv = din("aWv", (D, D))
    aWpq = din("aWpq", (D, D)); aWpk = din("aWpk", (D, D))
    a_brows = din("a_brows", (1, 5 * D))
    a_lnq = din("a_lnq", (128, 2 * KC))
    a_lnk = din("a_lnk", (128, 2 * KC))
    a_lnv_bc = din("a_lnv_bc", (128, 2 * D))
    a_lna_bc = din("a_lna_bc", (128, 2 * D))
    aW1 = din("aW1", (D, MLP)); a_b1c = din("a_b1c", (1, 128, HC))
    aW2 = din("aW2", (MLP, D)); a_b2r = din("a_b2r", (1, 1, D))
    a_gam = din("a_gam", (128, 1))
    a_cbc = din("a_cbc", (128, KC))
    a_cw = din("a_cw", (KC * 5, 128, 128))
    bWqT = din("bWqT", (D, D)); bWkT = din("bWkT", (D, D))
    bWvT = din("bWvT", (D, D)); bWoT = din("bWoT", (D, D))
    b_bqc = din("b_bqc", (128, KC)); b_bkc = din("b_bkc", (128, KC))
    b_bvr = din("b_bvr", (1, D)); b_bor = din("b_bor", (1, D))
    bPt = din("bPt", (DH, DH))
    fW1s = din("fW1s", (2, D, MLP)); fW2s = din("fW2s", (2, MLP, D))
    fb1s = din("fb1s", (2, 128, HC)); fb2s = din("fb2s", (2, 1, D))
    fW1r = din("fW1r", (8, D, MLP)); fW2r = din("fW2r", (8, MLP, D))
    fb1r = din("fb1r", (8, 128, HC)); fb2r = din("fb2r", (8, 1, D))
    f_gs = din("f_gs", (128, KC)); f_gr = din("f_gr", (128, KC))
    f_cen = din("f_cen", (D, 8))
    c_ident = din("c_ident", (128, 128))
    c_onesr = din("c_onesr", (1, 512))

    XR = nc.dram_tensor("XR", [T, D], F32).ap()
    SP_TQ = nc.dram_tensor("SP_TQ", [D, T], BF16).ap()
    SP_K = nc.dram_tensor("SP_K", [D, T], BF16).ap()
    SP_V = nc.dram_tensor("SP_V", [T, D], BF16).ap()
    SP_RN = nc.dram_tensor("SP_RN", [D, T], BF16).ap()
    SP_AT = nc.dram_tensor("SP_AT", [T, D], F32).ap()
    PADL = L + 4
    SP_AF = nc.dram_tensor("SP_AF", [D, BC * PADL], BF16).ap()
    SP_PT = nc.dram_tensor("SP_PT", [T, D], F32).ap()
    WBF = nc.dram_tensor("WBF", [10 * 128, 2 * KC * MLP], BF16).ap()
    SP_PF = nc.dram_tensor("SP_PF", [D, T], BF16).ap()
    BNC = nc.dram_tensor("BNC", [64, 64], F32).ap()

    def fmv(sp, blk):
        return sp.rearrange("(c p) t -> p c t", p=128)[:, :, blk * BLK:(blk + 1) * BLK]

    def tmv(sp, blk):
        return sp[blk * BLK:(blk + 1) * BLK, :].rearrange("(c p) d -> p c d", p=128)

    sbt = p.ctx.enter_context
    identf = sbt(nc.sbuf_tensor("identf", [128, 128], F32))
    ident16 = sbt(nc.sbuf_tensor("ident16", [128, 128], BF16))
    ones16c = sbt(nc.sbuf_tensor("ones16c", [128, 1], BF16))
    ones16r = sbt(nc.sbuf_tensor("ones16r", [1, 512], BF16))
    junk = sbt(nc.sbuf_tensor("junk", [128, D], F32))
    cvals = sbt(nc.sbuf_tensor("cvals", [128, 3], F32))

    v = p.dma("sync", "ident", identf[:], c_ident)
    p.op("vector", "tensor_copy", ident16[:], identf[:], w=[v], inc=("DV", 1))
    v = p.dma("sync", "ones", junk[0:1, 0:512], c_onesr)
    p.op("vector", "tensor_copy", ones16r[:], junk[0:1, 0:512], w=[v],
         inc=("DV", 1))
    p.op("vector", "memset", ones16c[:], 1.0, inc=("DV", 1))
    p.op("vector", "memset", cvals[:, 0:1], EPS_LN, inc=("DV", 1))
    p.op("vector", "memset", cvals[:, 1:2], EPS_RMS, inc=("DV", 1))
    p.op("vector", "memset", cvals[:, 2:3], CNORM, inc=("DV", 1))
    c_epsln = cvals[:, 0:1]
    c_epsrms = cvals[:, 1:2]
    c_cnorm = cvals[:, 2:3]
    p.gate_all()

    def stream_w_bf16(wap_chunks, dst16, stage, gain=None, gate_pe=0, nc_chunks=8,
                      chunk_cols=D, conv="pool", nslot=2, gate_w=()):
        frees = [None] * nslot
        last = {}
        for j in range(nc_chunks):
            slot = j % nslot
            wl = []
            if j == 0 and gate_pe:
                wl.append(("PE", gate_pe))
            if j < nslot:
                wl += list(gate_w)
            if frees[slot]:
                wl.append(frees[slot])
            src = wap_chunks(j)
            stg = stage[:, slot * chunk_cols:(slot + 1) * chunk_cols]
            if len(src.shape) == 3:
                stg = stg.rearrange("p (c f) -> p c f", f=src.shape[2])
            lv = p.dma("gpsimd", f"wst{slot}", stg, src, w=wl)
            eng = ("gpsimd" if conv == "pool" or (conv == "mix" and j % 4 == 3)
                   else "vector")
            sem = "PL" if eng == "gpsimd" else "DV"
            dsl = dst16[:, j * chunk_cols:(j + 1) * chunk_cols]
            sstg = stage[:, slot * chunk_cols:(slot + 1) * chunk_cols]
            if gain is not None:
                v = p.op(eng, "tensor_scalar", dsl, sstg, gain(j), None,
                         ALU.mult, w=[lv], inc=(sem, 1))
            else:
                v = p.op(eng, "tensor_copy", dsl, sstg, w=[lv], inc=(sem, 1))
            frees[slot] = (sem, v)
            last[eng] = (sem, v)
        return list(last.values())

    jfree_g = [0]

    def ln_stats(p_, tile, ssum, ssq, wx):
        p.op("vector", "tensor_reduce", ssum, tile, AX.X, ALU.add, w=wx,
             inc=("DV", 1))
        n = tile.shape[-1]
        vsq = p.op("scalar", "activation", junk[:, 0:n], tile, AF.Square,
                   w=list(wx) + [("DV", jfree_g[0])], inc=("AC", 1))
        jfree_g[0] = p.op("vector", "tensor_reduce", ssq, junk[:, 0:n], AX.X,
                          ALU.add, w=[("AC", vsq)], inc=("DV", 1))

    def ln_finish(stc, i0, n, extra=None):
        sl = (slice(None), slice(i0, i0 + n))
        wdv = [("DV", p.val("DV"))]
        wac = [("AC", p.val("AC"))]
        p.op("vector", "tensor_scalar", stc["mu"][sl], stc["su"][sl], 1.0 / D,
             None, ALU.mult, w=wdv, inc=("DV", 1))
        p.op("vector", "tensor_scalar", stc["t1"][sl], stc["sq"][sl], 1.0 / D,
             None, ALU.mult, w=wac, inc=("DV", 1))
        p.op("vector", "tensor_tensor", stc["t2"][sl], stc["mu"][sl], stc["mu"][sl],
             ALU.mult, inc=("DV", 1))
        vdv = p.op("vector", "tensor_tensor", stc["t1"][sl], stc["t1"][sl],
                   stc["t2"][sl], ALU.subtract, inc=("DV", 1))
        if "lnf_copy" in DEBUG:
            p.op("vector", "tensor_copy", stc["si"][sl], stc["t1"][sl],
                 inc=("DV", 1))
        elif "lnf_recip" in DEBUG:
            p.op("vector", "reciprocal", stc["si"][sl], stc["t1"][sl],
                 inc=("DV", 1))
        else:
            vac = p.op("scalar", "activation", stc["t2"][sl], stc["t1"][sl],
                       AF.Sqrt, bias=c_epsln, w=[("DV", vdv)], inc=("AC", 1))
            p.op("vector", "reciprocal", stc["si"][sl], stc["t2"][sl],
                 w=[("AC", vac)], inc=("DV", 1))
        p.op("vector", "tensor_tensor", stc["bu"][sl], stc["mu"][sl], stc["si"][sl],
             ALU.mult, inc=("DV", 1))
        vdv = p.op("vector", "tensor_scalar", stc["bu"][sl], stc["bu"][sl], -1.0,
                   None, ALU.mult, inc=("DV", 1))
        if extra != "rms":
            return vdv
        p.op("vector", "tensor_scalar", stc["t2"][sl], stc["t1"][sl], EPS_LN, None,
             ALU.add, inc=("DV", 1))
        p.op("vector", "reciprocal", stc["t2"][sl], stc["t2"][sl], inc=("DV", 1))
        vdv = p.op("vector", "tensor_tensor", stc["t2"][sl], stc["t1"][sl],
                   stc["t2"][sl], ALU.mult, inc=("DV", 1))
        vac = p.op("scalar", "activation", stc["rm"][sl], stc["t2"][sl], AF.Sqrt,
                   bias=c_epsrms, w=[("DV", vdv)], inc=("AC", 1))
        p.op("vector", "reciprocal", stc["t2"][sl], stc["rm"][sl], w=[("AC", vac)],
             inc=("DV", 1))
        p.op("vector", "tensor_tensor", stc["ar"][sl], stc["si"][sl], stc["t2"][sl],
             ALU.mult, inc=("DV", 1))
        p.op("vector", "tensor_tensor", stc["br"][sl], stc["mu"][sl], stc["ar"][sl],
             ALU.mult, inc=("DV", 1))
        p.op("vector", "tensor_scalar", stc["br"][sl], stc["br"][sl], -1.0, None,
             ALU.mult, inc=("DV", 1))
        vdv = p.op("vector", "tensor_scalar", stc["a1"][sl], stc["si"][sl], 1.0,
                   None, ALU.add, inc=("DV", 1))
        return vdv

    def mkstats(ctx, tagn, cols=NT, keys=("su", "sq", "mu", "si", "bu", "t1", "t2")):
        return {k: ctx.enter_context(nc.sbuf_tensor(tagn + k, [128, cols], F32))
                for k in keys}

    def emit_mlp_pass(tag, experts, rn_sp, dst, wb=None):
        p.phase = tag + "_mlp"
        ctx = ExitStack()
        w1b = ctx.enter_context(nc.sbuf_tensor(tag + "w1b", [128, KC * MLP], BF16))
        w2b = ctx.enter_context(nc.sbuf_tensor(tag + "w2b", [128, HC * D], BF16))
        nslot = 2
        wst = ctx.enter_context(nc.sbuf_tensor(
            tag + "wst", [128, nslot * 1024], F32))
        hT = ctx.enter_context(nc.sbuf_tensor(tag + "hT", [128, HC * BLK], BF16))
        rnb = ctx.enter_context(nc.sbuf_tensor(tag + "rnb", [128, KC * BLK], BF16))
        xrb = ctx.enter_context(nc.sbuf_tensor(tag + "xrb", [128, NSUB * D], F32))
        b1c = ctx.enter_context(nc.sbuf_tensor(tag + "b1c", [128, 2 * HC], F32))
        b2s = ctx.enter_context(nc.sbuf_tensor(tag + "b2s", [1, D], F32))
        b2r = ctx.enter_context(nc.sbuf_tensor(tag + "b2r", [1, 2 * D], BF16))
        ph = [ctx.enter_context(nc.psum_tensor(tag + f"ph{i}", [128, 512], F32))
              for i in range(2)]
        po = [ctx.enter_context(nc.psum_tensor(tag + f"po{i}", [128, 512], F32))
              for i in range(2)]

        ne = len(experts)
        w1done = [0] * ne
        w2done = [0] * ne
        b1v = [0] * ne
        b2v = [0] * ne
        pe_h_free = [0] * (ne + 1)
        pe_o_free = [0] * (ne + 1)
        ac_b1_free = [0] * (ne + 1)
        pe_b2_free = [0] * (ne + 1)

        wb_tok = [()] * (ne + 1)

        def stream_exp_weights(e):
            ex = experts[e]
            ga = ex["gain"]
            if wb == "load":
                t1 = p.dma("gpsimd", "wbl1", w1b[:],
                           WBF[e * 128:(e + 1) * 128, 0:KC * MLP],
                           w=[("PE", pe_h_free[e])] + list(wb_tok[e]))
                t2 = p.dma("gpsimd", "wbl2", w2b[:],
                           WBF[e * 128:(e + 1) * 128, KC * MLP:2 * KC * MLP],
                           w=[("PE", pe_o_free[e])])
                w1done[e] = [t1]
                w2done[e] = [t2]
            else:
                w1done[e] = stream_w_bf16(
                    lambda j: ex["W1"][(j // 4) * 128:(j // 4 + 1) * 128,
                                       (j % 4) * 1024:(j % 4) * 1024 + 1024],
                    w1b, wst,
                    gain=None if ga is None else
                    (lambda j, ga=ga: ga[:, j // 4: j // 4 + 1]),
                    gate_pe=pe_h_free[e], nc_chunks=32, chunk_cols=1024,
                    conv="dve", nslot=nslot, gate_w=wb_tok[e])
                w2done[e] = stream_w_bf16(
                    lambda j: ex["W2"][j * 128:(j + 1) * 128, :],
                    w2b, wst, gain=None, gate_pe=pe_o_free[e], nc_chunks=32,
                    chunk_cols=1024, conv="dve", nslot=nslot,
                    gate_w=w1done[e])
                if wb == "save":
                    s1 = p.dma("scalar", "wbs1",
                               WBF[e * 128:(e + 1) * 128, 0:KC * MLP],
                               w1b[:], w=list(w1done[e]))
                    s2 = p.dma("scalar", "wbs2",
                               WBF[e * 128:(e + 1) * 128, KC * MLP:2 * KC * MLP],
                               w2b[:], w=list(w2done[e]))
                    wb_tok[e + 1] = (s1, s2)
            sl = e % 2
            b1v[e] = p.dma("sync", f"b1{sl}", b1c[:, sl * HC:(sl + 1) * HC],
                           ex["b1"], w=[("AC", ac_b1_free[e])])
            lv = p.dma("sync", "b2", b2s[:], ex["b2"],
                       w=[("PE", pe_b2_free[e]),
                          ("DV", b2v[e - 1]) if e else None])
            b2v[e] = p.op("vector", "tensor_copy", b2r[0:1, sl * D:(sl + 1) * D],
                          b2s[:], w=[lv], inc=("DV", 1))

        stream_exp_weights(0)
        rn_ring = Ring(1)
        xr_ring = Ring(1)
        gelu_free = [0, 0]
        oev_free = [0, 0]
        for e in range(ne):
            ex = experts[e]
            sl = e % 2
            last_h_mm = 0
            last_o_mm = 0
            last_gelu = 0
            for blk in range(NBLK):
                p.phase = f"{tag}e{e}b{blk}h"
                rv = p.dma("sync", "rnb",
                           rnb[:].rearrange("p (c t) -> p c t", t=BLK),
                           fmv(rn_sp, blk), w=rn_ring.waits(0))
                xv = p.dma("sync", "xrb",
                           xrb[:].rearrange("p (c d) -> p c d", d=D),
                           tmv(dst, blk), w=xr_ring.waits(0))
                for h in range(HC):
                    bank = ph[h % 2]
                    wl = [rv]
                    if h == 0 and blk == 0:
                        wl += list(w1done[e]) + [b1v[e]]
                    if gelu_free[h % 2]:
                        wl.append(("AC", gelu_free[h % 2]))
                    for kc in range(KC):
                        last_h_mm = p.op(
                            "tensor", "matmul", bank[:],
                            w1b[:, kc * MLP + h * 128: kc * MLP + (h + 1) * 128],
                            rnb[:, kc * BLK:(kc + 1) * BLK],
                            start=(kc == 0), stop=(kc == KC - 1),
                            w=wl if kc == 0 else (), inc=("PE", 1))
                    last_gelu = p.op(
                        "scalar", "activation", hT[:, h * BLK:(h + 1) * BLK],
                        bank[:], AF.Gelu, bias=b1c[:, sl * HC + h: sl * HC + h + 1],
                        w=[("PE", last_h_mm)], inc=("AC", 1))
                    gelu_free[h % 2] = last_gelu
                rn_ring.done(0, ("PE", last_h_mm))
                ac_h_done = last_gelu
                p.phase = f"{tag}e{e}b{blk}o"
                for sub in range(NSUB):
                    for fh in range(2):
                        bi = (sub * 2 + fh) % 2
                        bank = po[bi]
                        wl = [("AC", ac_h_done)]
                        if blk == 0 and sub == 0 and fh == 0:
                            wl += list(w2done[e]) + [("DV", b2v[e])]
                        if oev_free[bi]:
                            wl.append(("DV", oev_free[bi]))
                        p.op("tensor", "matmul", bank[:], ones16r[0:1, 0:128],
                             b2r[0:1, sl * D + fh * 512: sl * D + (fh + 1) * 512],
                             start=True, stop=False, w=wl, inc=("PE", 1))
                        for h in range(HC):
                            last_o_mm = p.op(
                                "tensor", "matmul", bank[:],
                                hT[:, h * BLK + sub * 128: h * BLK + (sub + 1) * 128],
                                w2b[:, h * D + fh * 512: h * D + (fh + 1) * 512],
                                start=False, stop=(h == HC - 1), inc=("PE", 1))
                        wc = ex["wcol"](blk, sub)
                        dsl = xrb[:, sub * D + fh * 512: sub * D + (fh + 1) * 512]
                        oev_free[bi] = p.op(
                            "vector", "scalar_tensor_tensor", dsl, bank[:],
                            1.0 if wc is None else wc, dsl, ALU.mult, ALU.add,
                            w=[("PE", last_o_mm), xv], inc=("DV", 1))
                sv = p.dma("sync", "xrbs", tmv(dst, blk),
                           xrb[:].rearrange("p (c d) -> p c d", d=D),
                           w=[("DV", p.val("DV"))])
                xr_ring.done(0, sv)
            pe_h_free[e + 1] = last_h_mm
            pe_o_free[e + 1] = last_o_mm
            ac_b1_free[e + 1] = last_gelu
            pe_b2_free[e + 1] = last_o_mm
            if e + 1 < ne:
                stream_exp_weights(e + 1)
        p.gate_all()
        ctx.close()

    def emit_ffn(name, src, dst, wb=None):
        p.phase = name + "F1"
        octx = ExitStack()
        wsb = octx.enter_context(nc.sbuf_tensor(name + "wsb", [128, NT * 8], F32))
        gsc = octx.enter_context(nc.sbuf_tensor(name + "gsc", [128, 2 * KC], F32))
        ctx = ExitStack()
        xt = ctx.enter_context(nc.sbuf_tensor(name + "xt", [128, 2 * NSUB * D], F32))
        xo = ctx.enter_context(nc.sbuf_tensor(name + "xo", [128, 2 * NSUB * D], F32))
        rtm = ctx.enter_context(nc.sbuf_tensor(name + "rtm", [128, 2 * D], BF16))
        rfm = ctx.enter_context(nc.sbuf_tensor(name + "rfm", [128, 2 * KC * BLK], BF16))
        cen = ctx.enter_context(nc.sbuf_tensor(name + "cen", [128, KC * 8], BF16))
        cstg = ctx.enter_context(nc.sbuf_tensor(name + "cstg", [128, KC * 8], F32))
        scr = ctx.enter_context(nc.sbuf_tensor(name + "scr", [128, NT * 8], F32))
        stc = mkstats(ctx, name + "s",
                      keys=("su", "sq", "mu", "si", "bu", "t1", "t2", "ar", "br",
                            "a1", "rm"))
        ptr = [ctx.enter_context(nc.psum_tensor(name + f"pt{i}", [128, 1024], BF16))
               for i in range(2)]
        psc = [ctx.enter_context(nc.psum_tensor(name + f"ps{i}", [128, 512], F32))
               for i in range(2)]

        gva = p.dma("sync", "gs0", gsc[:, 0:KC], f_gs)
        gvb = p.dma("sync", "gs1", gsc[:, KC:2 * KC], f_gr)
        cv = p.dma("sync", "cen", cstg[:].rearrange("p (c e) -> p c e", e=8),
                   f_cen.rearrange("(c p) e -> p c e", p=128))
        cenv = p.op("vector", "tensor_copy", cen[:], cstg[:],
                    w=[cv, gva, gvb], inc=("DV", 1))

        xt_ring = Ring(2)
        xo_ring = Ring(2)
        rtm_ring = Ring(2)
        rfm_ring = Ring(2)
        psc_ring = Ring(2)
        for blk in range(NBLK):
            bb = blk % 2
            xv = p.dma("sync", f"xt{bb}",
                       xt[:, bb * NSUB * D:(bb + 1) * NSUB * D].rearrange(
                           "p (c d) -> p c d", d=D),
                       tmv(src, blk), w=xt_ring.waits(blk))
            for s2 in range(NSUB):
                i = blk * NSUB + s2
                tile = xt[:, bb * NSUB * D + s2 * D: bb * NSUB * D + (s2 + 1) * D]
                ln_stats(p, tile, stc["su"][:, i:i + 1], stc["sq"][:, i:i + 1],
                         [xv])
            vstat = ln_finish(stc, blk * NSUB, NSUB, extra="rms")
            last_ac = 0
            last_sc_mm = 0
            for s2 in range(NSUB):
                i = blk * NSUB + s2
                sb2 = (blk * NSUB + s2) % 2
                tile = xt[:, bb * NSUB * D + s2 * D: bb * NSUB * D + (s2 + 1) * D]
                if "f1_min" in DEBUG:
                    last_ac = p.op("scalar", "activation",
                                   xo[:, bb * NSUB * D + s2 * D: bb * NSUB * D + (s2 + 1) * D],
                                   tile, AF.Identity, scale=stc["a1"][:, i:i + 1],
                                   bias=stc["bu"][:, i:i + 1],
                                   w=[("DV", vstat)] + (xo_ring.waits(blk) if s2 == 0 else []),
                                   inc=("AC", 1))
                    continue
                vrn = p.op("scalar", "activation", rtm[:, sb2 * D:(sb2 + 1) * D],
                           tile, AF.Identity, scale=stc["ar"][:, i:i + 1],
                           bias=stc["br"][:, i:i + 1],
                           w=[("DV", vstat)] + rtm_ring.waits(i), inc=("AC", 1))
                vpe = 0
                for j in range(KC):
                    vpe = p.op("tensor", "transpose",
                               ptr[sb2][:, j * 128:(j + 1) * 128],
                               rtm[:, sb2 * D + j * 128: sb2 * D + (j + 1) * 128],
                               ident16[:], w=[("AC", vrn)] if j == 0 else (),
                               inc=("PE", 1))
                rtm_ring.done(i, ("PE", vpe))
                vdve = p.op("vector", "tensor_copy",
                            rfm[:, bb * KC * BLK:(bb + 1) * KC * BLK].rearrange(
                                "p (c t) -> p c t", t=BLK)[:, :, s2 * 128:(s2 + 1) * 128],
                            ptr[sb2][:].rearrange("p (c t) -> p c t", t=128),
                            w=[("PE", vpe)] + (rfm_ring.waits(blk) if s2 == 0 else []),
                            inc=("DV", 1))
                for kc in range(KC):
                    wl = [("DV", vdve), ("DV", cenv)]
                    if s2 == 0 and kc == 0:
                        wl += psc_ring.waits(blk)
                    last_sc_mm = p.op(
                        "tensor", "matmul", psc[bb][:, s2 * 8:(s2 + 1) * 8],
                        rfm[:, bb * KC * BLK + kc * BLK + s2 * 128:
                            bb * KC * BLK + kc * BLK + (s2 + 1) * 128],
                        cen[:, kc * 8:(kc + 1) * 8],
                        start=(kc == 0), stop=(kc == KC - 1),
                        skip_group_check=True,
                        w=wl if kc == 0 else (), inc=("PE", 1))
                last_ac = p.op("scalar", "activation",
                               xo[:, bb * NSUB * D + s2 * D: bb * NSUB * D + (s2 + 1) * D],
                               tile, AF.Identity, scale=stc["a1"][:, i:i + 1],
                               bias=stc["bu"][:, i:i + 1],
                               w=xo_ring.waits(blk) if s2 == 0 else (),
                               inc=("AC", 1))
            xt_ring.done(blk, ("AC", last_ac))
            if "f1_min" not in DEBUG:
                vsc = p.op("vector", "tensor_copy", scr[:, blk * 32:(blk + 1) * 32],
                           psc[bb][:, 0:32], w=[("PE", last_sc_mm)], inc=("DV", 1))
                psc_ring.done(blk, ("DV", vsc))
                vrf = p.dma("sync", f"rnsp{bb}", fmv(SP_RN, blk),
                            rfm[:, bb * KC * BLK:(bb + 1) * KC * BLK].rearrange(
                                "p (c t) -> p c t", t=BLK),
                            w=[("DV", p.val("DV"))])
                rfm_ring.done(blk, vrf, ("PE", last_sc_mm))
            vxo = p.dma("sync", f"xo{bb}", tmv(dst, blk),
                        xo[:, bb * NSUB * D:(bb + 1) * NSUB * D].rearrange(
                            "p (c d) -> p c d", d=D),
                        w=[("AC", last_ac)])
            xo_ring.done(blk, vxo)
        if "dbgstats" in DEBUG:
            dbg_st = nc.dram_tensor(name + "dbg_st", [6, 128, NT], F32,
                                    kind="ExternalOutput").ap()
            for k2, nm in enumerate(("su", "sq", "mu", "si", "a1", "bu")):
                p.dma("sync", f"dst{k2 % 4}", dbg_st[k2], stc[nm][:],
                      w=[("DV", p.val("DV")), ("AC", p.val("AC"))])
        for i in range(NT if "f1_min" not in DEBUG else 0):
            ssl = scr[:, i * 8:(i + 1) * 8]
            wsl = wsb[:, i * 8:(i + 1) * 8]
            vac = p.op("scalar", "activation", ssl, ssl, AF.Sigmoid,
                       scale=stc["rm"][:, i:i + 1],
                       w=[("DV", p.val("DV"))] if i == 0 else (), inc=("AC", 1))
            p.op("vector", "reduce_max", stc["t1"][:, i:i + 1], ssl, AX.X,
                 w=[("AC", vac)], inc=("DV", 1))
            p.op("vector", "tensor_scalar", wsl, ssl, stc["t1"][:, i:i + 1], None,
                 ALU.is_equal, inc=("DV", 1))
            p.op("vector", "scalar_tensor_tensor", wsl, wsl, -1e30, ssl,
                 ALU.mult, ALU.add, inc=("DV", 1))
            p.op("vector", "reduce_max", stc["t2"][:, i:i + 1], wsl, AX.X,
                 inc=("DV", 1))
            p.op("vector", "reduce_sum", stc["t1"][:, i:i + 1], ssl, AX.X,
                 inc=("DV", 1))
            p.op("vector", "reciprocal", stc["t1"][:, i:i + 1],
                 stc["t1"][:, i:i + 1], inc=("DV", 1))
            p.op("vector", "tensor_scalar", wsl, ssl, stc["t2"][:, i:i + 1], None,
                 ALU.is_ge, inc=("DV", 1))
            p.op("vector", "tensor_tensor", wsl, wsl, ssl, ALU.mult, inc=("DV", 1))
            p.op("vector", "tensor_scalar", wsl, wsl, stc["t1"][:, i:i + 1], None,
                 ALU.mult, inc=("DV", 1))
        if "dbg" in DEBUG:
            dbg_rn = nc.dram_tensor(name + "dbg_rn", [D, T], BF16,
                                    kind="ExternalOutput").ap()
            dbg_w = nc.dram_tensor(name + "dbg_w", [128, NT * 8], F32,
                                   kind="ExternalOutput").ap()
            dbg_sc = nc.dram_tensor(name + "dbg_sc", [128, NT * 8], F32,
                                    kind="ExternalOutput").ap()
            p.dma("sync", "dbg0", dbg_rn, SP_RN,
                  w=[("DV", p.val("DV"))] + [(f"Q{i}", p.val(f"Q{i}"))
                                             for i in range(p.NPOOL)])
            p.dma("sync", "dbg1", dbg_w, wsb[:], w=[("DV", p.val("DV"))])
            p.dma("sync", "dbg2", dbg_sc, scr[:], w=[("DV", p.val("DV"))])
        p.gate_all()

        experts = []
        for e in range(2):
            experts.append(dict(W1=fW1s[e], W2=fW2s[e], b1=fb1s[e], b2=fb2s[e][0:1, :],
                                gain=gsc[:, 0:KC], wcol=lambda blk, sub: None))
        for e in range(8):
            def mkw(e=e):
                return lambda blk, sub: wsb[:, (blk * NSUB + sub) * 8 + e:
                                            (blk * NSUB + sub) * 8 + e + 1]
            experts.append(dict(W1=fW1r[e], W2=fW2r[e], b1=fb1r[e],
                                b2=fb2r[e][0:1, :], gain=gsc[:, KC:2 * KC],
                                wcol=mkw()))
        ctx.close()
        if "no_experts" not in DEBUG:
            emit_mlp_pass(name + "X", experts, SP_RN, dst, wb=wb)
        octx.close()

    def emit_attn_a(src, dst):
        octx = ExitStack()
        au = octx.enter_context(nc.sbuf_tensor("Aau", [128, NT], F32))
        buc = octx.enter_context(nc.sbuf_tensor("Abu", [128, NT], F32))
        qg = octx.enter_context(nc.sbuf_tensor("Aqg", [128, KC * 2], F32))
        atm = octx.enter_context(nc.sbuf_tensor("Aatm", [128, NT * H], F32))
        a0r = octx.enter_context(nc.sbuf_tensor("Aa0r", [1, 2 * H], F32))
        kvsb = octx.enter_context(nc.sbuf_tensor("Akv", [128, 2 * KC * 128], BF16))
        kse = octx.enter_context(nc.sbuf_tensor("Akse", [128, 2 * KC * H], BF16))

        p.phase = "A1"
        ctx = ExitStack()
        wq16 = ctx.enter_context(nc.sbuf_tensor("Awq", [128, KC * D], BF16))
        wk16 = ctx.enter_context(nc.sbuf_tensor("Awk", [128, KC * D], BF16))
        wv16 = ctx.enter_context(nc.sbuf_tensor("Awv", [128, KC * D], BF16))
        wpq16 = ctx.enter_context(nc.sbuf_tensor("Awpq", [128, KC * D], BF16))
        wstg = ctx.enter_context(nc.sbuf_tensor("Awstg", [128, 2 * D], F32))
        brow = ctx.enter_context(nc.sbuf_tensor("Abrow", [1, 5 * D], BF16))
        brst = ctx.enter_context(nc.sbuf_tensor("Abrst", [1, 5 * D], F32))
        lnqc = ctx.enter_context(nc.sbuf_tensor("Alnq", [128, 2 * KC], F32))
        lnkc = ctx.enter_context(nc.sbuf_tensor("Alnk", [128, 2 * KC], F32))
        lnvb = ctx.enter_context(nc.sbuf_tensor("Alnv", [128, 2 * D], F32))
        stc = mkstats(ctx, "As")
        st4 = mkstats(ctx, "A4", cols=4)
        xt = ctx.enter_context(nc.sbuf_tensor("A1xt", [128, NSUB * D], F32))
        utm = ctx.enter_context(nc.sbuf_tensor("A1ut", [128, 2 * D], BF16))
        ufm = ctx.enter_context(nc.sbuf_tensor("A1uf", [128, KC * BLK], BF16))
        qtm = ctx.enter_context(nc.sbuf_tensor("A1qt", [128, 2 * D], BF16))
        qfm = ctx.enter_context(nc.sbuf_tensor("A1qf", [128, KC * BLK], BF16))
        sfm = ctx.enter_context(nc.sbuf_tensor("A1sf", [128, KC * BLK], BF16))
        vtm = ctx.enter_context(nc.sbuf_tensor("A1vt", [128, NSUB * D], BF16))
        vrw = ctx.enter_context(nc.sbuf_tensor("A1vr", [128, D], F32))
        phf = ctx.enter_context(nc.sbuf_tensor("A1pf", [128, 512], F32))
        pt = [ctx.enter_context(nc.psum_tensor(f"A1pt{i}", [128, 1024], BF16))
              for i in range(2)]
        pj = [ctx.enter_context(nc.psum_tensor(f"A1pj{i}", [128, 512], F32))
              for i in range(2)]

        wq_done = stream_w_bf16(lambda j: aWq[j * 128:(j + 1) * 128, :], wq16, wstg)
        wk_done = stream_w_bf16(lambda j: aWk[j * 128:(j + 1) * 128, :], wk16, wstg)
        wv_done = stream_w_bf16(lambda j: aWv[j * 128:(j + 1) * 128, :], wv16, wstg)
        wpq_done = stream_w_bf16(lambda j: aWpq[j * 128:(j + 1) * 128, :], wpq16, wstg)
        lv = p.dma("sync", "brows", brst[:], a_brows)
        brow_v = p.op("vector", "tensor_copy", brow[:], brst[:], w=[lv],
                      inc=("DV", 1))
        lnq_lv = p.dma("sync", "lnq", lnqc[:], a_lnq)
        lnk_lv = p.dma("sync", "lnk", lnkc[:], a_lnk)
        ln_lv = p.dma("sync", "lnv", lnvb[:], a_lnv_bc)
        vz = p.op("vector", "memset", qg[:], 0.0, inc=("DV", 1))

        xt_ring = Ring(1)
        utm_ring = Ring(2)
        qtm_ring = Ring(2)
        pj_ring = Ring(2)
        pt_ring = Ring(2)
        ufm_ring = Ring(1)
        qfm_ring = Ring(1)
        sfm_ring = Ring(1)
        vtm_ring = Ring(1)
        phf_ring = Ring(1)

        def tm_projection_sub(blk, s2, w16, brow_off, wdone):
            mmv = [0, 0]
            for fh in range(2):
                bank = pj[fh]
                wl = list(wdone) + [("DV", brow_v)] + pj_ring.waits(fh)
                p.op("tensor", "matmul", bank[:], ones16r[0:1, 0:128],
                     brow[0:1, brow_off + fh * 512: brow_off + (fh + 1) * 512],
                     start=True, stop=False, w=wl, inc=("PE", 1))
                for kc in range(KC):
                    mmv[fh] = p.op(
                        "tensor", "matmul", bank[:],
                        ufm[:, kc * BLK + s2 * 128: kc * BLK + (s2 + 1) * 128],
                        w16[:, kc * D + fh * 512: kc * D + (fh + 1) * 512],
                        start=False, stop=(kc == KC - 1), inc=("PE", 1))
                vsq = p.op("scalar", "activation",
                           junk[:, fh * 512:(fh + 1) * 512], bank[:], AF.Square,
                           w=[("PE", mmv[fh]), ("DV", jfree_g[0])], inc=("AC", 1))
                p.op("vector", "tensor_reduce", st4["su"][:, fh:fh + 1],
                     bank[:], AX.X, ALU.add, w=[("AC", vsq)], inc=("DV", 1))
                jfree_g[0] = p.op("vector", "tensor_reduce",
                                  st4["sq"][:, fh:fh + 1],
                                  junk[:, fh * 512:(fh + 1) * 512], AX.X, ALU.add,
                                  inc=("DV", 1))
            p.op("vector", "tensor_tensor", st4["su"][:, 0:1], st4["su"][:, 0:1],
                 st4["su"][:, 1:2], ALU.add, inc=("DV", 1))
            p.op("vector", "tensor_tensor", st4["sq"][:, 0:1], st4["sq"][:, 0:1],
                 st4["sq"][:, 1:2], ALU.add, inc=("DV", 1))
            vstat = ln_finish(st4, 0, 1)
            acv = 0
            for fh in range(2):
                acv = p.op("scalar", "activation",
                           qtm[:, (s2 % 2) * D + fh * 512:(s2 % 2) * D + (fh + 1) * 512],
                           pj[fh][:], AF.Identity, scale=st4["si"][:, 0:1],
                           bias=st4["bu"][:, 0:1],
                           w=[("DV", vstat)] + (qtm_ring.waits(s2) if fh == 0 else []),
                           inc=("AC", 1))
                pj_ring.done(fh, ("AC", acv))
            return acv

        def fm_from_tm(blk, src_tm, s2, dstfm, lncol, wsrc, dst_waits=(),
                       lncol_w=()):
            vpe = 0
            for j in range(KC):
                vpe = p.op("tensor", "transpose", pt[s2 % 2][:, j * 128:(j + 1) * 128],
                           src_tm[:, (s2 % 2) * D + j * 128:(s2 % 2) * D + (j + 1) * 128],
                           ident16[:],
                           w=([("AC", wsrc)] + pt_ring.waits(s2)) if j == 0 else (),
                           inc=("PE", 1))
            vac = 0
            for j in range(KC):
                vac = p.op("scalar", "activation",
                           dstfm[:, j * BLK + s2 * 128: j * BLK + (s2 + 1) * 128],
                           pt[s2 % 2][:, j * 128:(j + 1) * 128], AF.Identity,
                           scale=lncol[:, 2 * j:2 * j + 1],
                           bias=lncol[:, 2 * j + 1:2 * j + 2],
                           w=([("PE", vpe)] + list(dst_waits) + list(lncol_w))
                           if j == 0 else (),
                           inc=("AC", 1))
            pt_ring.done(s2, ("AC", vac))
            return vpe, vac

        for blk in range(NBLK):
            seq = blk // NSUB
            xv = p.dma("sync", "a1xt", xt[:].rearrange("p (c d) -> p c d", d=D),
                       tmv(src, blk), w=xt_ring.waits(0))
            for s2 in range(NSUB):
                i = blk * NSUB + s2
                tile = xt[:, s2 * D:(s2 + 1) * D]
                ln_stats(p, tile, stc["su"][:, i:i + 1], stc["sq"][:, i:i + 1],
                         [xv])
            vstat = ln_finish(stc, blk * NSUB, NSUB)
            sl = (slice(None), slice(blk * NSUB, blk * NSUB + NSUB))
            p.op("vector", "tensor_copy", au[sl], stc["si"][sl], w=[("DV", vstat)],
                 inc=("DV", 1))
            p.op("vector", "tensor_copy", buc[sl], stc["bu"][sl], inc=("DV", 1))
            last_xt_use = 0
            uev = 0
            for s2 in range(NSUB):
                i = blk * NSUB + s2
                vu = p.op("scalar", "activation", utm[:, (s2 % 2) * D:(s2 % 2 + 1) * D],
                          xt[:, s2 * D:(s2 + 1) * D], AF.Identity,
                          scale=stc["si"][:, i:i + 1], bias=stc["bu"][:, i:i + 1],
                          w=[("DV", vstat)] + utm_ring.waits(s2), inc=("AC", 1))
                last_xt_use = vu
                vpe = 0
                for j in range(KC):
                    vpe = p.op("tensor", "transpose", pt[s2 % 2][:, j * 128:(j + 1) * 128],
                               utm[:, (s2 % 2) * D + j * 128:(s2 % 2) * D + (j + 1) * 128],
                               ident16[:],
                               w=([("AC", vu)] + pt_ring.waits(s2)) if j == 0 else (),
                               inc=("PE", 1))
                utm_ring.done(s2, ("PE", vpe))
                uev = p.op("vector", "tensor_copy",
                           ufm[:].rearrange("p (c t) -> p c t", t=BLK)[:, :, s2 * 128:(s2 + 1) * 128],
                           pt[s2 % 2][:].rearrange("p (c t) -> p c t", t=128),
                           w=[("PE", vpe)] + (ufm_ring.waits(0) if s2 == 0 else []),
                           inc=("DV", 1))
                pt_ring.done(s2, ("DV", uev))
            xt_ring.done(0, ("AC", last_xt_use))
            p.wait_only("tensor", [("DV", uev)])

            pe_q = 0
            ac_q = 0
            for s2 in range(NSUB):
                if ASUB < 2:
                    break
                acv = tm_projection_sub(blk, s2, wq16, 0 * D, wq_done)
                if ASUB < 3:
                    qtm_ring.done(s2, ("AC", acv))
                    continue
                pe_q, ac_q = fm_from_tm(blk, qtm, s2, qfm, lnqc, acv,
                                        dst_waits=qfm_ring.waits(0) if s2 == 0 else (),
                                        lncol_w=[lnq_lv])
                qtm_ring.done(s2, ("PE", pe_q))
            qfm_ready = ac_q
            vlast_t = 0
            for j in range(KC if ASUB >= 4 else 0):
                bank = pj[j % 2]
                wl = [("AC", qfm_ready), ("DV", brow_v)] + list(wpq_done)
                wl += pj_ring.waits(j % 2)
                p.op("tensor", "matmul", bank[:],
                     brow[0:1, 3 * D + j * 128: 3 * D + (j + 1) * 128],
                     ones16r[0:1, 0:512], start=True, stop=False, w=wl,
                     inc=("PE", 1))
                for kc in range(KC):
                    vpe = p.op("tensor", "matmul", bank[:],
                               wpq16[:, kc * D + j * 128: kc * D + (j + 1) * 128],
                               qfm[:, kc * BLK:(kc + 1) * BLK],
                               start=False, stop=(kc == KC - 1), inc=("PE", 1))
                vt = p.op("scalar", "activation", phf[:], bank[:], AF.Tanh,
                          w=[("PE", vpe)] + phf_ring.waits(0), inc=("AC", 1))
                vlast_t = p.op("vector", "tensor_scalar",
                               sfm[:, j * BLK:(j + 1) * BLK], phf[:], 1.0, None,
                               ALU.add, w=[("AC", vt)] + (sfm_ring.waits(0) if j == 0 else []),
                               inc=("DV", 1))
                phf_ring.done(0, ("DV", vlast_t))
                pj_ring.done(j % 2, ("AC", vt))
                p.op("vector", "tensor_reduce", st4["t1"][:, 2:3],
                     qfm[:, j * BLK:(j + 1) * BLK], AX.X, ALU.add, inc=("DV", 1))
                p.op("vector", "tensor_tensor", qg[:, j * 2 + seq:j * 2 + seq + 1],
                     qg[:, j * 2 + seq:j * 2 + seq + 1], st4["t1"][:, 2:3],
                     ALU.add, w=[("DV", vz)], inc=("DV", 1))
            if ASUB >= 4:
                qfm_ring.done(0, ("DV", p.val("DV")), ("PE", p.val("PE")))
                vsp = p.dma("sync", "tqsp", fmv(SP_TQ, blk),
                            sfm[:].rearrange("p (c t) -> p c t", t=BLK),
                            w=[("DV", vlast_t)])
                sfm_ring.done(0, vsp)

            ac_k = 0
            for s2 in range(NSUB if ASUB >= 5 else 0):
                acv = tm_projection_sub(blk, s2, wk16, 1 * D, wk_done)
                pe_k, ac_k = fm_from_tm(blk, qtm, s2, qfm, lnkc, acv,
                                        dst_waits=qfm_ring.waits(0) if s2 == 0 else (),
                                        lncol_w=[lnk_lv])
                qtm_ring.done(s2, ("PE", pe_k))
            if ASUB >= 5:
                vsp = p.dma("sync", "ksp", fmv(SP_K, blk),
                            qfm[:].rearrange("p (c t) -> p c t", t=BLK),
                            w=[("AC", ac_k)])
                qfm_ring.done(0, vsp)

            vdv = 0
            for s2 in range(NSUB if ASUB >= 6 else 0):
                acv = tm_projection_sub(blk, s2, wv16, 2 * D, wv_done)
                p.op("vector", "tensor_tensor", vrw[:],
                     qtm[:, (s2 % 2) * D:(s2 % 2 + 1) * D], lnvb[:, 0:D], ALU.mult,
                     w=[("AC", acv), ln_lv], inc=("DV", 1))
                vdv = p.op("vector", "tensor_tensor", vtm[:, s2 * D:(s2 + 1) * D],
                           vrw[:], lnvb[:, D:2 * D], ALU.add,
                           w=vtm_ring.waits(0) if s2 == 0 else (), inc=("DV", 1))
                qtm_ring.done(s2, ("DV", vdv))
            if ASUB >= 6:
                vsp = p.dma("sync", "vsp", tmv(SP_V, blk),
                            vtm[:].rearrange("p (c d) -> p c d", d=D),
                            w=[("DV", vdv)])
                vtm_ring.done(0, vsp)
            ufm_ring.done(0, ("PE", p.val("PE")))
        p.gate_all()
        ctx.close()
        if ALV <= 1:
            octx.close()
            return


        p.phase = "A2a"
        ctx = ExitStack()
        kfm = ctx.enter_context(nc.sbuf_tensor("A2k", [128, 2 * KC * BLK], BF16))
        qge = ctx.enter_context(nc.sbuf_tensor("A2qg", [128, 2 * KC * H], BF16))
        scs = ctx.enter_context(nc.sbuf_tensor("A2sc", [16, L], F32))
        alr = ctx.enter_context(nc.sbuf_tensor("A2al", [16, L], F32))
        alrp = ctx.enter_context(nc.sbuf_tensor("A2ap", [128, L], BF16))
        smc = ctx.enter_context(nc.sbuf_tensor("A2sm", [16, 8], F32))
        psA = ctx.enter_context(nc.psum_tensor("A2pA", [128, 2048], F32))
        ptA = [ctx.enter_context(nc.psum_tensor(f"A2pT{i}", [128, 1024], BF16))
               for i in range(2)]

        vdv = p.op("vector", "memset", qge[:], 0.0, inc=("DV", 1))
        p.op("vector", "memset", alrp[:], 0.0, inc=("DV", 1))
        qsc = 1.0 / (L * np.sqrt(DH))
        for s in range(2):
            for c in range(KC):
                p.op("vector", "tensor_scalar",
                     qge[0:64, (s * KC + c) * H + 2 * c: (s * KC + c) * H + 2 * c + 1],
                     qg[0:64, c * 2 + s: c * 2 + s + 1], qsc, None, ALU.mult,
                     inc=("DV", 1))
                p.op("vector", "tensor_scalar",
                     qge[64:128, (s * KC + c) * H + 2 * c + 1: (s * KC + c) * H + 2 * c + 2],
                     qg[64:128, c * 2 + s: c * 2 + s + 1], qsc, None, ALU.mult,
                     inc=("DV", 1))
        vqg = p.close("vector", "DV")
        kfm_ring = Ring(2)
        psA_free = 0
        for s in range(2):
            vpe = 0
            for bi in range(NSUB):
                blk = s * NSUB + bi
                lv = p.dma("sync", "kfm%d" % (blk % 2),
                           kfm[:, (blk % 2) * KC * BLK:(blk % 2 + 1) * KC * BLK].rearrange(
                               "p (c t) -> p c t", t=BLK),
                           fmv(SP_K, blk), w=kfm_ring.waits(blk))
                for kc in range(KC):
                    wl = [lv, ("DV", vqg)]
                    if bi == 0 and kc == 0 and psA_free:
                        wl.append(("DV", psA_free))
                    vpe = p.op("tensor", "matmul", psA[0:16, bi * 512:(bi + 1) * 512],
                               qge[:, (s * KC + kc) * H:(s * KC + kc + 1) * H],
                               kfm[:, (blk % 2) * KC * BLK + kc * BLK:
                                   (blk % 2) * KC * BLK + (kc + 1) * BLK],
                               start=(kc == 0), stop=(kc == KC - 1),
                               skip_group_check=True,
                               w=wl, inc=("PE", 1))
                kfm_ring.done(blk, ("PE", vpe))
            vev = p.op("vector", "tensor_copy", scs[:], psA[0:16, :],
                       w=[("PE", vpe), ("AC", p.val("AC"))], inc=("DV", 1))
            psA_free = vev
            p.op("vector", "reduce_max", smc[:, 0:1], scs[:], AX.X, inc=("DV", 1))
            p.op("vector", "tensor_scalar", smc[:, 0:1], smc[:, 0:1], 0.0, None,
                 ALU.max, inc=("DV", 1))
            vdv = p.op("vector", "tensor_scalar", smc[:, 1:2], smc[:, 0:1], -1.0,
                       None, ALU.mult, inc=("DV", 1))
            p.op("scalar", "activation", scs[:], scs[:], AF.Exp, bias=smc[:, 1:2],
                 w=[("DV", vdv)], inc=("AC", 1))
            vex = p.op("scalar", "activation", smc[:, 2:3], smc[:, 1:2], AF.Exp,
                       inc=("AC", 1))
            p.op("vector", "reduce_sum", smc[:, 3:4], scs[:], AX.X, w=[("AC", vex)],
                 inc=("DV", 1))
            p.op("vector", "tensor_tensor", smc[:, 3:4], smc[:, 3:4], smc[:, 2:3],
                 ALU.add, inc=("DV", 1))
            p.op("vector", "reciprocal", smc[:, 4:5], smc[:, 3:4], inc=("DV", 1))
            p.op("vector", "tensor_scalar", alr[:], scs[:], smc[:, 4:5], float(L),
                 ALU.mult, ALU.mult, w=[("PE", p.val("PE"))], inc=("DV", 1))
            vdv = p.op("vector", "tensor_scalar", smc[:, 5:6], smc[:, 2:3],
                       smc[:, 4:5], float(L), ALU.mult, ALU.mult, inc=("DV", 1))
            sv = p.dma("sync", "bncw", BNC[0:1, 0:16].rearrange("o n -> n o"),
                       smc[:, 5:6],
                       w=[("DV", vdv)] + ([a0v] if s > 0 else []))
            a0v = p.dma("sync", "bncr", a0r[0:1, s * H:(s + 1) * H],
                        BNC[0:1, 0:16], w=[sv])
            vdv = p.op("vector", "tensor_copy", alrp[0:16, :], alr[:],
                       inc=("DV", 1))
            cp_dv = [0, 0]
            for ti in range(L // 128):
                r = ti % 2
                wl = [("DV", vdv)]
                if cp_dv[r]:
                    wl.append(("DV", cp_dv[r]))
                vpe = p.op("tensor", "transpose", ptA[r][:, 0:128],
                           alrp[:, ti * 128:(ti + 1) * 128], ident16[:],
                           w=wl, inc=("PE", 1))
                cp_dv[r] = p.op(
                    "vector", "tensor_copy",
                    atm[:, (s * (L // 128) + ti) * H:(s * (L // 128) + ti + 1) * H],
                    ptA[r][:, 0:16], w=[("PE", vpe)], inc=("DV", 1))
        p.gate_all()
        ctx.close()
        if ALV <= 2:
            octx.close()
            return

        p.phase = "A2b"
        ctx = ExitStack()
        kfm2 = ctx.enter_context(nc.sbuf_tensor("A2bk", [128, 2 * KC * BLK], BF16))
        vtm2 = ctx.enter_context(nc.sbuf_tensor("A2bv", [128, 2 * NSUB * D], BF16))
        wpk16 = ctx.enter_context(nc.sbuf_tensor("A2bw", [128, KC * D], BF16))
        wstg2 = ctx.enter_context(nc.sbuf_tensor("A2bs", [128, 2 * D], F32))
        brow2 = ctx.enter_context(nc.sbuf_tensor("A2bbr", [1, D], BF16))
        brst2 = ctx.enter_context(nc.sbuf_tensor("A2bbs", [1, D], F32))
        tktm = ctx.enter_context(nc.sbuf_tensor("A2btk", [128, 2 * D], BF16))
        pks = ctx.enter_context(nc.sbuf_tensor("A2bp", [128, 2 * D], BF16))
        ksr = ctx.enter_context(nc.sbuf_tensor("A2bkr", [1, 2 * D], F32))
        ksr16 = ctx.enter_context(nc.sbuf_tensor("A2bk6", [1, 128], BF16))
        tk0 = ctx.enter_context(nc.sbuf_tensor("A2bt0", [1, D], F32))
        kvacc = ctx.enter_context(nc.sbuf_tensor("A2bkva", [128, KC * 128], F32))
        ppk = [ctx.enter_context(nc.psum_tensor(f"A2bp{i}", [128, 512], F32))
               for i in range(2)]
        pkv2 = [ctx.enter_context(nc.psum_tensor("A2bkv0", [128, 1024], F32))]
        pks_ = ctx.enter_context(nc.psum_tensor("A2bks", [128, 1024], F32))
        ptB = ctx.enter_context(nc.psum_tensor("A2bt", [128, 128], F32))

        wpk_done = stream_w_bf16(lambda j: aWpk[j * 128:(j + 1) * 128, :], wpk16,
                                 wstg2)
        lv = p.dma("sync", "bpk", brst2[:], a_brows[0:1, 4 * D:5 * D])
        vt = p.op("scalar", "activation", tk0[:], brst2[:], AF.Tanh, w=[lv],
                  inc=("AC", 1))
        tk0v = p.op("vector", "tensor_scalar", tk0[:], tk0[:], 1.0, None, ALU.add,
                    w=[("AC", vt)], inc=("DV", 1))
        br2v = p.op("vector", "tensor_copy", brow2[:], brst2[:], w=[lv],
                    inc=("DV", 1))
        ksez = p.op("vector", "memset", kse[:], 0.0, inc=("DV", 1))

        kfm_ring = Ring(2)
        vtm_ring = Ring(2)
        tk_ring = Ring(2)
        pks_ring = Ring(2)
        ppk_ring = Ring(2)
        pkv_ring = Ring(1)
        pks_free = [0]
        for s in range(2):
            last_kv = 0
            last_ks = 0
            vz_kv = p.op("vector", "memset", kvacc[:], 0.0, inc=("DV", 1))
            for bi in range(NSUB):
                blk = s * NSUB + bi
                lvk = p.dma("sync", "k2%d" % (blk % 2),
                            kfm2[:, (blk % 2) * KC * BLK:(blk % 2 + 1) * KC * BLK].rearrange(
                                "p (c t) -> p c t", t=BLK),
                            fmv(SP_K, blk), w=kfm_ring.waits(blk))
                lvv = p.dma("sync", "v2%d" % (blk % 2),
                            vtm2[:, (blk % 2) * NSUB * D:(blk % 2 + 1) * NSUB * D].rearrange(
                                "p (c d) -> p c d", d=D),
                            tmv(SP_V, blk), w=vtm_ring.waits(blk))
                for s2 in range(NSUB):
                    ti = blk * NSUB + s2
                    vth = 0
                    for fh in range(2):
                        bank = ppk[fh]
                        wl = [lvk, ("DV", br2v)] + list(wpk_done)
                        wl += ppk_ring.waits(fh)
                        p.op("tensor", "matmul", bank[:], ones16r[0:1, 0:128],
                             brow2[0:1, fh * 512:(fh + 1) * 512],
                             start=True, stop=False, w=wl, inc=("PE", 1))
                        for kc in range(KC):
                            vpe = p.op("tensor", "matmul", bank[:],
                                       kfm2[:, (blk % 2) * KC * BLK + kc * BLK + s2 * 128:
                                            (blk % 2) * KC * BLK + kc * BLK + (s2 + 1) * 128],
                                       wpk16[:, kc * D + fh * 512: kc * D + (fh + 1) * 512],
                                       start=False, stop=(kc == KC - 1),
                                       inc=("PE", 1))
                        vth = p.op("scalar", "activation",
                                   tktm[:, (s2 % 2) * D + fh * 512:(s2 % 2) * D + (fh + 1) * 512],
                                   bank[:], AF.Tanh,
                                   w=[("PE", vpe)] + (tk_ring.waits(s2) if fh == 0 else []),
                                   inc=("AC", 1))
                        ppk_ring.done(fh, ("AC", vth))
                    vdvl = 0
                    for h in range(H):
                        acol = atm[:, ti * H + h: ti * H + h + 1]
                        vdvl = p.op("vector", "tensor_scalar",
                                    pks[:, (s2 % 2) * D + h * DH:(s2 % 2) * D + (h + 1) * DH],
                                    tktm[:, (s2 % 2) * D + h * DH:(s2 % 2) * D + (h + 1) * DH],
                                    acol, acol, ALU.mult, ALU.add,
                                    w=([("AC", vth)] + pks_ring.waits(s2)) if h == 0 else (),
                                    inc=("DV", 1))
                    tk_ring.done(s2, ("DV", vdvl))
                    first = (bi == 0 and s2 == 0)
                    last = (bi == NSUB - 1 and s2 == NSUB - 1)
                    kvb = pkv2[0]
                    for c in range(KC):
                        wl = [("DV", vdvl), lvv] + (pkv_ring.waits(0)
                                                    if c == 0 else [])
                        last_kv = p.op(
                            "tensor", "matmul", kvb[:, c * 128:(c + 1) * 128],
                            pks[:, (s2 % 2) * D + c * 128:(s2 % 2) * D + (c + 1) * 128],
                            vtm2[:, (blk % 2) * NSUB * D + s2 * D + c * 128:
                                 (blk % 2) * NSUB * D + s2 * D + (c + 1) * 128],
                            start=True, stop=True, skip_group_check=True,
                            w=wl if c == 0 else (), inc=("PE", 1))
                    vkva = p.op("vector", "tensor_tensor", kvacc[:], kvacc[:],
                                kvb[:], ALU.add, w=[("PE", last_kv), ("DV", vz_kv)],
                                inc=("DV", 1))
                    pkv_ring.done(0, ("DV", vkva))
                    for fh in range(2):
                        wl = []
                        if first and fh == 0:
                            wl.append(("DV", pks_free[0]))
                        last_ks = p.op(
                            "tensor", "matmul", pks_[0:1, fh * 512:(fh + 1) * 512],
                            ones16c[:],
                            pks[:, (s2 % 2) * D + fh * 512:(s2 % 2) * D + (fh + 1) * 512],
                            start=first, stop=last, skip_group_check=True,
                            w=wl, inc=("PE", 1))
                    pks_ring.done(s2, ("PE", max(last_kv, last_ks)))
                kfm_ring.done(blk, ("PE", p.val("PE")))
                vtm_ring.done(blk, ("PE", p.val("PE")))
            vks = p.op("vector", "tensor_copy", ksr[0:1, s * D:(s + 1) * D],
                       pks_[0:1, 0:1024], w=[("PE", last_ks)], inc=("DV", 1))
            pks_free[0] = vks
            for h in range(H):
                vks = p.op("vector", "scalar_tensor_tensor",
                           ksr[0:1, s * D + h * DH: s * D + (h + 1) * DH],
                           tk0[0:1, h * DH:(h + 1) * DH],
                           a0r[0:1, s * H + h: s * H + h + 1],
                           ksr[0:1, s * D + h * DH: s * D + (h + 1) * DH],
                           ALU.mult, ALU.add, w=[("DV", tk0v)], inc=("DV", 1))
            ptb_free = 0
            for c in range(KC):
                vdv = p.op("vector", "tensor_copy", ksr16[:],
                           ksr[0:1, s * D + c * 128: s * D + (c + 1) * 128],
                           w=[("DV", vks)], inc=("DV", 1))
                vpe = p.op("tensor", "matmul", ptB[:, 0:1], ksr16[:],
                           ones16c[0:1, 0:1], start=True, stop=True,
                           skip_group_check=True,
                           w=[("DV", vdv)] + ([("DV", ptb_free)] if ptb_free else []),
                           inc=("PE", 1))
                p.op("vector", "tensor_copy",
                     kse[0:64, (s * KC + c) * H + 2 * c:(s * KC + c) * H + 2 * c + 1],
                     ptB[0:64, 0:1], w=[("PE", vpe), ("DV", ksez)], inc=("DV", 1))
                ptb_free = p.op("vector", "tensor_copy",
                                kse[64:128, (s * KC + c) * H + 2 * c + 1:(s * KC + c) * H + 2 * c + 2],
                                ptB[64:128, 0:1], inc=("DV", 1))
            p.op("vector", "memset", kvsb[:, s * KC * 128:(s + 1) * KC * 128], 0.0,
                 inc=("DV", 1))
            for c in range(KC):
                p.op("vector", "tensor_copy",
                     kvsb[0:64, s * KC * 128 + c * 128: s * KC * 128 + c * 128 + 64],
                     kvacc[0:64, c * 128:c * 128 + 64], inc=("DV", 1))
                p.op("vector", "tensor_copy",
                     kvsb[64:128, s * KC * 128 + c * 128 + 64: s * KC * 128 + (c + 1) * 128],
                     kvacc[64:128, c * 128 + 64:(c + 1) * 128], inc=("DV", 1))
        p.gate_all()
        ctx.close()
        if ALV <= 3:
            octx.close()
            return

        p.phase = "A3a"
        ctx = ExitStack()
        tqb = ctx.enter_context(nc.sbuf_tensor("A3tq", [128, 2 * KC * BLK], BF16))
        atf = ctx.enter_context(nc.sbuf_tensor("A3at", [128, NSUB * D], F32))
        a16 = ctx.enter_context(nc.sbuf_tensor("A3a6", [128, D], BF16))
        aff = ctx.enter_context(nc.sbuf_tensor("A3af", [128, KC * BLK], BF16))
        rden = ctx.enter_context(nc.sbuf_tensor("A3rd", [128, H], F32))
        zed = ctx.enter_context(nc.sbuf_tensor("A3z", [128, 16], BF16))
        pnum = [ctx.enter_context(nc.psum_tensor(f"A3n{i}", [128, 1024], F32))
                for i in range(2)]
        pden = ctx.enter_context(nc.psum_tensor("A3d", [128, 512], F32))
        ptC = [ctx.enter_context(nc.psum_tensor(f"A3t{i}", [128, 1024], BF16))
               for i in range(2)]

        vz = p.op("vector", "memset", zed[:], 0.0, inc=("DV", 1))
        for s in range(2):
            for cr in (0, PADL - 2):
                p.dma("sync", "zpad",
                      SP_AF.rearrange("(c p) t -> p c t", p=128)[:, :, s * PADL + cr: s * PADL + cr + 2],
                      zed[:].rearrange("p (c t) -> p c t", t=2), w=[("DV", vz)])

        tq_ring = Ring(2)
        atf_ring = Ring(1)
        aff_ring = Ring(1)
        num_ring = Ring(2)
        den_free = [0]
        a16_free = [0]
        ptC_ring = Ring(2)
        for blk in range(NBLK):
            s = blk // NSUB
            lv = p.dma("sync", "tq%d" % (blk % 2),
                       tqb[:, (blk % 2) * KC * BLK:(blk % 2 + 1) * KC * BLK].rearrange(
                           "p (c t) -> p c t", t=BLK),
                       fmv(SP_TQ, blk), w=tq_ring.waits(blk))
            last_pe = 0
            for s2 in range(NSUB):
                ti = blk * NSUB + s2
                nb = pnum[s2 % 2]
                for c in range(KC):
                    wl = [lv] + (num_ring.waits(s2) if c == 0 else [])
                    c0 = (blk % 2) * KC * BLK + c * BLK + s2 * 128
                    last_pe = p.op(
                        "tensor", "matmul", nb[:, c * 128:(c + 1) * 128],
                        tqb[:, c0:c0 + 128],
                        kvsb[:, s * KC * 128 + c * 128: s * KC * 128 + (c + 1) * 128],
                        start=True, stop=True, skip_group_check=True,
                        w=wl if c == 0 else (), inc=("PE", 1))
                for kc in range(KC):
                    wl = [lv]
                    if kc == 0:
                        wl.append(("DV", den_free[0]))
                    vde = p.op("tensor", "matmul", pden[0:128, 0:16],
                               tqb[:, (blk % 2) * KC * BLK + kc * BLK + s2 * 128:
                                   (blk % 2) * KC * BLK + kc * BLK + (s2 + 1) * 128],
                               kse[:, (s * KC + kc) * H:(s * KC + kc + 1) * H],
                               start=(kc == 0), stop=(kc == KC - 1),
                               skip_group_check=True, w=wl, inc=("PE", 1))
                p.op("vector", "tensor_scalar", rden[:], pden[0:128, 0:16], EPS_A,
                     None, ALU.add, w=[("PE", vde)], inc=("DV", 1))
                vrd = p.op("vector", "reciprocal", rden[:], rden[:], inc=("DV", 1))
                den_free[0] = vrd
                vat = 0
                for h in range(H):
                    vat = p.op("vector", "tensor_scalar",
                               atf[:, s2 * D + h * DH: s2 * D + (h + 1) * DH],
                               nb[:, h * DH:(h + 1) * DH], rden[:, h:h + 1], None,
                               ALU.mult,
                               w=([("DV", vrd), ("PE", last_pe)] + (atf_ring.waits(0) if (s2 == 0 and h == 0) else [])),
                               inc=("DV", 1))
                num_ring.done(s2, ("DV", vat))
                vc = p.op("scalar", "activation", a16[:], atf[:, s2 * D:(s2 + 1) * D],
                          AF.Copy, w=[("DV", vat), ("PE", a16_free[0])],
                          inc=("AC", 1))
                vpe = 0
                for j in range(KC):
                    vpe = p.op("tensor", "transpose",
                               ptC[s2 % 2][:, j * 128:(j + 1) * 128],
                               a16[:, j * 128:(j + 1) * 128], ident16[:],
                               w=([("AC", vc)] + ptC_ring.waits(s2)) if j == 0 else (),
                               inc=("PE", 1))
                a16_free[0] = vpe
                vdve = p.op("vector", "tensor_copy",
                            aff[:].rearrange("p (c t) -> p c t", t=BLK)[:, :, s2 * 128:(s2 + 1) * 128],
                            ptC[s2 % 2][:].rearrange("p (c t) -> p c t", t=128),
                            w=[("PE", vpe)] + (aff_ring.waits(0) if s2 == 0 else []),
                            inc=("DV", 1))
                ptC_ring.done(s2, ("DV", vdve))
            tq_ring.done(blk, ("PE", p.val("PE")))
            vs1 = p.dma("sync", "atsp", tmv(SP_AT, blk),
                        atf[:].rearrange("p (c d) -> p c d", d=D),
                        w=[("AC", p.val("AC")), ("DV", p.val("DV"))])
            atf_ring.done(0, vs1)
            bi = blk % NSUB
            vs2 = p.dma("sync", "afsp",
                        SP_AF.rearrange("(c p) t -> p c t", p=128)[:, :, s * PADL + 2 + bi * BLK: s * PADL + 2 + (bi + 1) * BLK],
                        aff[:].rearrange("p (c t) -> p c t", t=BLK),
                        w=[("DV", p.val("DV"))])
            aff_ring.done(0, vs2)
        p.gate_all()
        ctx.close()
        if ALV <= 4:
            octx.close()
            return

        p.phase = "A3b"
        ctx = ExitStack()
        afw = ctx.enter_context(nc.sbuf_tensor("A3bw", [128, 2 * KC * 516], BF16))
        atb = ctx.enter_context(nc.sbuf_tensor("A3ba", [128, 2 * NSUB * D], F32))
        cw16 = ctx.enter_context(nc.sbuf_tensor("A3bc", [128, 40 * 128], BF16))
        cstg = ctx.enter_context(nc.sbuf_tensor("A3bg", [128, 2 * 1024], F32))
        loc16 = ctx.enter_context(nc.sbuf_tensor("A3bl", [128, 512], BF16))
        gamc = ctx.enter_context(nc.sbuf_tensor("A3bgc", [128, 1], F32))
        cbc = ctx.enter_context(nc.sbuf_tensor("A3bcb", [128, KC], F32))
        gcb = ctx.enter_context(nc.sbuf_tensor("A3bgb", [128, KC], F32))
        lnab = ctx.enter_context(nc.sbuf_tensor("A3bln", [128, 2 * D], F32))
        apr = ctx.enter_context(nc.sbuf_tensor("A3bap", [128, D], F32))
        apo = ctx.enter_context(nc.sbuf_tensor("A3bao", [128, NSUB * D], F32))
        ap16 = ctx.enter_context(nc.sbuf_tensor("A3ba6", [128, D], BF16))
        apf = ctx.enter_context(nc.sbuf_tensor("A3baf", [128, KC * BLK], BF16))
        stc = mkstats(ctx, "A3bs", cols=NSUB)
        plo = [ctx.enter_context(nc.psum_tensor(f"A3bp{i}", [128, 512], F32))
               for i in range(2)]
        ptD = [ctx.enter_context(nc.psum_tensor(f"A3bt{i}", [128, 512], BF16))
               for i in range(2)]
        ptE = ctx.enter_context(nc.psum_tensor("A3be", [128, 1024], BF16))

        cw_done = stream_w_bf16(
            lambda j: a_cw[j * 8:(j + 1) * 8].rearrange("g p n -> p g n"),
            cw16, cstg, nc_chunks=5, chunk_cols=1024)
        lv = p.dma("sync", "gam", gamc[:], a_gam)
        lv2 = p.dma("sync", "cbc", cbc[:], a_cbc)
        vgb = p.op("vector", "tensor_scalar", gcb[:], cbc[:], gamc[:, 0:1], None,
                   ALU.mult, w=[lv, lv2], inc=("DV", 1))
        ln_lv = p.dma("sync", "lna", lnab[:], a_lna_bc)

        afw_ring = Ring(2)
        atb_ring = Ring(2)
        loc_ring = Ring(1)
        plo_ring = Ring(2)
        ptD_ring = Ring(2)
        apo_ring = Ring(1)
        apf_ring = Ring(1)
        for blk in range(NBLK):
            s = blk // NSUB
            bi = blk % NSUB
            bb = blk % 2
            lva = p.dma("sync", "afw%d" % bb,
                        afw[:, bb * KC * 516:(bb + 1) * KC * 516].rearrange(
                            "p (c t) -> p c t", t=516),
                        SP_AF.rearrange("(c p) t -> p c t", p=128)[:, :, s * PADL + bi * BLK: s * PADL + bi * BLK + 516],
                        w=afw_ring.waits(blk))
            lvt = p.dma("sync", "atb%d" % bb,
                        atb[:, bb * NSUB * D:(bb + 1) * NSUB * D].rearrange(
                            "p (c d) -> p c d", d=D),
                        tmv(SP_AT, blk), w=atb_ring.waits(blk))
            for c in range(KC):
                bank = plo[c % 2]
                wl = [lva] + list(cw_done) + plo_ring.waits(c)
                for tp in range(5):
                    vpe = p.op("tensor", "matmul", bank[:],
                               cw16[:, (c * 5 + tp) * 128:(c * 5 + tp + 1) * 128],
                               afw[:, bb * KC * 516 + c * 516 + tp:
                                   bb * KC * 516 + c * 516 + tp + 512],
                               start=(tp == 0), stop=(tp == 4),
                               w=wl if tp == 0 else (), inc=("PE", 1))
                vlc = p.op("scalar", "activation", loc16[:], bank[:], AF.Identity,
                           scale=gamc[:, 0:1], bias=gcb[:, c:c + 1],
                           w=[("PE", vpe), ("DV", vgb)] + loc_ring.waits(0),
                           inc=("AC", 1))
                plo_ring.done(c, ("AC", vlc))
                vdd = 0
                for s2 in range(NSUB):
                    vpt = p.op("tensor", "transpose",
                               ptD[c % 2][:, s2 * 128:(s2 + 1) * 128],
                               loc16[:, s2 * 128:(s2 + 1) * 128], ident16[:],
                               w=([("AC", vlc)] + ptD_ring.waits(c)) if s2 == 0 else (),
                               inc=("PE", 1))
                    vdd = p.op("vector", "tensor_tensor",
                               atb[:, bb * NSUB * D + s2 * D + c * 128:
                                   bb * NSUB * D + s2 * D + (c + 1) * 128],
                               atb[:, bb * NSUB * D + s2 * D + c * 128:
                                   bb * NSUB * D + s2 * D + (c + 1) * 128],
                               ptD[c % 2][:, s2 * 128:(s2 + 1) * 128], ALU.add,
                               w=[("PE", vpt), lvt], inc=("DV", 1))
                loc_ring.done(0, ("PE", vpt))
                ptD_ring.done(c, ("DV", vdd))
            afw_ring.done(blk, ("PE", p.val("PE")))
            ptE_free = [0]
            for s2 in range(NSUB):
                tile = atb[:, bb * NSUB * D + s2 * D: bb * NSUB * D + (s2 + 1) * D]
                ln_stats(p, tile, stc["su"][:, s2:s2 + 1], stc["sq"][:, s2:s2 + 1],
                         [("DV", p.val("DV"))])
                vstat = ln_finish(stc, s2, 1)
                vap = p.op("scalar", "activation", apr[:], tile, AF.Identity,
                           scale=stc["si"][:, s2:s2 + 1], bias=stc["bu"][:, s2:s2 + 1],
                           w=[("DV", vstat)], inc=("AC", 1))
                p.op("vector", "tensor_tensor", apr[:], apr[:], lnab[:, 0:D],
                     ALU.mult, w=[("AC", vap), ln_lv], inc=("DV", 1))
                vdv = p.op("vector", "tensor_tensor", apo[:, s2 * D:(s2 + 1) * D],
                           apr[:], lnab[:, D:2 * D], ALU.add,
                           w=apo_ring.waits(0) if s2 == 0 else (), inc=("DV", 1))
                vc = p.op("scalar", "activation", ap16[:], apo[:, s2 * D:(s2 + 1) * D],
                          AF.Copy, w=[("DV", vdv)], inc=("AC", 1))
                vpe = 0
                for j in range(KC):
                    vpe = p.op("tensor", "transpose", ptE[:, j * 128:(j + 1) * 128],
                               ap16[:, j * 128:(j + 1) * 128], ident16[:],
                               w=([("AC", vc), ("DV", ptE_free[0])]) if j == 0 else (),
                               inc=("PE", 1))
                vdve = p.op("vector", "tensor_copy",
                            apf[:].rearrange("p (c t) -> p c t", t=BLK)[:, :, s2 * 128:(s2 + 1) * 128],
                            ptE[:].rearrange("p (c t) -> p c t", t=128),
                            w=[("PE", vpe)] + (apf_ring.waits(0) if s2 == 0 else []),
                            inc=("DV", 1))
                ptE_free[0] = vdve
            atb_ring.done(blk, ("AC", p.val("AC")))
            vs1 = p.dma("sync", "ptsp", tmv(SP_PT, blk),
                        apo[:].rearrange("p (c d) -> p c d", d=D),
                        w=[("DV", p.val("DV"))])
            apo_ring.done(0, vs1)
            vs2 = p.dma("sync", "pfsp", fmv(SP_PF, blk),
                        apf[:].rearrange("p (c t) -> p c t", t=BLK),
                        w=[("DV", p.val("DV"))])
            apf_ring.done(0, vs2)
        p.gate_all()
        ctx.close()
        if ALV <= 5:
            octx.close()
            return

        p.phase = "A3d"
        ctx = ExitStack()
        xtd = ctx.enter_context(nc.sbuf_tensor("A3dx", [128, 2 * NSUB * D], F32))
        apd = ctx.enter_context(nc.sbuf_tensor("A3da", [128, 2 * NSUB * D], F32))
        utl = ctx.enter_context(nc.sbuf_tensor("A3du", [128, D], F32))
        xtd_ring = Ring(2)
        apd_ring = Ring(2)
        for blk in range(NBLK):
            bb = blk % 2
            xv = p.dma("sync", "xtd%d" % bb,
                       xtd[:, bb * NSUB * D:(bb + 1) * NSUB * D].rearrange(
                           "p (c d) -> p c d", d=D),
                       tmv(src, blk), w=xtd_ring.waits(blk))
            av = p.dma("sync", "apd%d" % bb,
                       apd[:, bb * NSUB * D:(bb + 1) * NSUB * D].rearrange(
                           "p (c d) -> p c d", d=D),
                       tmv(SP_PT, blk), w=apd_ring.waits(blk))
            vdv = 0
            for s2 in range(NSUB):
                i = blk * NSUB + s2
                tile = xtd[:, bb * NSUB * D + s2 * D: bb * NSUB * D + (s2 + 1) * D]
                vu = p.op("scalar", "activation", utl[:], tile, AF.Identity,
                          scale=au[:, i:i + 1], bias=buc[:, i:i + 1],
                          w=[xv, ("DV", vdv)], inc=("AC", 1))
                p.op("vector", "tensor_tensor", tile, tile, utl[:], ALU.add,
                     w=[("AC", vu)], inc=("DV", 1))
                vdv = p.op("vector", "tensor_tensor", tile, tile,
                           apd[:, bb * NSUB * D + s2 * D: bb * NSUB * D + (s2 + 1) * D],
                           ALU.add, w=[av], inc=("DV", 1))
            sv = p.dma("sync", "xds%d" % bb, tmv(dst, blk),
                       xtd[:, bb * NSUB * D:(bb + 1) * NSUB * D].rearrange(
                           "p (c d) -> p c d", d=D),
                       w=[("DV", vdv)])
            xtd_ring.done(blk, sv)
            apd_ring.done(blk, ("DV", vdv))
        p.gate_all()
        ctx.close()
        octx.close()
        if ALV <= 6:
            return

        emit_mlp_pass("A3c", [dict(W1=aW1, W2=aW2, b1=a_b1c[0], b2=a_b2r[0],
                                   gain=None, wcol=lambda blk, sub: None)],
                      SP_PF, dst)

    def emit_attn_b(src, dst):
        p.phase = "Bprep"
        octx = ExitStack()
        wq16 = octx.enter_context(nc.sbuf_tensor("Bwq", [128, KC * D], BF16))
        wk16 = octx.enter_context(nc.sbuf_tensor("Bwk", [128, KC * D], BF16))
        wv16 = octx.enter_context(nc.sbuf_tensor("Bwv", [128, KC * D], BF16))
        wo16 = octx.enter_context(nc.sbuf_tensor("Bwo", [128, KC * D], BF16))
        wstg = octx.enter_context(nc.sbuf_tensor("Bws", [128, 2 * D], F32))
        ptd = octx.enter_context(nc.sbuf_tensor("Bpt", [128, 128], BF16))
        ptds = octx.enter_context(nc.sbuf_tensor("Bpts", [64, DH], F32))
        cbq = octx.enter_context(nc.sbuf_tensor("Bcbq", [128, KC], F32))
        cbk = octx.enter_context(nc.sbuf_tensor("Bcbk", [128, KC], F32))
        brow = octx.enter_context(nc.sbuf_tensor("Bbr", [1, 2 * D], BF16))
        brst = octx.enter_context(nc.sbuf_tensor("Bbs", [1, 2 * D], F32))
        au2 = octx.enter_context(nc.sbuf_tensor("Bau", [128, NT], F32))
        bu2 = octx.enter_context(nc.sbuf_tensor("Bbu", [128, NT], F32))
        kvbs = octx.enter_context(nc.sbuf_tensor("Bkv", [128, 2 * KC * 128], BF16))
        kseB = octx.enter_context(nc.sbuf_tensor("Bkse", [128, 2 * KC * H], BF16))
        ksc = octx.enter_context(nc.sbuf_tensor("Bksc", [128, KC], F32))
        ksc16 = octx.enter_context(nc.sbuf_tensor("Bks6", [128, KC], BF16))
        stc = mkstats(octx, "Bs")
        st4 = mkstats(octx, "B4", cols=4)

        wq_done = stream_w_bf16(lambda j: bWqT[j * 128:(j + 1) * 128, :], wq16, wstg)
        wk_done = stream_w_bf16(lambda j: bWkT[j * 128:(j + 1) * 128, :], wk16, wstg)
        wv_done = stream_w_bf16(lambda j: bWvT[j * 128:(j + 1) * 128, :], wv16, wstg)
        wo_done = stream_w_bf16(lambda j: bWoT[j * 128:(j + 1) * 128, :], wo16, wstg)
        lv = p.dma("sync", "bpt", ptds[:], bPt)
        p.op("vector", "memset", ptd[:], 0.0, inc=("DV", 1))
        p.op("vector", "tensor_copy", ptd[0:64, 0:64], ptds[:], w=[lv],
             inc=("DV", 1))
        ptdv = p.op("vector", "tensor_copy", ptd[64:128, 64:128], ptds[:],
                    inc=("DV", 1))
        lv = p.dma("sync", "bbq", cbq[:], b_bqc)
        lv2 = p.dma("sync", "bbk", cbk[:], b_bkc)
        p.op("vector", "tensor_scalar", cbq[:], cbq[:], CNORM, None, ALU.mult,
             w=[lv, lv2], inc=("DV", 1))
        cbv = p.op("vector", "tensor_scalar", cbk[:], cbk[:], CNORM, None, ALU.mult,
                   inc=("DV", 1))
        lv = p.dma("sync", "bbv", brst[0:1, 0:D], b_bvr)
        lv2 = p.dma("sync", "bbo", brst[0:1, D:2 * D], b_bor)
        brv = p.op("vector", "tensor_copy", brow[:], brst[:],
                   w=[lv, lv2], inc=("DV", 1))
        ksbz = p.op("vector", "memset", kseB[:], 0.0, inc=("DV", 1))

        p.phase = "B1"
        ctx = ExitStack()
        xt = ctx.enter_context(nc.sbuf_tensor("B1x", [128, NSUB * D], F32))
        utm = ctx.enter_context(nc.sbuf_tensor("B1u", [128, 2 * D], BF16))
        ufm = ctx.enter_context(nc.sbuf_tensor("B1uf", [128, KC * BLK], BF16))
        kfsc = ctx.enter_context(nc.sbuf_tensor("B1kf", [128, KC * BLK], BF16))
        krel = ctx.enter_context(nc.sbuf_tensor("B1kr", [128, D], BF16))
        kftm = ctx.enter_context(nc.sbuf_tensor("B1kt", [128, 2 * D], BF16))
        vtm = ctx.enter_context(nc.sbuf_tensor("B1vt", [128, 2 * D], BF16))
        kvaccB = ctx.enter_context(nc.sbuf_tensor("B1kva", [128, KC * 128], F32))
        ksaccB = ctx.enter_context(nc.sbuf_tensor("B1ksa", [128, KC], F32))
        pt = [ctx.enter_context(nc.psum_tensor("B1t0", [128, 1024], BF16))]
        pj = [ctx.enter_context(nc.psum_tensor(f"B1j{i}", [128, 512], F32))
              for i in range(2)]
        pkf = ctx.enter_context(nc.psum_tensor("B1kp", [128, 1024], F32))
        pkv2B = ctx.enter_context(nc.psum_tensor("B1kv", [128, 1024], F32))
        pksC = ctx.enter_context(nc.psum_tensor("B1ks", [128, 512], F32))

        xt_ring = Ring(1)
        utm_ring = Ring(2)
        pt_ring = Ring(1)
        ufm_ring = Ring(1)
        pj_ring = Ring(2)
        kfsc_ring = Ring(1)
        pkf_free = [0]
        krel_free = [0]
        kftm_ring = Ring(2)
        vtm_ring = Ring(2)
        pkv_ringB = Ring(1)
        pks_ringB = Ring(1)

        def make_u(blk, stats_save, xv):
            _ml = 4
            for s2 in range(NSUB):
                i = blk * NSUB + s2
                tile = xt[:, s2 * D:(s2 + 1) * D]
                if stats_save:
                    ln_stats(p, tile, stc["su"][:, i:i + 1], stc["sq"][:, i:i + 1],
                             [xv])
            if _ml < 2:
                xt_ring.done(0, ("DV", p.val("DV")))
                return p.val("DV")
            if stats_save:
                vstat = ln_finish(stc, blk * NSUB, NSUB)
                sl = (slice(None), slice(blk * NSUB, blk * NSUB + NSUB))
                p.op("vector", "tensor_copy", au2[sl], stc["si"][sl],
                     w=[("DV", vstat)], inc=("DV", 1))
                p.op("vector", "tensor_copy", bu2[sl], stc["bu"][sl], inc=("DV", 1))
            if _ml < 3:
                xt_ring.done(0, ("DV", p.val("DV")))
                return p.val("DV")
            uev = 0
            for s2 in range(NSUB):
                i = blk * NSUB + s2
                vu = p.op("scalar", "activation", utm[:, (s2 % 2) * D:(s2 % 2 + 1) * D],
                          xt[:, s2 * D:(s2 + 1) * D], AF.Identity,
                          scale=au2[:, i:i + 1], bias=bu2[:, i:i + 1],
                          w=[xv, ("DV", p.val("DV"))] + utm_ring.waits(s2),
                          inc=("AC", 1))
                if _ml < 4:
                    utm_ring.done(s2, ("AC", vu))
                    uev = p.val("DV")
                    continue
                vpe = 0
                for j in range(KC):
                    vpe = p.op("tensor", "transpose", pt[0][:, j * 128:(j + 1) * 128],
                               utm[:, (s2 % 2) * D + j * 128:(s2 % 2) * D + (j + 1) * 128],
                               ident16[:],
                               w=([("AC", vu)] + pt_ring.waits(0)) if j == 0 else (),
                               inc=("PE", 1))
                utm_ring.done(s2, ("PE", vpe))
                uev = p.op("vector", "tensor_copy",
                           ufm[:].rearrange("p (c t) -> p c t", t=BLK)[:, :, s2 * 128:(s2 + 1) * 128],
                           pt[0][:].rearrange("p (c t) -> p c t", t=128),
                           w=[("PE", vpe)] + (ufm_ring.waits(0) if s2 == 0 else []),
                           inc=("DV", 1))
                pt_ring.done(0, ("DV", uev))
            p.wait_only("tensor", [("DV", uev)])
            return uev

        def fm_projection(w16, wdone, biascol, outfm, out_ring):
            vac = 0
            for j in range(KC):
                bank = pj[j % 2]
                wl = list(wdone) + [("DV", cbv)] + pj_ring.waits(j % 2)
                if j == 0:
                    wl += out_ring.waits(0)
                for kc in range(KC):
                    vpe = p.op("tensor", "matmul", bank[:],
                               w16[:, kc * D + j * 128: kc * D + (j + 1) * 128],
                               ufm[:, kc * BLK:(kc + 1) * BLK],
                               start=(kc == 0), stop=(kc == KC - 1),
                               w=wl if kc == 0 else (), inc=("PE", 1))
                vac = p.op("scalar", "activation", outfm[:, j * BLK:(j + 1) * BLK],
                           bank[:], AF.Identity, scale=c_cnorm, bias=biascol[:, j:j + 1],
                           w=[("PE", vpe)], inc=("AC", 1))
                pj_ring.done(j % 2, ("AC", vac))
            return vac

        for s in range(2 if "no_b1" not in DEBUG else 0):
            last_kv = 0
            last_ks = 0
            vz_kvB = p.op("vector", "memset", kvaccB[:], 0.0, inc=("DV", 1))
            p.op("vector", "memset", ksaccB[:], 0.0, inc=("DV", 1))
            for bi in range(NSUB):
                blk = s * NSUB + bi
                xv = p.dma("sync", "b1xt",
                           xt[:].rearrange("p (c d) -> p c d", d=D),
                           tmv(src, blk), w=xt_ring.waits(0))
                make_u(blk, True, xv)
                _lvl = 5
                if _lvl < 2:
                    xt_ring.done(0, ("AC", p.val("AC")))
                    ufm_ring.done(0, ("DV", p.val("DV")))
                    continue
                kv_ac = fm_projection(wk16, wk_done, cbk, kfsc, kfsc_ring)
                if _lvl < 3:
                    xt_ring.done(0, ("AC", p.val("AC")))
                    ufm_ring.done(0, ("PE", p.val("PE")))
                    kfsc_ring.done(0, ("AC", p.val("AC")))
                    continue
                for s2 in range(NSUB):
                    ti = blk * NSUB + s2
                    for c in range(KC):
                        wl = [("AC", kv_ac), ("DV", ptdv)]
                        if c == 0:
                            wl.append(("AC", pkf_free[0]))
                        vpe = p.op(
                            "tensor", "matmul", pkf[:, c * 128:(c + 1) * 128],
                            kfsc[:, c * BLK + s2 * 128: c * BLK + (s2 + 1) * 128],
                            ptd[:],
                            start=True, stop=True, skip_group_check=True,
                            w=wl if c == 0 else (), inc=("PE", 1))
                    vre = p.op("scalar", "activation", krel[:], pkf[:], AF.Relu,
                               w=[("PE", vpe), ("DV", krel_free[0])], inc=("AC", 1))
                    pkf_free[0] = vre
                    vkf = p.op("vector", "tensor_scalar",
                               kftm[:, (s2 % 2) * D:(s2 % 2 + 1) * D], krel[:],
                               EPS_A, None, ALU.add,
                               w=[("AC", vre)] + kftm_ring.waits(s2), inc=("DV", 1))
                    krel_free[0] = vkf
                    if _lvl < 4:
                        kftm_ring.done(s2, ("DV", vkf))
                        continue
                    vdv = 0
                    for fh in range(2):
                        bank = pj[fh]
                        wl = list(wv_done) + [("DV", brv)] + pj_ring.waits(fh)
                        p.op("tensor", "matmul", bank[:], ones16r[0:1, 0:128],
                             brow[0:1, fh * 512:(fh + 1) * 512],
                             start=True, stop=False, w=wl, inc=("PE", 1))
                        for kc in range(KC):
                            vpe = p.op("tensor", "matmul", bank[:],
                                       ufm[:, kc * BLK + s2 * 128: kc * BLK + (s2 + 1) * 128],
                                       wv16[:, kc * D + fh * 512: kc * D + (fh + 1) * 512],
                                       start=False, stop=(kc == KC - 1),
                                       inc=("PE", 1))
                        vdv = p.op("vector", "tensor_copy",
                                   vtm[:, (s2 % 2) * D + fh * 512:(s2 % 2) * D + (fh + 1) * 512],
                                   bank[:],
                                   w=[("PE", vpe)] + (vtm_ring.waits(s2) if fh == 0 else []),
                                   inc=("DV", 1))
                        pj_ring.done(fh, ("DV", vdv))
                    if _lvl < 5:
                        kftm_ring.done(s2, ("DV", vkf))
                        vtm_ring.done(s2, ("DV", vdv))
                        continue
                    first = (bi == 0 and s2 == 0)
                    for c in range(KC):
                        wl = [("DV", vkf), ("DV", vdv)] + (pkv_ringB.waits(0)
                                                           if c == 0 else [])
                        last_kv = p.op(
                            "tensor", "matmul", pkv2B[:, c * 128:(c + 1) * 128],
                            kftm[:, (s2 % 2) * D + c * 128:(s2 % 2) * D + (c + 1) * 128],
                            vtm[:, (s2 % 2) * D + c * 128:(s2 % 2) * D + (c + 1) * 128],
                            start=True, stop=True, skip_group_check=True,
                            w=wl if c == 0 else (), inc=("PE", 1))
                    for g in range(KC):
                        wl = pks_ringB.waits(0) if g == 0 else []
                        last_ks = p.op(
                            "tensor", "matmul", pksC[:, g * 4:g * 4 + 1],
                            kftm[:, (s2 % 2) * D + g * 128:(s2 % 2) * D + (g + 1) * 128],
                            ones16c[:],
                            start=True, stop=True, skip_group_check=True,
                            w=wl, inc=("PE", 1))
                    vkva = p.op("vector", "tensor_tensor", kvaccB[:], kvaccB[:],
                                pkv2B[:], ALU.add, w=[("PE", last_kv), ("DV", vz_kvB)],
                                inc=("DV", 1))
                    pkv_ringB.done(0, ("DV", vkva))
                    vksa = p.op("vector", "tensor_tensor", ksaccB[:], ksaccB[:],
                                pksC[:, 0:32].rearrange("p (g o) -> p g o", o=4)[:, :, 0],
                                ALU.add, w=[("PE", last_ks)], inc=("DV", 1))
                    pks_ringB.done(0, ("DV", vksa))
                    kftm_ring.done(s2, ("PE", max(last_kv, last_ks)))
                    vtm_ring.done(s2, ("PE", last_kv))
                xt_ring.done(0, ("AC", p.val("AC")))
                kfsc_ring.done(0, ("PE", p.val("PE")))
                ufm_ring.done(0, ("PE", p.val("PE")))
            vks = p.op("vector", "tensor_copy", ksc16[:], ksaccB[:], inc=("DV", 1))
            for c in range(KC):
                p.op("vector", "tensor_copy",
                     kseB[0:64, (s * KC + c) * H + 2 * c:(s * KC + c) * H + 2 * c + 1],
                     ksc16[0:64, c:c + 1], w=[("DV", ksbz)], inc=("DV", 1))
                p.op("vector", "tensor_copy",
                     kseB[64:128, (s * KC + c) * H + 2 * c + 1:(s * KC + c) * H + 2 * c + 2],
                     ksc16[64:128, c:c + 1], inc=("DV", 1))
            p.op("vector", "memset", kvbs[:, s * KC * 128:(s + 1) * KC * 128], 0.0,
                 inc=("DV", 1))
            for c in range(KC):
                p.op("vector", "tensor_copy",
                     kvbs[0:64, s * KC * 128 + c * 128: s * KC * 128 + c * 128 + 64],
                     kvaccB[0:64, c * 128:c * 128 + 64], inc=("DV", 1))
                p.op("vector", "tensor_copy",
                     kvbs[64:128, s * KC * 128 + c * 128 + 64: s * KC * 128 + (c + 1) * 128],
                     kvaccB[64:128, c * 128 + 64:(c + 1) * 128], inc=("DV", 1))
        p.gate_all()
        ctx.close()

        p.phase = "B2"
        if "no_b2" in DEBUG:
            octx.close()
            return
        ctx = ExitStack()
        xt2 = ctx.enter_context(nc.sbuf_tensor("B2x", [128, NSUB * D], F32))
        utm2 = ctx.enter_context(nc.sbuf_tensor("B2u", [128, 2 * D], BF16))
        ufm2 = ctx.enter_context(nc.sbuf_tensor("B2uf", [128, KC * BLK], BF16))
        qfsc = ctx.enter_context(nc.sbuf_tensor("B2qf", [128, KC * BLK], BF16))
        qrel = ctx.enter_context(nc.sbuf_tensor("B2qr", [128, 512], BF16))
        qf16 = ctx.enter_context(nc.sbuf_tensor("B2q6", [128, KC * BLK], BF16))
        at16 = ctx.enter_context(nc.sbuf_tensor("B2a6", [128, D], BF16))
        abfm = ctx.enter_context(nc.sbuf_tensor("B2ab", [128, KC * BLK], BF16))
        rdb = ctx.enter_context(nc.sbuf_tensor("B2rd", [128, H], F32))
        pt2 = [ctx.enter_context(nc.psum_tensor(f"B2t{i}", [128, 1024], BF16))
               for i in range(2)]
        pj2 = [ctx.enter_context(nc.psum_tensor(f"B2j{i}", [128, 512], F32))
               for i in range(2)]
        pqf = ctx.enter_context(nc.psum_tensor("B2qp", [128, 512], F32))
        pnb = ctx.enter_context(nc.psum_tensor("B2n", [128, 1024], F32))
        pdb = ctx.enter_context(nc.psum_tensor("B2d", [128, 512], F32))

        xt_ring2 = Ring(1)
        utm_ring2 = Ring(2)
        pt_ring2 = Ring(2)
        ufm_ring2 = Ring(1)
        pj_ring2 = Ring(2)
        qfsc_ring = Ring(1)
        pqf_free = [0]
        qrel_free = [0]
        qf16_ring = Ring(1)
        pnb_free = [0]
        pdb_free = [0]
        at16_free = [0]
        abfm_ring = Ring(1)
        for blk in range(NBLK):
            s = blk // NSUB
            xv = p.dma("sync", "b2xt",
                       xt2[:].rearrange("p (c d) -> p c d", d=D),
                       tmv(src, blk), w=xt_ring2.waits(0))
            uev = 0
            for s2 in range(NSUB):
                i = blk * NSUB + s2
                vu = p.op("scalar", "activation", utm2[:, (s2 % 2) * D:(s2 % 2 + 1) * D],
                          xt2[:, s2 * D:(s2 + 1) * D], AF.Identity,
                          scale=au2[:, i:i + 1], bias=bu2[:, i:i + 1],
                          w=[xv] + utm_ring2.waits(s2), inc=("AC", 1))
                vpe = 0
                for j in range(KC):
                    vpe = p.op("tensor", "transpose", pt2[s2 % 2][:, j * 128:(j + 1) * 128],
                               utm2[:, (s2 % 2) * D + j * 128:(s2 % 2) * D + (j + 1) * 128],
                               ident16[:],
                               w=([("AC", vu)] + pt_ring2.waits(s2)) if j == 0 else (),
                               inc=("PE", 1))
                utm_ring2.done(s2, ("PE", vpe))
                uev = p.op("vector", "tensor_copy",
                           ufm2[:].rearrange("p (c t) -> p c t", t=BLK)[:, :, s2 * 128:(s2 + 1) * 128],
                           pt2[s2 % 2][:].rearrange("p (c t) -> p c t", t=128),
                           w=[("PE", vpe)] + (ufm_ring2.waits(0) if s2 == 0 else []),
                           inc=("DV", 1))
                pt_ring2.done(s2, ("DV", uev))
            p.wait_only("tensor", [("DV", uev)])
            vac = 0
            for j in range(KC):
                bank = pj2[j % 2]
                wl = list(wq_done) + [("DV", cbv)] + pj_ring2.waits(j % 2)
                if j == 0:
                    wl += qfsc_ring.waits(0)
                for kc in range(KC):
                    vpe = p.op("tensor", "matmul", bank[:],
                               wq16[:, kc * D + j * 128: kc * D + (j + 1) * 128],
                               ufm2[:, kc * BLK:(kc + 1) * BLK],
                               start=(kc == 0), stop=(kc == KC - 1),
                               w=wl if kc == 0 else (), inc=("PE", 1))
                vac = p.op("scalar", "activation", qfsc[:, j * BLK:(j + 1) * BLK],
                           bank[:], AF.Identity, scale=c_cnorm, bias=cbq[:, j:j + 1],
                           w=[("PE", vpe)], inc=("AC", 1))
                pj_ring2.done(j % 2, ("AC", vac))
            vqf = 0
            for c in range(KC):
                wl = [("AC", vac), ("DV", ptdv), ("AC", pqf_free[0])]
                vpe = p.op("tensor", "matmul", pqf[:, 0:512],
                           ptd[:], qfsc[:, c * BLK:(c + 1) * BLK],
                           start=True, stop=True, skip_group_check=True,
                           w=wl, inc=("PE", 1))
                vre = p.op("scalar", "activation", qrel[:], pqf[:], AF.Relu,
                           w=[("PE", vpe), ("DV", qrel_free[0])], inc=("AC", 1))
                pqf_free[0] = vre
                qrel_free[0] = p.op("vector", "tensor_scalar",
                                    qf16[:, c * BLK:(c + 1) * BLK], qrel[:],
                                    EPS_A, None, ALU.add,
                                    w=[("AC", vre)] + (qf16_ring.waits(0) if c == 0 else []),
                                    inc=("DV", 1))
            vqf = qrel_free[0]
            vwo_dv = 0
            for s2 in range(NSUB):
                for c in range(KC):
                    wl = [("DV", vqf)]
                    if c == 0:
                        wl.append(("DV", pnb_free[0]))
                    vnm = p.op(
                        "tensor", "matmul", pnb[:, c * 128:(c + 1) * 128],
                        qf16[:, c * BLK + s2 * 128: c * BLK + (s2 + 1) * 128],
                        kvbs[:, s * KC * 128 + c * 128: s * KC * 128 + (c + 1) * 128],
                        start=True, stop=True, skip_group_check=True,
                        w=wl if c == 0 else (), inc=("PE", 1))
                for kc in range(KC):
                    wl = [("DV", vqf)]
                    if kc == 0:
                        wl.append(("DV", pdb_free[0]))
                    vde = p.op("tensor", "matmul", pdb[0:128, 0:16],
                               qf16[:, kc * BLK + s2 * 128: kc * BLK + (s2 + 1) * 128],
                               kseB[:, (s * KC + kc) * H:(s * KC + kc + 1) * H],
                               start=(kc == 0), stop=(kc == KC - 1),
                               skip_group_check=True, w=wl, inc=("PE", 1))
                p.op("vector", "tensor_scalar", rdb[:], pdb[0:128, 0:16], EPS_A,
                     None, ALU.add, w=[("PE", vde)], inc=("DV", 1))
                vrd = p.op("vector", "reciprocal", rdb[:], rdb[:], inc=("DV", 1))
                pdb_free[0] = vrd
                vat = 0
                for h in range(H):
                    vat = p.op("vector", "tensor_scalar",
                               at16[:, h * DH:(h + 1) * DH],
                               pnb[:, h * DH:(h + 1) * DH], rdb[:, h:h + 1], None,
                               ALU.mult,
                               w=([("DV", vrd), ("PE", vnm), ("PE", at16_free[0])] if h == 0 else ()),
                               inc=("DV", 1))
                pnb_free[0] = vat
                vpe = 0
                for j in range(KC):
                    vpe = p.op("tensor", "transpose",
                               pt2[s2 % 2][:, j * 128:(j + 1) * 128],
                               at16[:, j * 128:(j + 1) * 128], ident16[:],
                               w=([("DV", vat)] + pt_ring2.waits(s2)) if j == 0 else (),
                               inc=("PE", 1))
                at16_free[0] = vpe
                vab = p.op("vector", "tensor_copy",
                           abfm[:].rearrange("p (c t) -> p c t", t=BLK)[:, :, s2 * 128:(s2 + 1) * 128],
                           pt2[s2 % 2][:].rearrange("p (c t) -> p c t", t=128),
                           w=[("PE", vpe)] + (abfm_ring.waits(0) if s2 == 0 else []),
                           inc=("DV", 1))
                pt_ring2.done(s2, ("DV", vab))
                for fh in range(2):
                    bank = pj2[fh]
                    wl = list(wo_done) + [("DV", brv), ("DV", vab)]
                    wl += pj_ring2.waits(fh)
                    p.op("tensor", "matmul", bank[:], ones16r[0:1, 0:128],
                         brow[0:1, D + fh * 512: D + (fh + 1) * 512],
                         start=True, stop=False, w=wl, inc=("PE", 1))
                    for kc in range(KC):
                        vpe = p.op("tensor", "matmul", bank[:],
                                   abfm[:, kc * BLK + s2 * 128: kc * BLK + (s2 + 1) * 128],
                                   wo16[:, kc * D + fh * 512: kc * D + (fh + 1) * 512],
                                   start=False, stop=(kc == KC - 1), inc=("PE", 1))
                    dsl = xt2[:, s2 * D + fh * 512: s2 * D + (fh + 1) * 512]
                    vwo_dv = p.op("vector", "scalar_tensor_tensor", dsl, bank[:],
                                  1.0, dsl, ALU.mult, ALU.add,
                                  w=[("PE", vpe)], inc=("DV", 1))
                    pj_ring2.done(fh, ("DV", vwo_dv))
            abfm_ring.done(0, ("PE", p.val("PE")))
            qf16_ring.done(0, ("PE", p.val("PE")))
            qfsc_ring.done(0, ("PE", p.val("PE")))
            ufm_ring2.done(0, ("PE", p.val("PE")))
            sv = p.dma("sync", "b2st", tmv(dst, blk),
                       xt2[:].rearrange("p (c d) -> p c d", d=D),
                       w=[("DV", vwo_dv)])
            xt_ring2.done(0, sv)
        p.gate_all()
        ctx.close()
        octx.close()

    srcs = []
    cur = x_in
    ffn_seen = [0]
    for i, ph in enumerate(seq):
        dst = y_out if i == len(seq) - 1 else XR
        if ph == "ffn":
            emit_ffn(f"F{i}_", cur, dst,
                     wb="save" if ffn_seen[0] == 0 else "load")
            ffn_seen[0] += 1
        elif ph == "attn_a":
            emit_attn_a(cur, dst)
        elif ph == "attn_b":
            emit_attn_b(cur, dst)
        cur = dst

    p.wait_only("sync", [(s, v) for s, v in p.cnt.items() if v > 0])
    p.finish()
    p.ctx.close()
    return nc


SEQ = ("attn_a", "ffn", "attn_b", "ffn")


def prep_inputs(inputs):
    g = lambda n: np.ascontiguousarray(np.asarray(inputs[n], dtype=np.float32))
    cols = lambda v: np.ascontiguousarray(np.asarray(v, np.float32).reshape(-1, 128).T)

    def interleave(gv, bv):
        arr = np.empty((128, 2 * KC), np.float32)
        arr[:, 0::2] = cols(gv)
        arr[:, 1::2] = cols(bv)
        return np.ascontiguousarray(arr)

    def bc2(gv, bv):
        out = np.empty((128, 2 * D), np.float32)
        out[:, 0:D] = np.asarray(gv, np.float32)[None, :]
        out[:, D:2 * D] = np.asarray(bv, np.float32)[None, :]
        return np.ascontiguousarray(out)

    cm = {}
    cm["aWq"] = g("L0_attn_a_Wq"); cm["aWk"] = g("L0_attn_a_Wk")
    cm["aWv"] = g("L0_attn_a_Wv")
    cm["aWpq"] = g("L0_attn_a_Wphi_q"); cm["aWpk"] = g("L0_attn_a_Wphi_k")
    cm["a_brows"] = np.concatenate(
        [g("L0_attn_a_bq"), g("L0_attn_a_bk"), g("L0_attn_a_bv"),
         g("L0_attn_a_bphi_q"), g("L0_attn_a_bphi_k")]).reshape(1, 5 * D)
    cm["a_lnq"] = interleave(inputs["L0_attn_a_lnq_g"], inputs["L0_attn_a_lnq_b"])
    cm["a_lnk"] = interleave(inputs["L0_attn_a_lnk_g"], inputs["L0_attn_a_lnk_b"])
    cm["a_lnv_bc"] = bc2(inputs["L0_attn_a_lnv_g"], inputs["L0_attn_a_lnv_b"])
    cm["a_lna_bc"] = bc2(inputs["L0_attn_a_lna_g"], inputs["L0_attn_a_lna_b"])
    cm["aW1"] = g("L0_attn_a_W1")
    cm["a_b1c"] = np.ascontiguousarray(
        g("L0_attn_a_b1").reshape(1, HC, 128).transpose(0, 2, 1))
    cm["aW2"] = g("L0_attn_a_W2")
    cm["a_b2r"] = g("L0_attn_a_b2").reshape(1, 1, D)
    cm["a_gam"] = np.ascontiguousarray(
        np.broadcast_to(np.asarray(inputs["L0_attn_a_gamma"], np.float32), (128, 1)))
    cm["a_cbc"] = cols(inputs["L0_attn_a_conv_b"])
    w4 = g("L0_attn_a_conv_w").reshape(H, DH, DH, 5)
    cw = np.zeros((KC * 5, 128, 128), np.float32)
    for c in range(KC):
        for t in range(5):
            for par in range(2):
                cw[c * 5 + t, par * 64:(par + 1) * 64, par * 64:(par + 1) * 64] = \
                    w4[2 * c + par, :, :, t].T
    cm["a_cw"] = cw
    cm["bWqT"] = np.ascontiguousarray(g("L0_attn_b_Wq").T)
    cm["bWkT"] = np.ascontiguousarray(g("L0_attn_b_Wk").T)
    cm["bWvT"] = np.ascontiguousarray(g("L0_attn_b_Wv").T)
    cm["bWoT"] = np.ascontiguousarray(g("L0_attn_b_Wo").T)
    cm["b_bqc"] = cols(inputs["L0_attn_b_bq"])
    cm["b_bkc"] = cols(inputs["L0_attn_b_bk"])
    cm["b_bvr"] = g("L0_attn_b_bv").reshape(1, D)
    cm["b_bor"] = g("L0_attn_b_bo").reshape(1, D)
    cm["bPt"] = np.ascontiguousarray(g("L0_attn_b_P").T)
    cm["fW1s"] = g("L0_ffn_W1_s"); cm["fW2s"] = g("L0_ffn_W2_s")
    cm["fb1s"] = np.ascontiguousarray(
        g("L0_ffn_b1_s").reshape(2, HC, 128).transpose(0, 2, 1))
    cm["fb2s"] = g("L0_ffn_b2_s").reshape(2, 1, D)
    cm["fW1r"] = g("L0_ffn_W1_r"); cm["fW2r"] = g("L0_ffn_W2_r")
    cm["fb1r"] = np.ascontiguousarray(
        g("L0_ffn_b1_r").reshape(8, HC, 128).transpose(0, 2, 1))
    cm["fb2r"] = g("L0_ffn_b2_r").reshape(8, 1, D)
    cm["f_gs"] = cols(inputs["L0_ffn_g_shared"])
    cm["f_gr"] = cols(inputs["L0_ffn_g_routed"])
    cm["f_cen"] = g("L0_ffn_centroids")
    cm["c_ident"] = np.eye(128, dtype=np.float32)
    cm["c_onesr"] = np.ones((1, 512), np.float32)
    x = g("L0_attn_a_x")
    xs = [np.ascontiguousarray(x[2 * c:2 * c + 2].reshape(T, D))
          for c in range(NCORES)]
    return cm, xs


def kernel(**inputs):
    cm, xs = prep_inputs(inputs)
    nc = build_nc(SEQ)
    in_maps = [dict(cm, x=xs[c]) for c in range(NCORES)]
    res = run_bass_kernel_spmd(nc, in_maps, core_ids=list(range(NCORES)))
    out = np.concatenate([np.asarray(r["y"]).reshape(2, L, D)
                          for r in res.results], axis=0)
    return out


# version 05
# speedup vs baseline: 1.2622x; 1.2622x over previous
import numpy as np
import ml_dtypes
import concourse.bass as bass
import concourse.mybir as mybir
from concourse.bass_utils import run_bass_kernel_spmd
from contextlib import ExitStack

F32 = mybir.dt.float32
BF16 = mybir.dt.bfloat16
AF = mybir.ActivationFunctionType
ALU = mybir.AluOpType
AX = mybir.AxisListType

NCORES = 8
ALV = 99
ASUB = 99
BC = 2
L = 2048
T = BC * L
D = 1024
H = 16
DH = 64
MLP = 4096
KC = 8
HC = 32
NBLK = 8
BLK = 512
NSUB = 4
NT = 32
EPS_LN = 1e-5
EPS_RMS = float(np.finfo(np.float32).eps)
EPS_A = 1e-6
CNORM = float(1.0 / np.sqrt(np.sqrt(DH)))

ENGINES = ("sync", "tensor", "vector", "scalar", "gpsimd")
DEBUG = set()


class Prog:

    NPOOL = 14
    GPOOL = 4
    APOOL = 2

    def __init__(self, nc):
        self.nc = nc
        self.ops = {e: [] for e in ENGINES}
        self.sems = {}
        self.cnt = {}
        self.ctx = ExitStack()
        self.gates = {e: [] for e in ENGINES}
        for s in ("PE", "AC", "DV", "PL"):
            self.sem(s)
        for i in range(self.NPOOL):
            self.sem(f"Q{i}")
        for i in range(self.GPOOL):
            self.sem(f"G{i}")
        for i in range(self.APOOL):
            self.sem(f"A{i}")
        self.dma_keys = {}
        self.phase = "init"

    def dma(self, queue, key, out, in_, w=(), **kw):
        pool, npool = {"gpsimd": ("G", self.GPOOL), "sync": ("Q", self.NPOOL),
                       "scalar": ("A", self.APOOL)}[queue]
        k = (queue, key)
        if k not in self.dma_keys:
            used = sum(1 for (q, _) in self.dma_keys if q == queue)
            assert used < npool, f"dma sem pool exhausted for {key}"
            self.dma_keys[k] = f"{pool}{used}"
        s = self.dma_keys[k]
        v = self.op(queue, "dma_start", w=w, inc=(s, 16), out=out, in_=in_, **kw)
        return (s, v)

    def sem(self, name):
        if name not in self.sems:
            self.sems[name] = self.ctx.enter_context(self.nc.semaphore(name))
            self.cnt[name] = 0
        return self.sems[name]

    _ENG_SEM = {"vector": "DV", "scalar": "AC", "gpsimd": "PL"}

    def op(self, eng, meth, *args, w=(), inc=None, **kw):
        for x in w:
            if x is not None:
                assert x[1] <= self.cnt[x[0]], \
                    f"future wait {x} > {self.cnt[x[0]]} ({eng} {meth})"
        waits = self.gates[eng] + [x for x in w if x is not None and x[1] > 0]
        es = self._ENG_SEM.get(eng)
        if es is not None and meth != "dma_start":
            if self.cnt[es] > 0:
                waits.append((es, self.cnt[es]))
            if inc is None:
                inc = (es, 1)
        self.gates[eng] = []
        rec = [meth, args, kw, waits, [], self.phase]
        self.ops[eng].append(rec)
        if inc is not None:
            s, n = inc
            self.cnt[s] += n
            rec[4].append((s, n))
            return self.cnt[s]
        return None

    def wait_only(self, eng, w):
        waits = self.gates[eng] + [x for x in w if x is not None and x[1] > 0]
        self.gates[eng] = []
        if waits:
            self.ops[eng].append([None, (), {}, waits, [], self.phase])

    def close(self, eng, semname):
        for rec in reversed(self.ops[eng]):
            if rec[0] is None:
                continue
            if rec[4]:
                return self.cnt[semname]
            self.cnt[semname] += 1
            rec[4].append((semname, 1))
            return self.cnt[semname]
        return self.cnt[semname]

    def val(self, name):
        return self.cnt[name]

    def gate_all(self):
        for e in ("tensor", "vector", "scalar", "gpsimd"):
            self.close(e, {"tensor": "PE", "vector": "DV", "scalar": "AC",
                           "gpsimd": "PL"}[e])
        snap = [(s, v) for s, v in self.cnt.items() if v > 0]
        for e in ENGINES:
            self.gates[e] = list(snap)
        self.dma_keys = {}

    def finish(self):
        sems = self.sems

        nc = self.nc

        def runner(recs):
            def f(engobj):
                cur = [None, None]

                def setph(ph):
                    if ph == cur[0]:
                        return
                    if cur[0] is not None:
                        nc.leave_named_scope(cur[0], cur[1], False)
                    sid, _ = nc.enter_named_scope(ph, False)
                    cur[0], cur[1] = ph, sid
                for meth, args, kw, waits, incs, ph in recs:
                    setph(ph)
                    for s, v in waits:
                        engobj.wait_ge(sems[s], v)
                    if meth is None:
                        continue
                    ins = getattr(engobj, meth)(*args, **kw)
                    for s, n in incs:
                        ins.then_inc(sems[s], n)
                if cur[0] is not None:
                    nc.leave_named_scope(cur[0], cur[1], False)
            return f

        blk = self.ctx.enter_context(self.nc.Block())
        blk.sync(runner(self.ops["sync"]))
        blk.tensor(runner(self.ops["tensor"]))
        blk.vector(runner(self.ops["vector"]))
        blk.scalar(runner(self.ops["scalar"]))
        blk.gpsimd(runner(self.ops["gpsimd"]))


class Ring:

    def __init__(self, n):
        self.n = n
        self.free = [[] for _ in range(n)]

    def waits(self, i):
        return list(self.free[i % self.n])

    def done(self, i, *sv):
        self.free[i % self.n] = [x for x in sv if x is not None]


def build_nc(seq=("attn_a", "ffn", "attn_b", "ffn")):
    nc = bass.Bass("TRN2", target_bir_lowering=False)
    p = Prog(nc)

    def din(name, shape):
        return nc.dram_tensor(name, list(shape), F32, kind="ExternalInput").ap()

    x_in = din("x", (T, D))
    y_out = nc.dram_tensor("y", [T, D], F32, kind="ExternalOutput").ap()

    aWq = din("aWq", (D, D)); aWk = din("aWk", (D, D)); aWv = din("aWv", (D, D))
    aWpq = din("aWpq", (D, D)); aWpk = din("aWpk", (D, D))
    a_brows = din("a_brows", (1, 5 * D))
    a_lnq = din("a_lnq", (128, 2 * KC))
    a_lnk = din("a_lnk", (128, 2 * KC))
    a_lnv_bc = din("a_lnv_bc", (128, 2 * D))
    a_lna_bc = din("a_lna_bc", (128, 2 * D))
    aW1 = din("aW1", (D, MLP)); a_b1c = din("a_b1c", (1, 128, HC))
    aW2 = din("aW2", (MLP, D)); a_b2r = din("a_b2r", (1, 1, D))
    a_gam = din("a_gam", (128, 1))
    a_cbc = din("a_cbc", (128, KC))
    a_cw = din("a_cw", (KC * 5, 128, 128))
    bWqT = din("bWqT", (D, D)); bWkT = din("bWkT", (D, D))
    bWvT = din("bWvT", (D, D)); bWoT = din("bWoT", (D, D))
    b_bqc = din("b_bqc", (128, KC)); b_bkc = din("b_bkc", (128, KC))
    b_bvr = din("b_bvr", (1, D)); b_bor = din("b_bor", (1, D))
    bPt = din("bPt", (DH, DH))
    fW1s = din("fW1s", (2, D, MLP)); fW2s = din("fW2s", (2, MLP, D))
    fb1s = din("fb1s", (2, 128, HC)); fb2s = din("fb2s", (2, 1, D))
    fW1r = din("fW1r", (8, D, MLP)); fW2r = din("fW2r", (8, MLP, D))
    fb1r = din("fb1r", (8, 128, HC)); fb2r = din("fb2r", (8, 1, D))
    f_gs = din("f_gs", (128, KC)); f_gr = din("f_gr", (128, KC))
    f_cen = din("f_cen", (D, 8))
    c_ident = din("c_ident", (128, 128))
    c_onesr = din("c_onesr", (1, 512))

    XR = nc.dram_tensor("XR", [T, D], F32).ap()
    SP_TQ = nc.dram_tensor("SP_TQ", [D, T], BF16).ap()
    SP_K = nc.dram_tensor("SP_K", [D, T], BF16).ap()
    SP_V = nc.dram_tensor("SP_V", [T, D], BF16).ap()
    SP_RN = nc.dram_tensor("SP_RN", [D, T], BF16).ap()
    SP_AT = nc.dram_tensor("SP_AT", [T, D], F32).ap()
    PADL = L + 4
    SP_AF = nc.dram_tensor("SP_AF", [D, BC * PADL], BF16).ap()
    SP_PT = nc.dram_tensor("SP_PT", [T, D], F32).ap()
    SP_PF = nc.dram_tensor("SP_PF", [D, T], BF16).ap()
    BNC = nc.dram_tensor("BNC", [64, 64], F32).ap()

    def fmv(sp, blk):
        return sp.rearrange("(c p) t -> p c t", p=128)[:, :, blk * BLK:(blk + 1) * BLK]

    def tmv(sp, blk):
        return sp[blk * BLK:(blk + 1) * BLK, :].rearrange("(c p) d -> p c d", p=128)

    sbt = p.ctx.enter_context
    identf = sbt(nc.sbuf_tensor("identf", [128, 128], F32))
    ident16 = sbt(nc.sbuf_tensor("ident16", [128, 128], BF16))
    ones16c = sbt(nc.sbuf_tensor("ones16c", [128, 1], BF16))
    ones16r = sbt(nc.sbuf_tensor("ones16r", [1, 512], BF16))
    junk = sbt(nc.sbuf_tensor("junk", [128, D], F32))
    cvals = sbt(nc.sbuf_tensor("cvals", [128, 3], F32))

    v = p.dma("sync", "ident", identf[:], c_ident)
    p.op("vector", "tensor_copy", ident16[:], identf[:], w=[v], inc=("DV", 1))
    v = p.dma("sync", "ones", junk[0:1, 0:512], c_onesr)
    p.op("vector", "tensor_copy", ones16r[:], junk[0:1, 0:512], w=[v],
         inc=("DV", 1))
    p.op("vector", "memset", ones16c[:], 1.0, inc=("DV", 1))
    p.op("vector", "memset", cvals[:, 0:1], EPS_LN, inc=("DV", 1))
    p.op("vector", "memset", cvals[:, 1:2], EPS_RMS, inc=("DV", 1))
    p.op("vector", "memset", cvals[:, 2:3], CNORM, inc=("DV", 1))
    c_epsln = cvals[:, 0:1]
    c_epsrms = cvals[:, 1:2]
    c_cnorm = cvals[:, 2:3]
    p.gate_all()

    def stream_w_bf16(wap_chunks, dst16, stage, gain=None, gate_pe=0, nc_chunks=8,
                      chunk_cols=D, conv="pool", nslot=2, gate_w=()):
        frees = [None] * nslot
        last = {}
        for j in range(nc_chunks):
            slot = j % nslot
            wl = []
            if j == 0 and gate_pe:
                wl.append(("PE", gate_pe))
            if j < nslot:
                wl += list(gate_w)
            if frees[slot]:
                wl.append(frees[slot])
            src = wap_chunks(j)
            stg = stage[:, slot * chunk_cols:(slot + 1) * chunk_cols]
            if len(src.shape) == 3:
                stg = stg.rearrange("p (c f) -> p c f", f=src.shape[2])
            lv = p.dma("gpsimd", f"wst{slot}", stg, src, w=wl)
            eng = ("gpsimd" if conv == "pool" or (conv == "mix" and j % 4 == 3)
                   else "vector")
            sem = "PL" if eng == "gpsimd" else "DV"
            dsl = dst16[:, j * chunk_cols:(j + 1) * chunk_cols]
            sstg = stage[:, slot * chunk_cols:(slot + 1) * chunk_cols]
            if gain is not None:
                v = p.op(eng, "tensor_scalar", dsl, sstg, gain(j), None,
                         ALU.mult, w=[lv], inc=(sem, 1))
            else:
                v = p.op(eng, "tensor_copy", dsl, sstg, w=[lv], inc=(sem, 1))
            frees[slot] = (sem, v)
            last[eng] = (sem, v)
        return list(last.values())

    def stream_w_cast(src_ap3, dst3, gain=None, gate_pe=0, gate_w=(),
                      gain_cols=None):
        wl = []
        if gate_pe:
            wl.append(("PE", gate_pe))
        wl += [t for t in gate_w if t]
        tok = p.dma("gpsimd", f"wc{stream_w_cast.k % 4}", dst3, src_ap3, w=wl)
        stream_w_cast.k += 1
        if gain is None:
            return [tok]
        v = 0
        for kc, dsl in enumerate(gain_cols):
            v = p.op("vector", "tensor_scalar", dsl[0], dsl[0], gain(kc), None,
                     ALU.mult, w=[tok] if kc == 0 else (), inc=("DV", 1))
        return [("DV", v)]
    stream_w_cast.k = 0

    jfree_g = [0]

    def ln_stats(p_, tile, ssum, ssq, wx):
        p.op("vector", "tensor_reduce", ssum, tile, AX.X, ALU.add, w=wx,
             inc=("DV", 1))
        n = tile.shape[-1]
        vsq = p.op("scalar", "activation", junk[:, 0:n], tile, AF.Square,
                   w=list(wx) + [("DV", jfree_g[0])], inc=("AC", 1))
        jfree_g[0] = p.op("vector", "tensor_reduce", ssq, junk[:, 0:n], AX.X,
                          ALU.add, w=[("AC", vsq)], inc=("DV", 1))

    def ln_finish(stc, i0, n, extra=None):
        sl = (slice(None), slice(i0, i0 + n))
        wdv = [("DV", p.val("DV"))]
        wac = [("AC", p.val("AC"))]
        p.op("vector", "tensor_scalar", stc["mu"][sl], stc["su"][sl], 1.0 / D,
             None, ALU.mult, w=wdv, inc=("DV", 1))
        p.op("vector", "tensor_scalar", stc["t1"][sl], stc["sq"][sl], 1.0 / D,
             None, ALU.mult, w=wac, inc=("DV", 1))
        p.op("vector", "tensor_tensor", stc["t2"][sl], stc["mu"][sl], stc["mu"][sl],
             ALU.mult, inc=("DV", 1))
        vdv = p.op("vector", "tensor_tensor", stc["t1"][sl], stc["t1"][sl],
                   stc["t2"][sl], ALU.subtract, inc=("DV", 1))
        if "lnf_copy" in DEBUG:
            p.op("vector", "tensor_copy", stc["si"][sl], stc["t1"][sl],
                 inc=("DV", 1))
        elif "lnf_recip" in DEBUG:
            p.op("vector", "reciprocal", stc["si"][sl], stc["t1"][sl],
                 inc=("DV", 1))
        else:
            vac = p.op("scalar", "activation", stc["t2"][sl], stc["t1"][sl],
                       AF.Sqrt, bias=c_epsln, w=[("DV", vdv)], inc=("AC", 1))
            p.op("vector", "reciprocal", stc["si"][sl], stc["t2"][sl],
                 w=[("AC", vac)], inc=("DV", 1))
        p.op("vector", "tensor_tensor", stc["bu"][sl], stc["mu"][sl], stc["si"][sl],
             ALU.mult, inc=("DV", 1))
        vdv = p.op("vector", "tensor_scalar", stc["bu"][sl], stc["bu"][sl], -1.0,
                   None, ALU.mult, inc=("DV", 1))
        if extra != "rms":
            return vdv
        p.op("vector", "tensor_scalar", stc["t2"][sl], stc["t1"][sl], EPS_LN, None,
             ALU.add, inc=("DV", 1))
        p.op("vector", "reciprocal", stc["t2"][sl], stc["t2"][sl], inc=("DV", 1))
        vdv = p.op("vector", "tensor_tensor", stc["t2"][sl], stc["t1"][sl],
                   stc["t2"][sl], ALU.mult, inc=("DV", 1))
        vac = p.op("scalar", "activation", stc["rm"][sl], stc["t2"][sl], AF.Sqrt,
                   bias=c_epsrms, w=[("DV", vdv)], inc=("AC", 1))
        p.op("vector", "reciprocal", stc["t2"][sl], stc["rm"][sl], w=[("AC", vac)],
             inc=("DV", 1))
        p.op("vector", "tensor_tensor", stc["ar"][sl], stc["si"][sl], stc["t2"][sl],
             ALU.mult, inc=("DV", 1))
        p.op("vector", "tensor_tensor", stc["br"][sl], stc["mu"][sl], stc["ar"][sl],
             ALU.mult, inc=("DV", 1))
        p.op("vector", "tensor_scalar", stc["br"][sl], stc["br"][sl], -1.0, None,
             ALU.mult, inc=("DV", 1))
        vdv = p.op("vector", "tensor_scalar", stc["a1"][sl], stc["si"][sl], 1.0,
                   None, ALU.add, inc=("DV", 1))
        return vdv

    def mkstats(ctx, tagn, cols=NT, keys=("su", "sq", "mu", "si", "bu", "t1", "t2")):
        return {k: ctx.enter_context(nc.sbuf_tensor(tagn + k, [128, cols], F32))
                for k in keys}

    def emit_mlp_pass(tag, experts, rn_sp, dst):
        p.phase = tag + "_mlp"
        ctx = ExitStack()
        w1b = ctx.enter_context(nc.sbuf_tensor(tag + "w1b", [128, KC * MLP], BF16))
        w2b = ctx.enter_context(nc.sbuf_tensor(tag + "w2b", [128, HC * D], BF16))
        hT = ctx.enter_context(nc.sbuf_tensor(tag + "hT", [128, HC * BLK], BF16))
        rnb = ctx.enter_context(nc.sbuf_tensor(tag + "rnb", [128, KC * BLK], BF16))
        xrb = ctx.enter_context(nc.sbuf_tensor(tag + "xrb", [128, NSUB * D], F32))
        b1c = ctx.enter_context(nc.sbuf_tensor(tag + "b1c", [128, 2 * HC], F32))
        b2s = ctx.enter_context(nc.sbuf_tensor(tag + "b2s", [1, D], F32))
        b2r = ctx.enter_context(nc.sbuf_tensor(tag + "b2r", [1, 2 * D], BF16))
        ph = [ctx.enter_context(nc.psum_tensor(tag + f"ph{i}", [128, 512], F32))
              for i in range(2)]
        po = [ctx.enter_context(nc.psum_tensor(tag + f"po{i}", [128, 512], F32))
              for i in range(2)]

        ne = len(experts)
        w1done = [0] * ne
        w2done = [0] * ne
        b1v = [0] * ne
        b2v = [0] * ne
        pe_h_free = [0] * (ne + 1)
        pe_o_free = [0] * (ne + 1)
        ac_b1_free = [0] * (ne + 1)
        pe_b2_free = [0] * (ne + 1)

        def stream_exp_weights(e):
            ex = experts[e]
            ga = ex["gain"]
            w1done[e] = stream_w_cast(
                ex["W1"].rearrange("(c p) f -> p c f", p=128),
                w1b[:].rearrange("p (c f) -> p c f", f=MLP),
                gain=None if ga is None else
                (lambda kc, ga=ga: ga[:, kc:kc + 1]),
                gate_pe=pe_h_free[e],
                gain_cols=[(w1b[:, kc * MLP:(kc + 1) * MLP],)
                           for kc in range(KC)])
            w2done[e] = stream_w_cast(
                ex["W2"].rearrange("(h p) f -> p h f", p=128),
                w2b[:].rearrange("p (h f) -> p h f", f=D),
                gate_pe=pe_o_free[e])
            sl = e % 2
            b1v[e] = p.dma("sync", f"b1{sl}", b1c[:, sl * HC:(sl + 1) * HC],
                           ex["b1"], w=[("AC", ac_b1_free[e])])
            lv = p.dma("sync", "b2", b2s[:], ex["b2"],
                       w=[("PE", pe_b2_free[e]),
                          ("DV", b2v[e - 1]) if e else None])
            b2v[e] = p.op("vector", "tensor_copy", b2r[0:1, sl * D:(sl + 1) * D],
                          b2s[:], w=[lv], inc=("DV", 1))

        stream_exp_weights(0)
        rn_ring = Ring(1)
        xr_ring = Ring(1)
        gelu_free = [0, 0]
        oev_free = [0, 0]
        for e in range(ne):
            ex = experts[e]
            sl = e % 2
            last_h_mm = 0
            last_o_mm = 0
            last_gelu = 0
            for blk in range(NBLK):
                p.phase = f"{tag}e{e}b{blk}h"
                rv = p.dma("sync", "rnb",
                           rnb[:].rearrange("p (c t) -> p c t", t=BLK),
                           fmv(rn_sp, blk), w=rn_ring.waits(0))
                xv = p.dma("sync", "xrb",
                           xrb[:].rearrange("p (c d) -> p c d", d=D),
                           tmv(dst, blk), w=xr_ring.waits(0))
                for h in range(HC):
                    bank = ph[h % 2]
                    wl = [rv]
                    if h == 0 and blk == 0:
                        wl += list(w1done[e]) + [b1v[e]]
                    if gelu_free[h % 2]:
                        wl.append(("AC", gelu_free[h % 2]))
                    for kc in range(KC):
                        last_h_mm = p.op(
                            "tensor", "matmul", bank[:],
                            w1b[:, kc * MLP + h * 128: kc * MLP + (h + 1) * 128],
                            rnb[:, kc * BLK:(kc + 1) * BLK],
                            start=(kc == 0), stop=(kc == KC - 1),
                            w=wl if kc == 0 else (), inc=("PE", 1))
                    last_gelu = p.op(
                        "scalar", "activation", hT[:, h * BLK:(h + 1) * BLK],
                        bank[:], AF.Gelu, bias=b1c[:, sl * HC + h: sl * HC + h + 1],
                        w=[("PE", last_h_mm)], inc=("AC", 1))
                    gelu_free[h % 2] = last_gelu
                rn_ring.done(0, ("PE", last_h_mm))
                ac_h_done = last_gelu
                p.phase = f"{tag}e{e}b{blk}o"
                for sub in range(NSUB):
                    for fh in range(2):
                        bi = (sub * 2 + fh) % 2
                        bank = po[bi]
                        wl = [("AC", ac_h_done)]
                        if blk == 0 and sub == 0 and fh == 0:
                            wl += list(w2done[e]) + [("DV", b2v[e])]
                        if oev_free[bi]:
                            wl.append(("DV", oev_free[bi]))
                        p.op("tensor", "matmul", bank[:], ones16r[0:1, 0:128],
                             b2r[0:1, sl * D + fh * 512: sl * D + (fh + 1) * 512],
                             start=True, stop=False, w=wl, inc=("PE", 1))
                        for h in range(HC):
                            last_o_mm = p.op(
                                "tensor", "matmul", bank[:],
                                hT[:, h * BLK + sub * 128: h * BLK + (sub + 1) * 128],
                                w2b[:, h * D + fh * 512: h * D + (fh + 1) * 512],
                                start=False, stop=(h == HC - 1), inc=("PE", 1))
                        wc = ex["wcol"](blk, sub)
                        dsl = xrb[:, sub * D + fh * 512: sub * D + (fh + 1) * 512]
                        oev_free[bi] = p.op(
                            "vector", "scalar_tensor_tensor", dsl, bank[:],
                            1.0 if wc is None else wc, dsl, ALU.mult, ALU.add,
                            w=[("PE", last_o_mm), xv], inc=("DV", 1))
                sv = p.dma("sync", "xrbs", tmv(dst, blk),
                           xrb[:].rearrange("p (c d) -> p c d", d=D),
                           w=[("DV", p.val("DV"))])
                xr_ring.done(0, sv)
            pe_h_free[e + 1] = last_h_mm
            pe_o_free[e + 1] = last_o_mm
            ac_b1_free[e + 1] = last_gelu
            pe_b2_free[e + 1] = last_o_mm
            if e + 1 < ne:
                stream_exp_weights(e + 1)
        p.gate_all()
        ctx.close()

    def emit_ffn(name, src, dst):
        p.phase = name + "F1"
        octx = ExitStack()
        wsb = octx.enter_context(nc.sbuf_tensor(name + "wsb", [128, NT * 8], F32))
        gsc = octx.enter_context(nc.sbuf_tensor(name + "gsc", [128, 2 * KC], F32))
        ctx = ExitStack()
        xt = ctx.enter_context(nc.sbuf_tensor(name + "xt", [128, 2 * NSUB * D], F32))
        xo = ctx.enter_context(nc.sbuf_tensor(name + "xo", [128, 2 * NSUB * D], F32))
        rtm = ctx.enter_context(nc.sbuf_tensor(name + "rtm", [128, 2 * D], BF16))
        rfm = ctx.enter_context(nc.sbuf_tensor(name + "rfm", [128, 2 * KC * BLK], BF16))
        cen = ctx.enter_context(nc.sbuf_tensor(name + "cen", [128, KC * 8], BF16))
        cstg = ctx.enter_context(nc.sbuf_tensor(name + "cstg", [128, KC * 8], F32))
        scr = ctx.enter_context(nc.sbuf_tensor(name + "scr", [128, NT * 8], F32))
        stc = mkstats(ctx, name + "s",
                      keys=("su", "sq", "mu", "si", "bu", "t1", "t2", "ar", "br",
                            "a1", "rm"))
        ptr = [ctx.enter_context(nc.psum_tensor(name + f"pt{i}", [128, 1024], BF16))
               for i in range(2)]
        psc = [ctx.enter_context(nc.psum_tensor(name + f"ps{i}", [128, 512], F32))
               for i in range(2)]

        gva = p.dma("sync", "gs0", gsc[:, 0:KC], f_gs)
        gvb = p.dma("sync", "gs1", gsc[:, KC:2 * KC], f_gr)
        cv = p.dma("sync", "cen", cstg[:].rearrange("p (c e) -> p c e", e=8),
                   f_cen.rearrange("(c p) e -> p c e", p=128))
        cenv = p.op("vector", "tensor_copy", cen[:], cstg[:],
                    w=[cv, gva, gvb], inc=("DV", 1))

        xt_ring = Ring(2)
        xo_ring = Ring(2)
        rtm_ring = Ring(2)
        rfm_ring = Ring(2)
        psc_ring = Ring(2)
        for blk in range(NBLK):
            bb = blk % 2
            xv = p.dma("sync", f"xt{bb}",
                       xt[:, bb * NSUB * D:(bb + 1) * NSUB * D].rearrange(
                           "p (c d) -> p c d", d=D),
                       tmv(src, blk), w=xt_ring.waits(blk))
            for s2 in range(NSUB):
                i = blk * NSUB + s2
                tile = xt[:, bb * NSUB * D + s2 * D: bb * NSUB * D + (s2 + 1) * D]
                ln_stats(p, tile, stc["su"][:, i:i + 1], stc["sq"][:, i:i + 1],
                         [xv])
            vstat = ln_finish(stc, blk * NSUB, NSUB, extra="rms")
            last_ac = 0
            last_sc_mm = 0
            for s2 in range(NSUB):
                i = blk * NSUB + s2
                sb2 = (blk * NSUB + s2) % 2
                tile = xt[:, bb * NSUB * D + s2 * D: bb * NSUB * D + (s2 + 1) * D]
                if "f1_min" in DEBUG:
                    last_ac = p.op("scalar", "activation",
                                   xo[:, bb * NSUB * D + s2 * D: bb * NSUB * D + (s2 + 1) * D],
                                   tile, AF.Identity, scale=stc["a1"][:, i:i + 1],
                                   bias=stc["bu"][:, i:i + 1],
                                   w=[("DV", vstat)] + (xo_ring.waits(blk) if s2 == 0 else []),
                                   inc=("AC", 1))
                    continue
                vrn = p.op("scalar", "activation", rtm[:, sb2 * D:(sb2 + 1) * D],
                           tile, AF.Identity, scale=stc["ar"][:, i:i + 1],
                           bias=stc["br"][:, i:i + 1],
                           w=[("DV", vstat)] + rtm_ring.waits(i), inc=("AC", 1))
                vpe = 0
                for j in range(KC):
                    vpe = p.op("tensor", "transpose",
                               ptr[sb2][:, j * 128:(j + 1) * 128],
                               rtm[:, sb2 * D + j * 128: sb2 * D + (j + 1) * 128],
                               ident16[:], w=[("AC", vrn)] if j == 0 else (),
                               inc=("PE", 1))
                rtm_ring.done(i, ("PE", vpe))
                vdve = p.op("vector", "tensor_copy",
                            rfm[:, bb * KC * BLK:(bb + 1) * KC * BLK].rearrange(
                                "p (c t) -> p c t", t=BLK)[:, :, s2 * 128:(s2 + 1) * 128],
                            ptr[sb2][:].rearrange("p (c t) -> p c t", t=128),
                            w=[("PE", vpe)] + (rfm_ring.waits(blk) if s2 == 0 else []),
                            inc=("DV", 1))
                for kc in range(KC):
                    wl = [("DV", vdve), ("DV", cenv)]
                    if s2 == 0 and kc == 0:
                        wl += psc_ring.waits(blk)
                    last_sc_mm = p.op(
                        "tensor", "matmul", psc[bb][:, s2 * 8:(s2 + 1) * 8],
                        rfm[:, bb * KC * BLK + kc * BLK + s2 * 128:
                            bb * KC * BLK + kc * BLK + (s2 + 1) * 128],
                        cen[:, kc * 8:(kc + 1) * 8],
                        start=(kc == 0), stop=(kc == KC - 1),
                        skip_group_check=True,
                        w=wl if kc == 0 else (), inc=("PE", 1))
                last_ac = p.op("scalar", "activation",
                               xo[:, bb * NSUB * D + s2 * D: bb * NSUB * D + (s2 + 1) * D],
                               tile, AF.Identity, scale=stc["a1"][:, i:i + 1],
                               bias=stc["bu"][:, i:i + 1],
                               w=xo_ring.waits(blk) if s2 == 0 else (),
                               inc=("AC", 1))
            xt_ring.done(blk, ("AC", last_ac))
            if "f1_min" not in DEBUG:
                vsc = p.op("vector", "tensor_copy", scr[:, blk * 32:(blk + 1) * 32],
                           psc[bb][:, 0:32], w=[("PE", last_sc_mm)], inc=("DV", 1))
                psc_ring.done(blk, ("DV", vsc))
                vrf = p.dma("sync", f"rnsp{bb}", fmv(SP_RN, blk),
                            rfm[:, bb * KC * BLK:(bb + 1) * KC * BLK].rearrange(
                                "p (c t) -> p c t", t=BLK),
                            w=[("DV", p.val("DV"))])
                rfm_ring.done(blk, vrf, ("PE", last_sc_mm))
            vxo = p.dma("sync", f"xo{bb}", tmv(dst, blk),
                        xo[:, bb * NSUB * D:(bb + 1) * NSUB * D].rearrange(
                            "p (c d) -> p c d", d=D),
                        w=[("AC", last_ac)])
            xo_ring.done(blk, vxo)
        if "dbgstats" in DEBUG:
            dbg_st = nc.dram_tensor(name + "dbg_st", [6, 128, NT], F32,
                                    kind="ExternalOutput").ap()
            for k2, nm in enumerate(("su", "sq", "mu", "si", "a1", "bu")):
                p.dma("sync", f"dst{k2 % 4}", dbg_st[k2], stc[nm][:],
                      w=[("DV", p.val("DV")), ("AC", p.val("AC"))])
        for i in range(NT if "f1_min" not in DEBUG else 0):
            ssl = scr[:, i * 8:(i + 1) * 8]
            wsl = wsb[:, i * 8:(i + 1) * 8]
            vac = p.op("scalar", "activation", ssl, ssl, AF.Sigmoid,
                       scale=stc["rm"][:, i:i + 1],
                       w=[("DV", p.val("DV"))] if i == 0 else (), inc=("AC", 1))
            p.op("vector", "reduce_max", stc["t1"][:, i:i + 1], ssl, AX.X,
                 w=[("AC", vac)], inc=("DV", 1))
            p.op("vector", "tensor_scalar", wsl, ssl, stc["t1"][:, i:i + 1], None,
                 ALU.is_equal, inc=("DV", 1))
            p.op("vector", "scalar_tensor_tensor", wsl, wsl, -1e30, ssl,
                 ALU.mult, ALU.add, inc=("DV", 1))
            p.op("vector", "reduce_max", stc["t2"][:, i:i + 1], wsl, AX.X,
                 inc=("DV", 1))
            p.op("vector", "reduce_sum", stc["t1"][:, i:i + 1], ssl, AX.X,
                 inc=("DV", 1))
            p.op("vector", "reciprocal", stc["t1"][:, i:i + 1],
                 stc["t1"][:, i:i + 1], inc=("DV", 1))
            p.op("vector", "tensor_scalar", wsl, ssl, stc["t2"][:, i:i + 1], None,
                 ALU.is_ge, inc=("DV", 1))
            p.op("vector", "tensor_tensor", wsl, wsl, ssl, ALU.mult, inc=("DV", 1))
            p.op("vector", "tensor_scalar", wsl, wsl, stc["t1"][:, i:i + 1], None,
                 ALU.mult, inc=("DV", 1))
        if "dbg" in DEBUG:
            dbg_rn = nc.dram_tensor(name + "dbg_rn", [D, T], BF16,
                                    kind="ExternalOutput").ap()
            dbg_w = nc.dram_tensor(name + "dbg_w", [128, NT * 8], F32,
                                   kind="ExternalOutput").ap()
            dbg_sc = nc.dram_tensor(name + "dbg_sc", [128, NT * 8], F32,
                                    kind="ExternalOutput").ap()
            p.dma("sync", "dbg0", dbg_rn, SP_RN,
                  w=[("DV", p.val("DV"))] + [(f"Q{i}", p.val(f"Q{i}"))
                                             for i in range(p.NPOOL)])
            p.dma("sync", "dbg1", dbg_w, wsb[:], w=[("DV", p.val("DV"))])
            p.dma("sync", "dbg2", dbg_sc, scr[:], w=[("DV", p.val("DV"))])
        p.gate_all()

        experts = []
        for e in range(2):
            experts.append(dict(W1=fW1s[e], W2=fW2s[e], b1=fb1s[e], b2=fb2s[e][0:1, :],
                                gain=gsc[:, 0:KC], wcol=lambda blk, sub: None))
        for e in range(8):
            def mkw(e=e):
                return lambda blk, sub: wsb[:, (blk * NSUB + sub) * 8 + e:
                                            (blk * NSUB + sub) * 8 + e + 1]
            experts.append(dict(W1=fW1r[e], W2=fW2r[e], b1=fb1r[e],
                                b2=fb2r[e][0:1, :], gain=gsc[:, KC:2 * KC],
                                wcol=mkw()))
        ctx.close()
        if "no_experts" not in DEBUG:
            emit_mlp_pass(name + "X", experts, SP_RN, dst)
        octx.close()

    def emit_attn_a(src, dst):
        octx = ExitStack()
        au = octx.enter_context(nc.sbuf_tensor("Aau", [128, NT], F32))
        buc = octx.enter_context(nc.sbuf_tensor("Abu", [128, NT], F32))
        qg = octx.enter_context(nc.sbuf_tensor("Aqg", [128, KC * 2], F32))
        atm = octx.enter_context(nc.sbuf_tensor("Aatm", [128, NT * H], F32))
        a0r = octx.enter_context(nc.sbuf_tensor("Aa0r", [1, 2 * H], F32))
        kvsb = octx.enter_context(nc.sbuf_tensor("Akv", [128, 2 * KC * 128], BF16))
        kse = octx.enter_context(nc.sbuf_tensor("Akse", [128, 2 * KC * H], BF16))

        p.phase = "A1"
        ctx = ExitStack()
        wq16 = ctx.enter_context(nc.sbuf_tensor("Awq", [128, KC * D], BF16))
        wk16 = ctx.enter_context(nc.sbuf_tensor("Awk", [128, KC * D], BF16))
        wv16 = ctx.enter_context(nc.sbuf_tensor("Awv", [128, KC * D], BF16))
        wpq16 = ctx.enter_context(nc.sbuf_tensor("Awpq", [128, KC * D], BF16))
        brow = ctx.enter_context(nc.sbuf_tensor("Abrow", [1, 5 * D], BF16))
        brst = ctx.enter_context(nc.sbuf_tensor("Abrst", [1, 5 * D], F32))
        lnqc = ctx.enter_context(nc.sbuf_tensor("Alnq", [128, 2 * KC], F32))
        lnkc = ctx.enter_context(nc.sbuf_tensor("Alnk", [128, 2 * KC], F32))
        lnvb = ctx.enter_context(nc.sbuf_tensor("Alnv", [128, 2 * D], F32))
        stc = mkstats(ctx, "As")
        st4 = mkstats(ctx, "A4", cols=4)
        xt = ctx.enter_context(nc.sbuf_tensor("A1xt", [128, NSUB * D], F32))
        utm = ctx.enter_context(nc.sbuf_tensor("A1ut", [128, 2 * D], BF16))
        ufm = ctx.enter_context(nc.sbuf_tensor("A1uf", [128, KC * BLK], BF16))
        qtm = ctx.enter_context(nc.sbuf_tensor("A1qt", [128, 2 * D], BF16))
        qfm = ctx.enter_context(nc.sbuf_tensor("A1qf", [128, KC * BLK], BF16))
        sfm = ctx.enter_context(nc.sbuf_tensor("A1sf", [128, KC * BLK], BF16))
        vtm = ctx.enter_context(nc.sbuf_tensor("A1vt", [128, NSUB * D], BF16))
        vrw = ctx.enter_context(nc.sbuf_tensor("A1vr", [128, D], F32))
        phf = ctx.enter_context(nc.sbuf_tensor("A1pf", [128, 512], F32))
        pt = [ctx.enter_context(nc.psum_tensor(f"A1pt{i}", [128, 1024], BF16))
              for i in range(2)]
        pj = [ctx.enter_context(nc.psum_tensor(f"A1pj{i}", [128, 512], F32))
              for i in range(2)]

        wc = lambda W, dst: stream_w_cast(
            W.rearrange("(c p) f -> p c f", p=128),
            dst[:].rearrange("p (c f) -> p c f", f=D))
        wq_done = wc(aWq, wq16)
        wk_done = wc(aWk, wk16)
        wv_done = wc(aWv, wv16)
        wpq_done = wc(aWpq, wpq16)
        lv = p.dma("sync", "brows", brst[:], a_brows)
        brow_v = p.op("vector", "tensor_copy", brow[:], brst[:], w=[lv],
                      inc=("DV", 1))
        lnq_lv = p.dma("sync", "lnq", lnqc[:], a_lnq)
        lnk_lv = p.dma("sync", "lnk", lnkc[:], a_lnk)
        ln_lv = p.dma("sync", "lnv", lnvb[:], a_lnv_bc)
        vz = p.op("vector", "memset", qg[:], 0.0, inc=("DV", 1))

        xt_ring = Ring(1)
        utm_ring = Ring(2)
        qtm_ring = Ring(2)
        pj_ring = Ring(2)
        pt_ring = Ring(2)
        ufm_ring = Ring(1)
        qfm_ring = Ring(1)
        sfm_ring = Ring(1)
        vtm_ring = Ring(1)
        phf_ring = Ring(1)

        def tm_projection_sub(blk, s2, w16, brow_off, wdone):
            mmv = [0, 0]
            for fh in range(2):
                bank = pj[fh]
                wl = list(wdone) + [("DV", brow_v)] + pj_ring.waits(fh)
                p.op("tensor", "matmul", bank[:], ones16r[0:1, 0:128],
                     brow[0:1, brow_off + fh * 512: brow_off + (fh + 1) * 512],
                     start=True, stop=False, w=wl, inc=("PE", 1))
                for kc in range(KC):
                    mmv[fh] = p.op(
                        "tensor", "matmul", bank[:],
                        ufm[:, kc * BLK + s2 * 128: kc * BLK + (s2 + 1) * 128],
                        w16[:, kc * D + fh * 512: kc * D + (fh + 1) * 512],
                        start=False, stop=(kc == KC - 1), inc=("PE", 1))
                vsq = p.op("scalar", "activation",
                           junk[:, fh * 512:(fh + 1) * 512], bank[:], AF.Square,
                           w=[("PE", mmv[fh]), ("DV", jfree_g[0])], inc=("AC", 1))
                p.op("vector", "tensor_reduce", st4["su"][:, fh:fh + 1],
                     bank[:], AX.X, ALU.add, w=[("AC", vsq)], inc=("DV", 1))
                jfree_g[0] = p.op("vector", "tensor_reduce",
                                  st4["sq"][:, fh:fh + 1],
                                  junk[:, fh * 512:(fh + 1) * 512], AX.X, ALU.add,
                                  inc=("DV", 1))
            p.op("vector", "tensor_tensor", st4["su"][:, 0:1], st4["su"][:, 0:1],
                 st4["su"][:, 1:2], ALU.add, inc=("DV", 1))
            p.op("vector", "tensor_tensor", st4["sq"][:, 0:1], st4["sq"][:, 0:1],
                 st4["sq"][:, 1:2], ALU.add, inc=("DV", 1))
            vstat = ln_finish(st4, 0, 1)
            acv = 0
            for fh in range(2):
                acv = p.op("scalar", "activation",
                           qtm[:, (s2 % 2) * D + fh * 512:(s2 % 2) * D + (fh + 1) * 512],
                           pj[fh][:], AF.Identity, scale=st4["si"][:, 0:1],
                           bias=st4["bu"][:, 0:1],
                           w=[("DV", vstat)] + (qtm_ring.waits(s2) if fh == 0 else []),
                           inc=("AC", 1))
                pj_ring.done(fh, ("AC", acv))
            return acv

        def fm_from_tm(blk, src_tm, s2, dstfm, lncol, wsrc, dst_waits=(),
                       lncol_w=()):
            vpe = 0
            for j in range(KC):
                vpe = p.op("tensor", "transpose", pt[s2 % 2][:, j * 128:(j + 1) * 128],
                           src_tm[:, (s2 % 2) * D + j * 128:(s2 % 2) * D + (j + 1) * 128],
                           ident16[:],
                           w=([("AC", wsrc)] + pt_ring.waits(s2)) if j == 0 else (),
                           inc=("PE", 1))
            vac = 0
            for j in range(KC):
                vac = p.op("scalar", "activation",
                           dstfm[:, j * BLK + s2 * 128: j * BLK + (s2 + 1) * 128],
                           pt[s2 % 2][:, j * 128:(j + 1) * 128], AF.Identity,
                           scale=lncol[:, 2 * j:2 * j + 1],
                           bias=lncol[:, 2 * j + 1:2 * j + 2],
                           w=([("PE", vpe)] + list(dst_waits) + list(lncol_w))
                           if j == 0 else (),
                           inc=("AC", 1))
            pt_ring.done(s2, ("AC", vac))
            return vpe, vac

        for blk in range(NBLK):
            seq = blk // NSUB
            xv = p.dma("sync", "a1xt", xt[:].rearrange("p (c d) -> p c d", d=D),
                       tmv(src, blk), w=xt_ring.waits(0))
            for s2 in range(NSUB):
                i = blk * NSUB + s2
                tile = xt[:, s2 * D:(s2 + 1) * D]
                ln_stats(p, tile, stc["su"][:, i:i + 1], stc["sq"][:, i:i + 1],
                         [xv])
            vstat = ln_finish(stc, blk * NSUB, NSUB)
            sl = (slice(None), slice(blk * NSUB, blk * NSUB + NSUB))
            p.op("vector", "tensor_copy", au[sl], stc["si"][sl], w=[("DV", vstat)],
                 inc=("DV", 1))
            p.op("vector", "tensor_copy", buc[sl], stc["bu"][sl], inc=("DV", 1))
            last_xt_use = 0
            uev = 0
            for s2 in range(NSUB):
                i = blk * NSUB + s2
                vu = p.op("scalar", "activation", utm[:, (s2 % 2) * D:(s2 % 2 + 1) * D],
                          xt[:, s2 * D:(s2 + 1) * D], AF.Identity,
                          scale=stc["si"][:, i:i + 1], bias=stc["bu"][:, i:i + 1],
                          w=[("DV", vstat)] + utm_ring.waits(s2), inc=("AC", 1))
                last_xt_use = vu
                vpe = 0
                for j in range(KC):
                    vpe = p.op("tensor", "transpose", pt[s2 % 2][:, j * 128:(j + 1) * 128],
                               utm[:, (s2 % 2) * D + j * 128:(s2 % 2) * D + (j + 1) * 128],
                               ident16[:],
                               w=([("AC", vu)] + pt_ring.waits(s2)) if j == 0 else (),
                               inc=("PE", 1))
                utm_ring.done(s2, ("PE", vpe))
                uev = p.op("vector", "tensor_copy",
                           ufm[:].rearrange("p (c t) -> p c t", t=BLK)[:, :, s2 * 128:(s2 + 1) * 128],
                           pt[s2 % 2][:].rearrange("p (c t) -> p c t", t=128),
                           w=[("PE", vpe)] + (ufm_ring.waits(0) if s2 == 0 else []),
                           inc=("DV", 1))
                pt_ring.done(s2, ("DV", uev))
            xt_ring.done(0, ("AC", last_xt_use))
            p.wait_only("tensor", [("DV", uev)])

            pe_q = 0
            ac_q = 0
            for s2 in range(NSUB):
                if ASUB < 2:
                    break
                acv = tm_projection_sub(blk, s2, wq16, 0 * D, wq_done)
                if ASUB < 3:
                    qtm_ring.done(s2, ("AC", acv))
                    continue
                pe_q, ac_q = fm_from_tm(blk, qtm, s2, qfm, lnqc, acv,
                                        dst_waits=qfm_ring.waits(0) if s2 == 0 else (),
                                        lncol_w=[lnq_lv])
                qtm_ring.done(s2, ("PE", pe_q))
            qfm_ready = ac_q
            vlast_t = 0
            for j in range(KC if ASUB >= 4 else 0):
                bank = pj[j % 2]
                wl = [("AC", qfm_ready), ("DV", brow_v)] + list(wpq_done)
                wl += pj_ring.waits(j % 2)
                p.op("tensor", "matmul", bank[:],
                     brow[0:1, 3 * D + j * 128: 3 * D + (j + 1) * 128],
                     ones16r[0:1, 0:512], start=True, stop=False, w=wl,
                     inc=("PE", 1))
                for kc in range(KC):
                    vpe = p.op("tensor", "matmul", bank[:],
                               wpq16[:, kc * D + j * 128: kc * D + (j + 1) * 128],
                               qfm[:, kc * BLK:(kc + 1) * BLK],
                               start=False, stop=(kc == KC - 1), inc=("PE", 1))
                vt = p.op("scalar", "activation", phf[:], bank[:], AF.Tanh,
                          w=[("PE", vpe)] + phf_ring.waits(0), inc=("AC", 1))
                vlast_t = p.op("vector", "tensor_scalar",
                               sfm[:, j * BLK:(j + 1) * BLK], phf[:], 1.0, None,
                               ALU.add, w=[("AC", vt)] + (sfm_ring.waits(0) if j == 0 else []),
                               inc=("DV", 1))
                phf_ring.done(0, ("DV", vlast_t))
                pj_ring.done(j % 2, ("AC", vt))
                p.op("vector", "tensor_reduce", st4["t1"][:, 2:3],
                     qfm[:, j * BLK:(j + 1) * BLK], AX.X, ALU.add, inc=("DV", 1))
                p.op("vector", "tensor_tensor", qg[:, j * 2 + seq:j * 2 + seq + 1],
                     qg[:, j * 2 + seq:j * 2 + seq + 1], st4["t1"][:, 2:3],
                     ALU.add, w=[("DV", vz)], inc=("DV", 1))
            if ASUB >= 4:
                qfm_ring.done(0, ("DV", p.val("DV")), ("PE", p.val("PE")))
                vsp = p.dma("sync", "tqsp", fmv(SP_TQ, blk),
                            sfm[:].rearrange("p (c t) -> p c t", t=BLK),
                            w=[("DV", vlast_t)])
                sfm_ring.done(0, vsp)

            ac_k = 0
            for s2 in range(NSUB if ASUB >= 5 else 0):
                acv = tm_projection_sub(blk, s2, wk16, 1 * D, wk_done)
                pe_k, ac_k = fm_from_tm(blk, qtm, s2, qfm, lnkc, acv,
                                        dst_waits=qfm_ring.waits(0) if s2 == 0 else (),
                                        lncol_w=[lnk_lv])
                qtm_ring.done(s2, ("PE", pe_k))
            if ASUB >= 5:
                vsp = p.dma("sync", "ksp", fmv(SP_K, blk),
                            qfm[:].rearrange("p (c t) -> p c t", t=BLK),
                            w=[("AC", ac_k)])
                qfm_ring.done(0, vsp)

            vdv = 0
            for s2 in range(NSUB if ASUB >= 6 else 0):
                acv = tm_projection_sub(blk, s2, wv16, 2 * D, wv_done)
                p.op("vector", "tensor_tensor", vrw[:],
                     qtm[:, (s2 % 2) * D:(s2 % 2 + 1) * D], lnvb[:, 0:D], ALU.mult,
                     w=[("AC", acv), ln_lv], inc=("DV", 1))
                vdv = p.op("vector", "tensor_tensor", vtm[:, s2 * D:(s2 + 1) * D],
                           vrw[:], lnvb[:, D:2 * D], ALU.add,
                           w=vtm_ring.waits(0) if s2 == 0 else (), inc=("DV", 1))
                qtm_ring.done(s2, ("DV", vdv))
            if ASUB >= 6:
                vsp = p.dma("sync", "vsp", tmv(SP_V, blk),
                            vtm[:].rearrange("p (c d) -> p c d", d=D),
                            w=[("DV", vdv)])
                vtm_ring.done(0, vsp)
            ufm_ring.done(0, ("PE", p.val("PE")))
        p.gate_all()
        ctx.close()
        if ALV <= 1:
            octx.close()
            return


        p.phase = "A2a"
        ctx = ExitStack()
        kfm = ctx.enter_context(nc.sbuf_tensor("A2k", [128, 2 * KC * BLK], BF16))
        qge = ctx.enter_context(nc.sbuf_tensor("A2qg", [128, 2 * KC * H], BF16))
        scs = ctx.enter_context(nc.sbuf_tensor("A2sc", [16, L], F32))
        alr = ctx.enter_context(nc.sbuf_tensor("A2al", [16, L], F32))
        alrp = ctx.enter_context(nc.sbuf_tensor("A2ap", [128, L], BF16))
        smc = ctx.enter_context(nc.sbuf_tensor("A2sm", [16, 8], F32))
        psA = ctx.enter_context(nc.psum_tensor("A2pA", [128, 2048], F32))
        ptA = [ctx.enter_context(nc.psum_tensor(f"A2pT{i}", [128, 1024], BF16))
               for i in range(2)]

        vdv = p.op("vector", "memset", qge[:], 0.0, inc=("DV", 1))
        p.op("vector", "memset", alrp[:], 0.0, inc=("DV", 1))
        qsc = 1.0 / (L * np.sqrt(DH))
        for s in range(2):
            for c in range(KC):
                p.op("vector", "tensor_scalar",
                     qge[0:64, (s * KC + c) * H + 2 * c: (s * KC + c) * H + 2 * c + 1],
                     qg[0:64, c * 2 + s: c * 2 + s + 1], qsc, None, ALU.mult,
                     inc=("DV", 1))
                p.op("vector", "tensor_scalar",
                     qge[64:128, (s * KC + c) * H + 2 * c + 1: (s * KC + c) * H + 2 * c + 2],
                     qg[64:128, c * 2 + s: c * 2 + s + 1], qsc, None, ALU.mult,
                     inc=("DV", 1))
        vqg = p.close("vector", "DV")
        kfm_ring = Ring(2)
        psA_free = 0
        for s in range(2):
            vpe = 0
            for bi in range(NSUB):
                blk = s * NSUB + bi
                lv = p.dma("sync", "kfm%d" % (blk % 2),
                           kfm[:, (blk % 2) * KC * BLK:(blk % 2 + 1) * KC * BLK].rearrange(
                               "p (c t) -> p c t", t=BLK),
                           fmv(SP_K, blk), w=kfm_ring.waits(blk))
                for kc in range(KC):
                    wl = [lv, ("DV", vqg)]
                    if bi == 0 and kc == 0 and psA_free:
                        wl.append(("DV", psA_free))
                    vpe = p.op("tensor", "matmul", psA[0:16, bi * 512:(bi + 1) * 512],
                               qge[:, (s * KC + kc) * H:(s * KC + kc + 1) * H],
                               kfm[:, (blk % 2) * KC * BLK + kc * BLK:
                                   (blk % 2) * KC * BLK + (kc + 1) * BLK],
                               start=(kc == 0), stop=(kc == KC - 1),
                               skip_group_check=True,
                               w=wl, inc=("PE", 1))
                kfm_ring.done(blk, ("PE", vpe))
            vev = p.op("vector", "tensor_copy", scs[:], psA[0:16, :],
                       w=[("PE", vpe), ("AC", p.val("AC"))], inc=("DV", 1))
            psA_free = vev
            p.op("vector", "reduce_max", smc[:, 0:1], scs[:], AX.X, inc=("DV", 1))
            p.op("vector", "tensor_scalar", smc[:, 0:1], smc[:, 0:1], 0.0, None,
                 ALU.max, inc=("DV", 1))
            vdv = p.op("vector", "tensor_scalar", smc[:, 1:2], smc[:, 0:1], -1.0,
                       None, ALU.mult, inc=("DV", 1))
            p.op("scalar", "activation", scs[:], scs[:], AF.Exp, bias=smc[:, 1:2],
                 w=[("DV", vdv)], inc=("AC", 1))
            vex = p.op("scalar", "activation", smc[:, 2:3], smc[:, 1:2], AF.Exp,
                       inc=("AC", 1))
            p.op("vector", "reduce_sum", smc[:, 3:4], scs[:], AX.X, w=[("AC", vex)],
                 inc=("DV", 1))
            p.op("vector", "tensor_tensor", smc[:, 3:4], smc[:, 3:4], smc[:, 2:3],
                 ALU.add, inc=("DV", 1))
            p.op("vector", "reciprocal", smc[:, 4:5], smc[:, 3:4], inc=("DV", 1))
            p.op("vector", "tensor_scalar", alr[:], scs[:], smc[:, 4:5], float(L),
                 ALU.mult, ALU.mult, w=[("PE", p.val("PE"))], inc=("DV", 1))
            vdv = p.op("vector", "tensor_scalar", smc[:, 5:6], smc[:, 2:3],
                       smc[:, 4:5], float(L), ALU.mult, ALU.mult, inc=("DV", 1))
            sv = p.dma("sync", "bncw", BNC[0:1, 0:16].rearrange("o n -> n o"),
                       smc[:, 5:6],
                       w=[("DV", vdv)] + ([a0v] if s > 0 else []))
            a0v = p.dma("sync", "bncr", a0r[0:1, s * H:(s + 1) * H],
                        BNC[0:1, 0:16], w=[sv])
            vdv = p.op("vector", "tensor_copy", alrp[0:16, :], alr[:],
                       inc=("DV", 1))
            cp_dv = [0, 0]
            for ti in range(L // 128):
                r = ti % 2
                wl = [("DV", vdv)]
                if cp_dv[r]:
                    wl.append(("DV", cp_dv[r]))
                vpe = p.op("tensor", "transpose", ptA[r][:, 0:128],
                           alrp[:, ti * 128:(ti + 1) * 128], ident16[:],
                           w=wl, inc=("PE", 1))
                cp_dv[r] = p.op(
                    "vector", "tensor_copy",
                    atm[:, (s * (L // 128) + ti) * H:(s * (L // 128) + ti + 1) * H],
                    ptA[r][:, 0:16], w=[("PE", vpe)], inc=("DV", 1))
        p.gate_all()
        ctx.close()
        if ALV <= 2:
            octx.close()
            return

        p.phase = "A2b"
        ctx = ExitStack()
        kfm2 = ctx.enter_context(nc.sbuf_tensor("A2bk", [128, 2 * KC * BLK], BF16))
        vtm2 = ctx.enter_context(nc.sbuf_tensor("A2bv", [128, 2 * NSUB * D], BF16))
        wpk16 = ctx.enter_context(nc.sbuf_tensor("A2bw", [128, KC * D], BF16))
        brow2 = ctx.enter_context(nc.sbuf_tensor("A2bbr", [1, D], BF16))
        brst2 = ctx.enter_context(nc.sbuf_tensor("A2bbs", [1, D], F32))
        tktm = ctx.enter_context(nc.sbuf_tensor("A2btk", [128, 2 * D], BF16))
        pks = ctx.enter_context(nc.sbuf_tensor("A2bp", [128, 2 * D], BF16))
        ksr = ctx.enter_context(nc.sbuf_tensor("A2bkr", [1, 2 * D], F32))
        ksr16 = ctx.enter_context(nc.sbuf_tensor("A2bk6", [1, 128], BF16))
        tk0 = ctx.enter_context(nc.sbuf_tensor("A2bt0", [1, D], F32))
        kvacc = ctx.enter_context(nc.sbuf_tensor("A2bkva", [128, KC * 128], F32))
        ppk = [ctx.enter_context(nc.psum_tensor(f"A2bp{i}", [128, 512], F32))
               for i in range(2)]
        pkv2 = [ctx.enter_context(nc.psum_tensor("A2bkv0", [128, 1024], F32))]
        pks_ = ctx.enter_context(nc.psum_tensor("A2bks", [128, 1024], F32))
        ptB = ctx.enter_context(nc.psum_tensor("A2bt", [128, 128], F32))

        wpk_done = stream_w_cast(
            aWpk.rearrange("(c p) f -> p c f", p=128),
            wpk16[:].rearrange("p (c f) -> p c f", f=D))
        lv = p.dma("sync", "bpk", brst2[:], a_brows[0:1, 4 * D:5 * D])
        vt = p.op("scalar", "activation", tk0[:], brst2[:], AF.Tanh, w=[lv],
                  inc=("AC", 1))
        tk0v = p.op("vector", "tensor_scalar", tk0[:], tk0[:], 1.0, None, ALU.add,
                    w=[("AC", vt)], inc=("DV", 1))
        br2v = p.op("vector", "tensor_copy", brow2[:], brst2[:], w=[lv],
                    inc=("DV", 1))
        ksez = p.op("vector", "memset", kse[:], 0.0, inc=("DV", 1))

        kfm_ring = Ring(2)
        vtm_ring = Ring(2)
        tk_ring = Ring(2)
        pks_ring = Ring(2)
        ppk_ring = Ring(2)
        pkv_ring = Ring(1)
        pks_free = [0]
        for s in range(2):
            last_kv = 0
            last_ks = 0
            vz_kv = p.op("vector", "memset", kvacc[:], 0.0, inc=("DV", 1))
            for bi in range(NSUB):
                blk = s * NSUB + bi
                lvk = p.dma("sync", "k2%d" % (blk % 2),
                            kfm2[:, (blk % 2) * KC * BLK:(blk % 2 + 1) * KC * BLK].rearrange(
                                "p (c t) -> p c t", t=BLK),
                            fmv(SP_K, blk), w=kfm_ring.waits(blk))
                lvv = p.dma("sync", "v2%d" % (blk % 2),
                            vtm2[:, (blk % 2) * NSUB * D:(blk % 2 + 1) * NSUB * D].rearrange(
                                "p (c d) -> p c d", d=D),
                            tmv(SP_V, blk), w=vtm_ring.waits(blk))
                for s2 in range(NSUB):
                    ti = blk * NSUB + s2
                    vth = 0
                    for fh in range(2):
                        bank = ppk[fh]
                        wl = [lvk, ("DV", br2v)] + list(wpk_done)
                        wl += ppk_ring.waits(fh)
                        p.op("tensor", "matmul", bank[:], ones16r[0:1, 0:128],
                             brow2[0:1, fh * 512:(fh + 1) * 512],
                             start=True, stop=False, w=wl, inc=("PE", 1))
                        for kc in range(KC):
                            vpe = p.op("tensor", "matmul", bank[:],
                                       kfm2[:, (blk % 2) * KC * BLK + kc * BLK + s2 * 128:
                                            (blk % 2) * KC * BLK + kc * BLK + (s2 + 1) * 128],
                                       wpk16[:, kc * D + fh * 512: kc * D + (fh + 1) * 512],
                                       start=False, stop=(kc == KC - 1),
                                       inc=("PE", 1))
                        vth = p.op("scalar", "activation",
                                   tktm[:, (s2 % 2) * D + fh * 512:(s2 % 2) * D + (fh + 1) * 512],
                                   bank[:], AF.Tanh,
                                   w=[("PE", vpe)] + (tk_ring.waits(s2) if fh == 0 else []),
                                   inc=("AC", 1))
                        ppk_ring.done(fh, ("AC", vth))
                    vdvl = 0
                    for h in range(H):
                        acol = atm[:, ti * H + h: ti * H + h + 1]
                        vdvl = p.op("vector", "tensor_scalar",
                                    pks[:, (s2 % 2) * D + h * DH:(s2 % 2) * D + (h + 1) * DH],
                                    tktm[:, (s2 % 2) * D + h * DH:(s2 % 2) * D + (h + 1) * DH],
                                    acol, acol, ALU.mult, ALU.add,
                                    w=([("AC", vth)] + pks_ring.waits(s2)) if h == 0 else (),
                                    inc=("DV", 1))
                    tk_ring.done(s2, ("DV", vdvl))
                    first = (bi == 0 and s2 == 0)
                    last = (bi == NSUB - 1 and s2 == NSUB - 1)
                    kvb = pkv2[0]
                    for c in range(KC):
                        wl = [("DV", vdvl), lvv] + (pkv_ring.waits(0)
                                                    if c == 0 else [])
                        last_kv = p.op(
                            "tensor", "matmul", kvb[:, c * 128:(c + 1) * 128],
                            pks[:, (s2 % 2) * D + c * 128:(s2 % 2) * D + (c + 1) * 128],
                            vtm2[:, (blk % 2) * NSUB * D + s2 * D + c * 128:
                                 (blk % 2) * NSUB * D + s2 * D + (c + 1) * 128],
                            start=True, stop=True, skip_group_check=True,
                            w=wl if c == 0 else (), inc=("PE", 1))
                    vkva = p.op("vector", "tensor_tensor", kvacc[:], kvacc[:],
                                kvb[:], ALU.add, w=[("PE", last_kv), ("DV", vz_kv)],
                                inc=("DV", 1))
                    pkv_ring.done(0, ("DV", vkva))
                    for fh in range(2):
                        wl = []
                        if first and fh == 0:
                            wl.append(("DV", pks_free[0]))
                        last_ks = p.op(
                            "tensor", "matmul", pks_[0:1, fh * 512:(fh + 1) * 512],
                            ones16c[:],
                            pks[:, (s2 % 2) * D + fh * 512:(s2 % 2) * D + (fh + 1) * 512],
                            start=first, stop=last, skip_group_check=True,
                            w=wl, inc=("PE", 1))
                    pks_ring.done(s2, ("PE", max(last_kv, last_ks)))
                kfm_ring.done(blk, ("PE", p.val("PE")))
                vtm_ring.done(blk, ("PE", p.val("PE")))
            vks = p.op("vector", "tensor_copy", ksr[0:1, s * D:(s + 1) * D],
                       pks_[0:1, 0:1024], w=[("PE", last_ks)], inc=("DV", 1))
            pks_free[0] = vks
            for h in range(H):
                vks = p.op("vector", "scalar_tensor_tensor",
                           ksr[0:1, s * D + h * DH: s * D + (h + 1) * DH],
                           tk0[0:1, h * DH:(h + 1) * DH],
                           a0r[0:1, s * H + h: s * H + h + 1],
                           ksr[0:1, s * D + h * DH: s * D + (h + 1) * DH],
                           ALU.mult, ALU.add, w=[("DV", tk0v)], inc=("DV", 1))
            ptb_free = 0
            for c in range(KC):
                vdv = p.op("vector", "tensor_copy", ksr16[:],
                           ksr[0:1, s * D + c * 128: s * D + (c + 1) * 128],
                           w=[("DV", vks)], inc=("DV", 1))
                vpe = p.op("tensor", "matmul", ptB[:, 0:1], ksr16[:],
                           ones16c[0:1, 0:1], start=True, stop=True,
                           skip_group_check=True,
                           w=[("DV", vdv)] + ([("DV", ptb_free)] if ptb_free else []),
                           inc=("PE", 1))
                p.op("vector", "tensor_copy",
                     kse[0:64, (s * KC + c) * H + 2 * c:(s * KC + c) * H + 2 * c + 1],
                     ptB[0:64, 0:1], w=[("PE", vpe), ("DV", ksez)], inc=("DV", 1))
                ptb_free = p.op("vector", "tensor_copy",
                                kse[64:128, (s * KC + c) * H + 2 * c + 1:(s * KC + c) * H + 2 * c + 2],
                                ptB[64:128, 0:1], inc=("DV", 1))
            p.op("vector", "memset", kvsb[:, s * KC * 128:(s + 1) * KC * 128], 0.0,
                 inc=("DV", 1))
            for c in range(KC):
                p.op("vector", "tensor_copy",
                     kvsb[0:64, s * KC * 128 + c * 128: s * KC * 128 + c * 128 + 64],
                     kvacc[0:64, c * 128:c * 128 + 64], inc=("DV", 1))
                p.op("vector", "tensor_copy",
                     kvsb[64:128, s * KC * 128 + c * 128 + 64: s * KC * 128 + (c + 1) * 128],
                     kvacc[64:128, c * 128 + 64:(c + 1) * 128], inc=("DV", 1))
        p.gate_all()
        ctx.close()
        if ALV <= 3:
            octx.close()
            return

        p.phase = "A3a"
        ctx = ExitStack()
        tqb = ctx.enter_context(nc.sbuf_tensor("A3tq", [128, 2 * KC * BLK], BF16))
        atf = ctx.enter_context(nc.sbuf_tensor("A3at", [128, NSUB * D], F32))
        a16 = ctx.enter_context(nc.sbuf_tensor("A3a6", [128, D], BF16))
        aff = ctx.enter_context(nc.sbuf_tensor("A3af", [128, KC * BLK], BF16))
        rden = ctx.enter_context(nc.sbuf_tensor("A3rd", [128, H], F32))
        zed = ctx.enter_context(nc.sbuf_tensor("A3z", [128, 16], BF16))
        pnum = [ctx.enter_context(nc.psum_tensor(f"A3n{i}", [128, 1024], F32))
                for i in range(2)]
        pden = ctx.enter_context(nc.psum_tensor("A3d", [128, 512], F32))
        ptC = [ctx.enter_context(nc.psum_tensor(f"A3t{i}", [128, 1024], BF16))
               for i in range(2)]

        vz = p.op("vector", "memset", zed[:], 0.0, inc=("DV", 1))
        for s in range(2):
            for cr in (0, PADL - 2):
                p.dma("sync", "zpad",
                      SP_AF.rearrange("(c p) t -> p c t", p=128)[:, :, s * PADL + cr: s * PADL + cr + 2],
                      zed[:].rearrange("p (c t) -> p c t", t=2), w=[("DV", vz)])

        tq_ring = Ring(2)
        atf_ring = Ring(1)
        aff_ring = Ring(1)
        num_ring = Ring(2)
        den_free = [0]
        a16_free = [0]
        ptC_ring = Ring(2)
        for blk in range(NBLK):
            s = blk // NSUB
            lv = p.dma("sync", "tq%d" % (blk % 2),
                       tqb[:, (blk % 2) * KC * BLK:(blk % 2 + 1) * KC * BLK].rearrange(
                           "p (c t) -> p c t", t=BLK),
                       fmv(SP_TQ, blk), w=tq_ring.waits(blk))
            last_pe = 0
            for s2 in range(NSUB):
                ti = blk * NSUB + s2
                nb = pnum[s2 % 2]
                for c in range(KC):
                    wl = [lv] + (num_ring.waits(s2) if c == 0 else [])
                    c0 = (blk % 2) * KC * BLK + c * BLK + s2 * 128
                    last_pe = p.op(
                        "tensor", "matmul", nb[:, c * 128:(c + 1) * 128],
                        tqb[:, c0:c0 + 128],
                        kvsb[:, s * KC * 128 + c * 128: s * KC * 128 + (c + 1) * 128],
                        start=True, stop=True, skip_group_check=True,
                        w=wl if c == 0 else (), inc=("PE", 1))
                for kc in range(KC):
                    wl = [lv]
                    if kc == 0:
                        wl.append(("DV", den_free[0]))
                    vde = p.op("tensor", "matmul", pden[0:128, 0:16],
                               tqb[:, (blk % 2) * KC * BLK + kc * BLK + s2 * 128:
                                   (blk % 2) * KC * BLK + kc * BLK + (s2 + 1) * 128],
                               kse[:, (s * KC + kc) * H:(s * KC + kc + 1) * H],
                               start=(kc == 0), stop=(kc == KC - 1),
                               skip_group_check=True, w=wl, inc=("PE", 1))
                p.op("vector", "tensor_scalar", rden[:], pden[0:128, 0:16], EPS_A,
                     None, ALU.add, w=[("PE", vde)], inc=("DV", 1))
                vrd = p.op("vector", "reciprocal", rden[:], rden[:], inc=("DV", 1))
                den_free[0] = vrd
                vat = 0
                for h in range(H):
                    vat = p.op("vector", "tensor_scalar",
                               atf[:, s2 * D + h * DH: s2 * D + (h + 1) * DH],
                               nb[:, h * DH:(h + 1) * DH], rden[:, h:h + 1], None,
                               ALU.mult,
                               w=([("DV", vrd), ("PE", last_pe)] + (atf_ring.waits(0) if (s2 == 0 and h == 0) else [])),
                               inc=("DV", 1))
                num_ring.done(s2, ("DV", vat))
                vc = p.op("scalar", "activation", a16[:], atf[:, s2 * D:(s2 + 1) * D],
                          AF.Copy, w=[("DV", vat), ("PE", a16_free[0])],
                          inc=("AC", 1))
                vpe = 0
                for j in range(KC):
                    vpe = p.op("tensor", "transpose",
                               ptC[s2 % 2][:, j * 128:(j + 1) * 128],
                               a16[:, j * 128:(j + 1) * 128], ident16[:],
                               w=([("AC", vc)] + ptC_ring.waits(s2)) if j == 0 else (),
                               inc=("PE", 1))
                a16_free[0] = vpe
                vdve = p.op("vector", "tensor_copy",
                            aff[:].rearrange("p (c t) -> p c t", t=BLK)[:, :, s2 * 128:(s2 + 1) * 128],
                            ptC[s2 % 2][:].rearrange("p (c t) -> p c t", t=128),
                            w=[("PE", vpe)] + (aff_ring.waits(0) if s2 == 0 else []),
                            inc=("DV", 1))
                ptC_ring.done(s2, ("DV", vdve))
            tq_ring.done(blk, ("PE", p.val("PE")))
            vs1 = p.dma("sync", "atsp", tmv(SP_AT, blk),
                        atf[:].rearrange("p (c d) -> p c d", d=D),
                        w=[("AC", p.val("AC")), ("DV", p.val("DV"))])
            atf_ring.done(0, vs1)
            bi = blk % NSUB
            vs2 = p.dma("sync", "afsp",
                        SP_AF.rearrange("(c p) t -> p c t", p=128)[:, :, s * PADL + 2 + bi * BLK: s * PADL + 2 + (bi + 1) * BLK],
                        aff[:].rearrange("p (c t) -> p c t", t=BLK),
                        w=[("DV", p.val("DV"))])
            aff_ring.done(0, vs2)
        p.gate_all()
        ctx.close()
        if ALV <= 4:
            octx.close()
            return

        p.phase = "A3b"
        ctx = ExitStack()
        afw = ctx.enter_context(nc.sbuf_tensor("A3bw", [128, 2 * KC * 516], BF16))
        atb = ctx.enter_context(nc.sbuf_tensor("A3ba", [128, 2 * NSUB * D], F32))
        cw16 = ctx.enter_context(nc.sbuf_tensor("A3bc", [128, 40 * 128], BF16))
        loc16 = ctx.enter_context(nc.sbuf_tensor("A3bl", [128, 512], BF16))
        gamc = ctx.enter_context(nc.sbuf_tensor("A3bgc", [128, 1], F32))
        cbc = ctx.enter_context(nc.sbuf_tensor("A3bcb", [128, KC], F32))
        gcb = ctx.enter_context(nc.sbuf_tensor("A3bgb", [128, KC], F32))
        lnab = ctx.enter_context(nc.sbuf_tensor("A3bln", [128, 2 * D], F32))
        apr = ctx.enter_context(nc.sbuf_tensor("A3bap", [128, D], F32))
        apo = ctx.enter_context(nc.sbuf_tensor("A3bao", [128, NSUB * D], F32))
        ap16 = ctx.enter_context(nc.sbuf_tensor("A3ba6", [128, D], BF16))
        apf = ctx.enter_context(nc.sbuf_tensor("A3baf", [128, KC * BLK], BF16))
        stc = mkstats(ctx, "A3bs", cols=NSUB)
        plo = [ctx.enter_context(nc.psum_tensor(f"A3bp{i}", [128, 512], F32))
               for i in range(2)]
        ptD = [ctx.enter_context(nc.psum_tensor(f"A3bt{i}", [128, 512], BF16))
               for i in range(2)]
        ptE = ctx.enter_context(nc.psum_tensor("A3be", [128, 1024], BF16))

        cw_done = stream_w_cast(
            a_cw.rearrange("g p n -> p g n"),
            cw16[:].rearrange("p (g n) -> p g n", n=128))
        lv = p.dma("sync", "gam", gamc[:], a_gam)
        lv2 = p.dma("sync", "cbc", cbc[:], a_cbc)
        vgb = p.op("vector", "tensor_scalar", gcb[:], cbc[:], gamc[:, 0:1], None,
                   ALU.mult, w=[lv, lv2], inc=("DV", 1))
        ln_lv = p.dma("sync", "lna", lnab[:], a_lna_bc)

        afw_ring = Ring(2)
        atb_ring = Ring(2)
        loc_ring = Ring(1)
        plo_ring = Ring(2)
        ptD_ring = Ring(2)
        apo_ring = Ring(1)
        apf_ring = Ring(1)
        for blk in range(NBLK):
            s = blk // NSUB
            bi = blk % NSUB
            bb = blk % 2
            lva = p.dma("sync", "afw%d" % bb,
                        afw[:, bb * KC * 516:(bb + 1) * KC * 516].rearrange(
                            "p (c t) -> p c t", t=516),
                        SP_AF.rearrange("(c p) t -> p c t", p=128)[:, :, s * PADL + bi * BLK: s * PADL + bi * BLK + 516],
                        w=afw_ring.waits(blk))
            lvt = p.dma("sync", "atb%d" % bb,
                        atb[:, bb * NSUB * D:(bb + 1) * NSUB * D].rearrange(
                            "p (c d) -> p c d", d=D),
                        tmv(SP_AT, blk), w=atb_ring.waits(blk))
            for c in range(KC):
                bank = plo[c % 2]
                wl = [lva] + list(cw_done) + plo_ring.waits(c)
                for tp in range(5):
                    vpe = p.op("tensor", "matmul", bank[:],
                               cw16[:, (c * 5 + tp) * 128:(c * 5 + tp + 1) * 128],
                               afw[:, bb * KC * 516 + c * 516 + tp:
                                   bb * KC * 516 + c * 516 + tp + 512],
                               start=(tp == 0), stop=(tp == 4),
                               w=wl if tp == 0 else (), inc=("PE", 1))
                vlc = p.op("scalar", "activation", loc16[:], bank[:], AF.Identity,
                           scale=gamc[:, 0:1], bias=gcb[:, c:c + 1],
                           w=[("PE", vpe), ("DV", vgb)] + loc_ring.waits(0),
                           inc=("AC", 1))
                plo_ring.done(c, ("AC", vlc))
                vdd = 0
                for s2 in range(NSUB):
                    vpt = p.op("tensor", "transpose",
                               ptD[c % 2][:, s2 * 128:(s2 + 1) * 128],
                               loc16[:, s2 * 128:(s2 + 1) * 128], ident16[:],
                               w=([("AC", vlc)] + ptD_ring.waits(c)) if s2 == 0 else (),
                               inc=("PE", 1))
                    vdd = p.op("vector", "tensor_tensor",
                               atb[:, bb * NSUB * D + s2 * D + c * 128:
                                   bb * NSUB * D + s2 * D + (c + 1) * 128],
                               atb[:, bb * NSUB * D + s2 * D + c * 128:
                                   bb * NSUB * D + s2 * D + (c + 1) * 128],
                               ptD[c % 2][:, s2 * 128:(s2 + 1) * 128], ALU.add,
                               w=[("PE", vpt), lvt], inc=("DV", 1))
                loc_ring.done(0, ("PE", vpt))
                ptD_ring.done(c, ("DV", vdd))
            afw_ring.done(blk, ("PE", p.val("PE")))
            ptE_free = [0]
            for s2 in range(NSUB):
                tile = atb[:, bb * NSUB * D + s2 * D: bb * NSUB * D + (s2 + 1) * D]
                ln_stats(p, tile, stc["su"][:, s2:s2 + 1], stc["sq"][:, s2:s2 + 1],
                         [("DV", p.val("DV"))])
                vstat = ln_finish(stc, s2, 1)
                vap = p.op("scalar", "activation", apr[:], tile, AF.Identity,
                           scale=stc["si"][:, s2:s2 + 1], bias=stc["bu"][:, s2:s2 + 1],
                           w=[("DV", vstat)], inc=("AC", 1))
                p.op("vector", "tensor_tensor", apr[:], apr[:], lnab[:, 0:D],
                     ALU.mult, w=[("AC", vap), ln_lv], inc=("DV", 1))
                vdv = p.op("vector", "tensor_tensor", apo[:, s2 * D:(s2 + 1) * D],
                           apr[:], lnab[:, D:2 * D], ALU.add,
                           w=apo_ring.waits(0) if s2 == 0 else (), inc=("DV", 1))
                vc = p.op("scalar", "activation", ap16[:], apo[:, s2 * D:(s2 + 1) * D],
                          AF.Copy, w=[("DV", vdv)], inc=("AC", 1))
                vpe = 0
                for j in range(KC):
                    vpe = p.op("tensor", "transpose", ptE[:, j * 128:(j + 1) * 128],
                               ap16[:, j * 128:(j + 1) * 128], ident16[:],
                               w=([("AC", vc), ("DV", ptE_free[0])]) if j == 0 else (),
                               inc=("PE", 1))
                vdve = p.op("vector", "tensor_copy",
                            apf[:].rearrange("p (c t) -> p c t", t=BLK)[:, :, s2 * 128:(s2 + 1) * 128],
                            ptE[:].rearrange("p (c t) -> p c t", t=128),
                            w=[("PE", vpe)] + (apf_ring.waits(0) if s2 == 0 else []),
                            inc=("DV", 1))
                ptE_free[0] = vdve
            atb_ring.done(blk, ("AC", p.val("AC")))
            vs1 = p.dma("sync", "ptsp", tmv(SP_PT, blk),
                        apo[:].rearrange("p (c d) -> p c d", d=D),
                        w=[("DV", p.val("DV"))])
            apo_ring.done(0, vs1)
            vs2 = p.dma("sync", "pfsp", fmv(SP_PF, blk),
                        apf[:].rearrange("p (c t) -> p c t", t=BLK),
                        w=[("DV", p.val("DV"))])
            apf_ring.done(0, vs2)
        p.gate_all()
        ctx.close()
        if ALV <= 5:
            octx.close()
            return

        p.phase = "A3d"
        ctx = ExitStack()
        xtd = ctx.enter_context(nc.sbuf_tensor("A3dx", [128, 2 * NSUB * D], F32))
        apd = ctx.enter_context(nc.sbuf_tensor("A3da", [128, 2 * NSUB * D], F32))
        utl = ctx.enter_context(nc.sbuf_tensor("A3du", [128, D], F32))
        xtd_ring = Ring(2)
        apd_ring = Ring(2)
        for blk in range(NBLK):
            bb = blk % 2
            xv = p.dma("sync", "xtd%d" % bb,
                       xtd[:, bb * NSUB * D:(bb + 1) * NSUB * D].rearrange(
                           "p (c d) -> p c d", d=D),
                       tmv(src, blk), w=xtd_ring.waits(blk))
            av = p.dma("sync", "apd%d" % bb,
                       apd[:, bb * NSUB * D:(bb + 1) * NSUB * D].rearrange(
                           "p (c d) -> p c d", d=D),
                       tmv(SP_PT, blk), w=apd_ring.waits(blk))
            vdv = 0
            for s2 in range(NSUB):
                i = blk * NSUB + s2
                tile = xtd[:, bb * NSUB * D + s2 * D: bb * NSUB * D + (s2 + 1) * D]
                vu = p.op("scalar", "activation", utl[:], tile, AF.Identity,
                          scale=au[:, i:i + 1], bias=buc[:, i:i + 1],
                          w=[xv, ("DV", vdv)], inc=("AC", 1))
                p.op("vector", "tensor_tensor", tile, tile, utl[:], ALU.add,
                     w=[("AC", vu)], inc=("DV", 1))
                vdv = p.op("vector", "tensor_tensor", tile, tile,
                           apd[:, bb * NSUB * D + s2 * D: bb * NSUB * D + (s2 + 1) * D],
                           ALU.add, w=[av], inc=("DV", 1))
            sv = p.dma("sync", "xds%d" % bb, tmv(dst, blk),
                       xtd[:, bb * NSUB * D:(bb + 1) * NSUB * D].rearrange(
                           "p (c d) -> p c d", d=D),
                       w=[("DV", vdv)])
            xtd_ring.done(blk, sv)
            apd_ring.done(blk, ("DV", vdv))
        p.gate_all()
        ctx.close()
        octx.close()
        if ALV <= 6:
            return

        emit_mlp_pass("A3c", [dict(W1=aW1, W2=aW2, b1=a_b1c[0], b2=a_b2r[0],
                                   gain=None, wcol=lambda blk, sub: None)],
                      SP_PF, dst)

    def emit_attn_b(src, dst):
        p.phase = "Bprep"
        octx = ExitStack()
        wq16 = octx.enter_context(nc.sbuf_tensor("Bwq", [128, KC * D], BF16))
        wk16 = octx.enter_context(nc.sbuf_tensor("Bwk", [128, KC * D], BF16))
        wv16 = octx.enter_context(nc.sbuf_tensor("Bwv", [128, KC * D], BF16))
        wo16 = octx.enter_context(nc.sbuf_tensor("Bwo", [128, KC * D], BF16))
        ptd = octx.enter_context(nc.sbuf_tensor("Bpt", [128, 128], BF16))
        ptds = octx.enter_context(nc.sbuf_tensor("Bpts", [64, DH], F32))
        cbq = octx.enter_context(nc.sbuf_tensor("Bcbq", [128, KC], F32))
        cbk = octx.enter_context(nc.sbuf_tensor("Bcbk", [128, KC], F32))
        brow = octx.enter_context(nc.sbuf_tensor("Bbr", [1, 2 * D], BF16))
        brst = octx.enter_context(nc.sbuf_tensor("Bbs", [1, 2 * D], F32))
        au2 = octx.enter_context(nc.sbuf_tensor("Bau", [128, NT], F32))
        bu2 = octx.enter_context(nc.sbuf_tensor("Bbu", [128, NT], F32))
        kvbs = octx.enter_context(nc.sbuf_tensor("Bkv", [128, 2 * KC * 128], BF16))
        kseB = octx.enter_context(nc.sbuf_tensor("Bkse", [128, 2 * KC * H], BF16))
        ksc = octx.enter_context(nc.sbuf_tensor("Bksc", [128, KC], F32))
        ksc16 = octx.enter_context(nc.sbuf_tensor("Bks6", [128, KC], BF16))
        stc = mkstats(octx, "Bs")
        st4 = mkstats(octx, "B4", cols=4)

        wcB = lambda W, dst: stream_w_cast(
            W.rearrange("(c p) f -> p c f", p=128),
            dst[:].rearrange("p (c f) -> p c f", f=D))
        wq_done = wcB(bWqT, wq16)
        wk_done = wcB(bWkT, wk16)
        wv_done = wcB(bWvT, wv16)
        wo_done = wcB(bWoT, wo16)
        lv = p.dma("sync", "bpt", ptds[:], bPt)
        p.op("vector", "memset", ptd[:], 0.0, inc=("DV", 1))
        p.op("vector", "tensor_copy", ptd[0:64, 0:64], ptds[:], w=[lv],
             inc=("DV", 1))
        ptdv = p.op("vector", "tensor_copy", ptd[64:128, 64:128], ptds[:],
                    inc=("DV", 1))
        lv = p.dma("sync", "bbq", cbq[:], b_bqc)
        lv2 = p.dma("sync", "bbk", cbk[:], b_bkc)
        p.op("vector", "tensor_scalar", cbq[:], cbq[:], CNORM, None, ALU.mult,
             w=[lv, lv2], inc=("DV", 1))
        cbv = p.op("vector", "tensor_scalar", cbk[:], cbk[:], CNORM, None, ALU.mult,
                   inc=("DV", 1))
        lv = p.dma("sync", "bbv", brst[0:1, 0:D], b_bvr)
        lv2 = p.dma("sync", "bbo", brst[0:1, D:2 * D], b_bor)
        brv = p.op("vector", "tensor_copy", brow[:], brst[:],
                   w=[lv, lv2], inc=("DV", 1))
        ksbz = p.op("vector", "memset", kseB[:], 0.0, inc=("DV", 1))

        p.phase = "B1"
        ctx = ExitStack()
        xt = ctx.enter_context(nc.sbuf_tensor("B1x", [128, NSUB * D], F32))
        utm = ctx.enter_context(nc.sbuf_tensor("B1u", [128, 2 * D], BF16))
        ufm = ctx.enter_context(nc.sbuf_tensor("B1uf", [128, KC * BLK], BF16))
        kfsc = ctx.enter_context(nc.sbuf_tensor("B1kf", [128, KC * BLK], BF16))
        krel = ctx.enter_context(nc.sbuf_tensor("B1kr", [128, D], BF16))
        kftm = ctx.enter_context(nc.sbuf_tensor("B1kt", [128, 2 * D], BF16))
        vtm = ctx.enter_context(nc.sbuf_tensor("B1vt", [128, 2 * D], BF16))
        kvaccB = ctx.enter_context(nc.sbuf_tensor("B1kva", [128, KC * 128], F32))
        ksaccB = ctx.enter_context(nc.sbuf_tensor("B1ksa", [128, KC], F32))
        pt = [ctx.enter_context(nc.psum_tensor("B1t0", [128, 1024], BF16))]
        pj = [ctx.enter_context(nc.psum_tensor(f"B1j{i}", [128, 512], F32))
              for i in range(2)]
        pkf = ctx.enter_context(nc.psum_tensor("B1kp", [128, 1024], F32))
        pkv2B = ctx.enter_context(nc.psum_tensor("B1kv", [128, 1024], F32))
        pksC = ctx.enter_context(nc.psum_tensor("B1ks", [128, 512], F32))

        xt_ring = Ring(1)
        utm_ring = Ring(2)
        pt_ring = Ring(1)
        ufm_ring = Ring(1)
        pj_ring = Ring(2)
        kfsc_ring = Ring(1)
        pkf_free = [0]
        krel_free = [0]
        kftm_ring = Ring(2)
        vtm_ring = Ring(2)
        pkv_ringB = Ring(1)
        pks_ringB = Ring(1)

        def make_u(blk, stats_save, xv):
            _ml = 4
            for s2 in range(NSUB):
                i = blk * NSUB + s2
                tile = xt[:, s2 * D:(s2 + 1) * D]
                if stats_save:
                    ln_stats(p, tile, stc["su"][:, i:i + 1], stc["sq"][:, i:i + 1],
                             [xv])
            if _ml < 2:
                xt_ring.done(0, ("DV", p.val("DV")))
                return p.val("DV")
            if stats_save:
                vstat = ln_finish(stc, blk * NSUB, NSUB)
                sl = (slice(None), slice(blk * NSUB, blk * NSUB + NSUB))
                p.op("vector", "tensor_copy", au2[sl], stc["si"][sl],
                     w=[("DV", vstat)], inc=("DV", 1))
                p.op("vector", "tensor_copy", bu2[sl], stc["bu"][sl], inc=("DV", 1))
            if _ml < 3:
                xt_ring.done(0, ("DV", p.val("DV")))
                return p.val("DV")
            uev = 0
            for s2 in range(NSUB):
                i = blk * NSUB + s2
                vu = p.op("scalar", "activation", utm[:, (s2 % 2) * D:(s2 % 2 + 1) * D],
                          xt[:, s2 * D:(s2 + 1) * D], AF.Identity,
                          scale=au2[:, i:i + 1], bias=bu2[:, i:i + 1],
                          w=[xv, ("DV", p.val("DV"))] + utm_ring.waits(s2),
                          inc=("AC", 1))
                if _ml < 4:
                    utm_ring.done(s2, ("AC", vu))
                    uev = p.val("DV")
                    continue
                vpe = 0
                for j in range(KC):
                    vpe = p.op("tensor", "transpose", pt[0][:, j * 128:(j + 1) * 128],
                               utm[:, (s2 % 2) * D + j * 128:(s2 % 2) * D + (j + 1) * 128],
                               ident16[:],
                               w=([("AC", vu)] + pt_ring.waits(0)) if j == 0 else (),
                               inc=("PE", 1))
                utm_ring.done(s2, ("PE", vpe))
                uev = p.op("vector", "tensor_copy",
                           ufm[:].rearrange("p (c t) -> p c t", t=BLK)[:, :, s2 * 128:(s2 + 1) * 128],
                           pt[0][:].rearrange("p (c t) -> p c t", t=128),
                           w=[("PE", vpe)] + (ufm_ring.waits(0) if s2 == 0 else []),
                           inc=("DV", 1))
                pt_ring.done(0, ("DV", uev))
            p.wait_only("tensor", [("DV", uev)])
            return uev

        def fm_projection(w16, wdone, biascol, outfm, out_ring):
            vac = 0
            for j in range(KC):
                bank = pj[j % 2]
                wl = list(wdone) + [("DV", cbv)] + pj_ring.waits(j % 2)
                if j == 0:
                    wl += out_ring.waits(0)
                for kc in range(KC):
                    vpe = p.op("tensor", "matmul", bank[:],
                               w16[:, kc * D + j * 128: kc * D + (j + 1) * 128],
                               ufm[:, kc * BLK:(kc + 1) * BLK],
                               start=(kc == 0), stop=(kc == KC - 1),
                               w=wl if kc == 0 else (), inc=("PE", 1))
                vac = p.op("scalar", "activation", outfm[:, j * BLK:(j + 1) * BLK],
                           bank[:], AF.Identity, scale=c_cnorm, bias=biascol[:, j:j + 1],
                           w=[("PE", vpe)], inc=("AC", 1))
                pj_ring.done(j % 2, ("AC", vac))
            return vac

        for s in range(2 if "no_b1" not in DEBUG else 0):
            last_kv = 0
            last_ks = 0
            vz_kvB = p.op("vector", "memset", kvaccB[:], 0.0, inc=("DV", 1))
            p.op("vector", "memset", ksaccB[:], 0.0, inc=("DV", 1))
            for bi in range(NSUB):
                blk = s * NSUB + bi
                xv = p.dma("sync", "b1xt",
                           xt[:].rearrange("p (c d) -> p c d", d=D),
                           tmv(src, blk), w=xt_ring.waits(0))
                make_u(blk, True, xv)
                _lvl = 5
                if _lvl < 2:
                    xt_ring.done(0, ("AC", p.val("AC")))
                    ufm_ring.done(0, ("DV", p.val("DV")))
                    continue
                kv_ac = fm_projection(wk16, wk_done, cbk, kfsc, kfsc_ring)
                if _lvl < 3:
                    xt_ring.done(0, ("AC", p.val("AC")))
                    ufm_ring.done(0, ("PE", p.val("PE")))
                    kfsc_ring.done(0, ("AC", p.val("AC")))
                    continue
                for s2 in range(NSUB):
                    ti = blk * NSUB + s2
                    for c in range(KC):
                        wl = [("AC", kv_ac), ("DV", ptdv)]
                        if c == 0:
                            wl.append(("AC", pkf_free[0]))
                        vpe = p.op(
                            "tensor", "matmul", pkf[:, c * 128:(c + 1) * 128],
                            kfsc[:, c * BLK + s2 * 128: c * BLK + (s2 + 1) * 128],
                            ptd[:],
                            start=True, stop=True, skip_group_check=True,
                            w=wl if c == 0 else (), inc=("PE", 1))
                    vre = p.op("scalar", "activation", krel[:], pkf[:], AF.Relu,
                               w=[("PE", vpe), ("DV", krel_free[0])], inc=("AC", 1))
                    pkf_free[0] = vre
                    vkf = p.op("vector", "tensor_scalar",
                               kftm[:, (s2 % 2) * D:(s2 % 2 + 1) * D], krel[:],
                               EPS_A, None, ALU.add,
                               w=[("AC", vre)] + kftm_ring.waits(s2), inc=("DV", 1))
                    krel_free[0] = vkf
                    if _lvl < 4:
                        kftm_ring.done(s2, ("DV", vkf))
                        continue
                    vdv = 0
                    for fh in range(2):
                        bank = pj[fh]
                        wl = list(wv_done) + [("DV", brv)] + pj_ring.waits(fh)
                        p.op("tensor", "matmul", bank[:], ones16r[0:1, 0:128],
                             brow[0:1, fh * 512:(fh + 1) * 512],
                             start=True, stop=False, w=wl, inc=("PE", 1))
                        for kc in range(KC):
                            vpe = p.op("tensor", "matmul", bank[:],
                                       ufm[:, kc * BLK + s2 * 128: kc * BLK + (s2 + 1) * 128],
                                       wv16[:, kc * D + fh * 512: kc * D + (fh + 1) * 512],
                                       start=False, stop=(kc == KC - 1),
                                       inc=("PE", 1))
                        vdv = p.op("vector", "tensor_copy",
                                   vtm[:, (s2 % 2) * D + fh * 512:(s2 % 2) * D + (fh + 1) * 512],
                                   bank[:],
                                   w=[("PE", vpe)] + (vtm_ring.waits(s2) if fh == 0 else []),
                                   inc=("DV", 1))
                        pj_ring.done(fh, ("DV", vdv))
                    if _lvl < 5:
                        kftm_ring.done(s2, ("DV", vkf))
                        vtm_ring.done(s2, ("DV", vdv))
                        continue
                    first = (bi == 0 and s2 == 0)
                    for c in range(KC):
                        wl = [("DV", vkf), ("DV", vdv)] + (pkv_ringB.waits(0)
                                                           if c == 0 else [])
                        last_kv = p.op(
                            "tensor", "matmul", pkv2B[:, c * 128:(c + 1) * 128],
                            kftm[:, (s2 % 2) * D + c * 128:(s2 % 2) * D + (c + 1) * 128],
                            vtm[:, (s2 % 2) * D + c * 128:(s2 % 2) * D + (c + 1) * 128],
                            start=True, stop=True, skip_group_check=True,
                            w=wl if c == 0 else (), inc=("PE", 1))
                    for g in range(KC):
                        wl = pks_ringB.waits(0) if g == 0 else []
                        last_ks = p.op(
                            "tensor", "matmul", pksC[:, g * 4:g * 4 + 1],
                            kftm[:, (s2 % 2) * D + g * 128:(s2 % 2) * D + (g + 1) * 128],
                            ones16c[:],
                            start=True, stop=True, skip_group_check=True,
                            w=wl, inc=("PE", 1))
                    vkva = p.op("vector", "tensor_tensor", kvaccB[:], kvaccB[:],
                                pkv2B[:], ALU.add, w=[("PE", last_kv), ("DV", vz_kvB)],
                                inc=("DV", 1))
                    pkv_ringB.done(0, ("DV", vkva))
                    vksa = p.op("vector", "tensor_tensor", ksaccB[:], ksaccB[:],
                                pksC[:, 0:32].rearrange("p (g o) -> p g o", o=4)[:, :, 0],
                                ALU.add, w=[("PE", last_ks)], inc=("DV", 1))
                    pks_ringB.done(0, ("DV", vksa))
                    kftm_ring.done(s2, ("PE", max(last_kv, last_ks)))
                    vtm_ring.done(s2, ("PE", last_kv))
                xt_ring.done(0, ("AC", p.val("AC")))
                kfsc_ring.done(0, ("PE", p.val("PE")))
                ufm_ring.done(0, ("PE", p.val("PE")))
            vks = p.op("vector", "tensor_copy", ksc16[:], ksaccB[:], inc=("DV", 1))
            for c in range(KC):
                p.op("vector", "tensor_copy",
                     kseB[0:64, (s * KC + c) * H + 2 * c:(s * KC + c) * H + 2 * c + 1],
                     ksc16[0:64, c:c + 1], w=[("DV", ksbz)], inc=("DV", 1))
                p.op("vector", "tensor_copy",
                     kseB[64:128, (s * KC + c) * H + 2 * c + 1:(s * KC + c) * H + 2 * c + 2],
                     ksc16[64:128, c:c + 1], inc=("DV", 1))
            p.op("vector", "memset", kvbs[:, s * KC * 128:(s + 1) * KC * 128], 0.0,
                 inc=("DV", 1))
            for c in range(KC):
                p.op("vector", "tensor_copy",
                     kvbs[0:64, s * KC * 128 + c * 128: s * KC * 128 + c * 128 + 64],
                     kvaccB[0:64, c * 128:c * 128 + 64], inc=("DV", 1))
                p.op("vector", "tensor_copy",
                     kvbs[64:128, s * KC * 128 + c * 128 + 64: s * KC * 128 + (c + 1) * 128],
                     kvaccB[64:128, c * 128 + 64:(c + 1) * 128], inc=("DV", 1))
        p.gate_all()
        ctx.close()

        p.phase = "B2"
        if "no_b2" in DEBUG:
            octx.close()
            return
        ctx = ExitStack()
        xt2 = ctx.enter_context(nc.sbuf_tensor("B2x", [128, NSUB * D], F32))
        utm2 = ctx.enter_context(nc.sbuf_tensor("B2u", [128, 2 * D], BF16))
        ufm2 = ctx.enter_context(nc.sbuf_tensor("B2uf", [128, KC * BLK], BF16))
        qfsc = ctx.enter_context(nc.sbuf_tensor("B2qf", [128, KC * BLK], BF16))
        qrel = ctx.enter_context(nc.sbuf_tensor("B2qr", [128, 512], BF16))
        qf16 = ctx.enter_context(nc.sbuf_tensor("B2q6", [128, KC * BLK], BF16))
        at16 = ctx.enter_context(nc.sbuf_tensor("B2a6", [128, D], BF16))
        abfm = ctx.enter_context(nc.sbuf_tensor("B2ab", [128, KC * BLK], BF16))
        rdb = ctx.enter_context(nc.sbuf_tensor("B2rd", [128, H], F32))
        pt2 = [ctx.enter_context(nc.psum_tensor(f"B2t{i}", [128, 1024], BF16))
               for i in range(2)]
        pj2 = [ctx.enter_context(nc.psum_tensor(f"B2j{i}", [128, 512], F32))
               for i in range(2)]
        pqf = ctx.enter_context(nc.psum_tensor("B2qp", [128, 512], F32))
        pnb = ctx.enter_context(nc.psum_tensor("B2n", [128, 1024], F32))
        pdb = ctx.enter_context(nc.psum_tensor("B2d", [128, 512], F32))

        xt_ring2 = Ring(1)
        utm_ring2 = Ring(2)
        pt_ring2 = Ring(2)
        ufm_ring2 = Ring(1)
        pj_ring2 = Ring(2)
        qfsc_ring = Ring(1)
        pqf_free = [0]
        qrel_free = [0]
        qf16_ring = Ring(1)
        pnb_free = [0]
        pdb_free = [0]
        at16_free = [0]
        abfm_ring = Ring(1)
        for blk in range(NBLK):
            s = blk // NSUB
            xv = p.dma("sync", "b2xt",
                       xt2[:].rearrange("p (c d) -> p c d", d=D),
                       tmv(src, blk), w=xt_ring2.waits(0))
            uev = 0
            for s2 in range(NSUB):
                i = blk * NSUB + s2
                vu = p.op("scalar", "activation", utm2[:, (s2 % 2) * D:(s2 % 2 + 1) * D],
                          xt2[:, s2 * D:(s2 + 1) * D], AF.Identity,
                          scale=au2[:, i:i + 1], bias=bu2[:, i:i + 1],
                          w=[xv] + utm_ring2.waits(s2), inc=("AC", 1))
                vpe = 0
                for j in range(KC):
                    vpe = p.op("tensor", "transpose", pt2[s2 % 2][:, j * 128:(j + 1) * 128],
                               utm2[:, (s2 % 2) * D + j * 128:(s2 % 2) * D + (j + 1) * 128],
                               ident16[:],
                               w=([("AC", vu)] + pt_ring2.waits(s2)) if j == 0 else (),
                               inc=("PE", 1))
                utm_ring2.done(s2, ("PE", vpe))
                uev = p.op("vector", "tensor_copy",
                           ufm2[:].rearrange("p (c t) -> p c t", t=BLK)[:, :, s2 * 128:(s2 + 1) * 128],
                           pt2[s2 % 2][:].rearrange("p (c t) -> p c t", t=128),
                           w=[("PE", vpe)] + (ufm_ring2.waits(0) if s2 == 0 else []),
                           inc=("DV", 1))
                pt_ring2.done(s2, ("DV", uev))
            p.wait_only("tensor", [("DV", uev)])
            vac = 0
            for j in range(KC):
                bank = pj2[j % 2]
                wl = list(wq_done) + [("DV", cbv)] + pj_ring2.waits(j % 2)
                if j == 0:
                    wl += qfsc_ring.waits(0)
                for kc in range(KC):
                    vpe = p.op("tensor", "matmul", bank[:],
                               wq16[:, kc * D + j * 128: kc * D + (j + 1) * 128],
                               ufm2[:, kc * BLK:(kc + 1) * BLK],
                               start=(kc == 0), stop=(kc == KC - 1),
                               w=wl if kc == 0 else (), inc=("PE", 1))
                vac = p.op("scalar", "activation", qfsc[:, j * BLK:(j + 1) * BLK],
                           bank[:], AF.Identity, scale=c_cnorm, bias=cbq[:, j:j + 1],
                           w=[("PE", vpe)], inc=("AC", 1))
                pj_ring2.done(j % 2, ("AC", vac))
            vqf = 0
            for c in range(KC):
                wl = [("AC", vac), ("DV", ptdv), ("AC", pqf_free[0])]
                vpe = p.op("tensor", "matmul", pqf[:, 0:512],
                           ptd[:], qfsc[:, c * BLK:(c + 1) * BLK],
                           start=True, stop=True, skip_group_check=True,
                           w=wl, inc=("PE", 1))
                vre = p.op("scalar", "activation", qrel[:], pqf[:], AF.Relu,
                           w=[("PE", vpe), ("DV", qrel_free[0])], inc=("AC", 1))
                pqf_free[0] = vre
                qrel_free[0] = p.op("vector", "tensor_scalar",
                                    qf16[:, c * BLK:(c + 1) * BLK], qrel[:],
                                    EPS_A, None, ALU.add,
                                    w=[("AC", vre)] + (qf16_ring.waits(0) if c == 0 else []),
                                    inc=("DV", 1))
            vqf = qrel_free[0]
            vwo_dv = 0
            for s2 in range(NSUB):
                for c in range(KC):
                    wl = [("DV", vqf)]
                    if c == 0:
                        wl.append(("DV", pnb_free[0]))
                    vnm = p.op(
                        "tensor", "matmul", pnb[:, c * 128:(c + 1) * 128],
                        qf16[:, c * BLK + s2 * 128: c * BLK + (s2 + 1) * 128],
                        kvbs[:, s * KC * 128 + c * 128: s * KC * 128 + (c + 1) * 128],
                        start=True, stop=True, skip_group_check=True,
                        w=wl if c == 0 else (), inc=("PE", 1))
                for kc in range(KC):
                    wl = [("DV", vqf)]
                    if kc == 0:
                        wl.append(("DV", pdb_free[0]))
                    vde = p.op("tensor", "matmul", pdb[0:128, 0:16],
                               qf16[:, kc * BLK + s2 * 128: kc * BLK + (s2 + 1) * 128],
                               kseB[:, (s * KC + kc) * H:(s * KC + kc + 1) * H],
                               start=(kc == 0), stop=(kc == KC - 1),
                               skip_group_check=True, w=wl, inc=("PE", 1))
                p.op("vector", "tensor_scalar", rdb[:], pdb[0:128, 0:16], EPS_A,
                     None, ALU.add, w=[("PE", vde)], inc=("DV", 1))
                vrd = p.op("vector", "reciprocal", rdb[:], rdb[:], inc=("DV", 1))
                pdb_free[0] = vrd
                vat = 0
                for h in range(H):
                    vat = p.op("vector", "tensor_scalar",
                               at16[:, h * DH:(h + 1) * DH],
                               pnb[:, h * DH:(h + 1) * DH], rdb[:, h:h + 1], None,
                               ALU.mult,
                               w=([("DV", vrd), ("PE", vnm), ("PE", at16_free[0])] if h == 0 else ()),
                               inc=("DV", 1))
                pnb_free[0] = vat
                vpe = 0
                for j in range(KC):
                    vpe = p.op("tensor", "transpose",
                               pt2[s2 % 2][:, j * 128:(j + 1) * 128],
                               at16[:, j * 128:(j + 1) * 128], ident16[:],
                               w=([("DV", vat)] + pt_ring2.waits(s2)) if j == 0 else (),
                               inc=("PE", 1))
                at16_free[0] = vpe
                vab = p.op("vector", "tensor_copy",
                           abfm[:].rearrange("p (c t) -> p c t", t=BLK)[:, :, s2 * 128:(s2 + 1) * 128],
                           pt2[s2 % 2][:].rearrange("p (c t) -> p c t", t=128),
                           w=[("PE", vpe)] + (abfm_ring.waits(0) if s2 == 0 else []),
                           inc=("DV", 1))
                pt_ring2.done(s2, ("DV", vab))
                for fh in range(2):
                    bank = pj2[fh]
                    wl = list(wo_done) + [("DV", brv), ("DV", vab)]
                    wl += pj_ring2.waits(fh)
                    p.op("tensor", "matmul", bank[:], ones16r[0:1, 0:128],
                         brow[0:1, D + fh * 512: D + (fh + 1) * 512],
                         start=True, stop=False, w=wl, inc=("PE", 1))
                    for kc in range(KC):
                        vpe = p.op("tensor", "matmul", bank[:],
                                   abfm[:, kc * BLK + s2 * 128: kc * BLK + (s2 + 1) * 128],
                                   wo16[:, kc * D + fh * 512: kc * D + (fh + 1) * 512],
                                   start=False, stop=(kc == KC - 1), inc=("PE", 1))
                    dsl = xt2[:, s2 * D + fh * 512: s2 * D + (fh + 1) * 512]
                    vwo_dv = p.op("vector", "scalar_tensor_tensor", dsl, bank[:],
                                  1.0, dsl, ALU.mult, ALU.add,
                                  w=[("PE", vpe)], inc=("DV", 1))
                    pj_ring2.done(fh, ("DV", vwo_dv))
            abfm_ring.done(0, ("PE", p.val("PE")))
            qf16_ring.done(0, ("PE", p.val("PE")))
            qfsc_ring.done(0, ("PE", p.val("PE")))
            ufm_ring2.done(0, ("PE", p.val("PE")))
            sv = p.dma("sync", "b2st", tmv(dst, blk),
                       xt2[:].rearrange("p (c d) -> p c d", d=D),
                       w=[("DV", vwo_dv)])
            xt_ring2.done(0, sv)
        p.gate_all()
        ctx.close()
        octx.close()

    srcs = []
    cur = x_in
    for i, ph in enumerate(seq):
        dst = y_out if i == len(seq) - 1 else XR
        if ph == "ffn":
            emit_ffn(f"F{i}_", cur, dst)
        elif ph == "attn_a":
            emit_attn_a(cur, dst)
        elif ph == "attn_b":
            emit_attn_b(cur, dst)
        cur = dst

    p.wait_only("sync", [(s, v) for s, v in p.cnt.items() if v > 0])
    p.finish()
    p.ctx.close()
    return nc


SEQ = ("attn_a", "ffn", "attn_b", "ffn")


def prep_inputs(inputs):
    g = lambda n: np.ascontiguousarray(np.asarray(inputs[n], dtype=np.float32))
    cols = lambda v: np.ascontiguousarray(np.asarray(v, np.float32).reshape(-1, 128).T)

    def interleave(gv, bv):
        arr = np.empty((128, 2 * KC), np.float32)
        arr[:, 0::2] = cols(gv)
        arr[:, 1::2] = cols(bv)
        return np.ascontiguousarray(arr)

    def bc2(gv, bv):
        out = np.empty((128, 2 * D), np.float32)
        out[:, 0:D] = np.asarray(gv, np.float32)[None, :]
        out[:, D:2 * D] = np.asarray(bv, np.float32)[None, :]
        return np.ascontiguousarray(out)

    cm = {}
    cm["aWq"] = g("L0_attn_a_Wq"); cm["aWk"] = g("L0_attn_a_Wk")
    cm["aWv"] = g("L0_attn_a_Wv")
    cm["aWpq"] = g("L0_attn_a_Wphi_q"); cm["aWpk"] = g("L0_attn_a_Wphi_k")
    cm["a_brows"] = np.concatenate(
        [g("L0_attn_a_bq"), g("L0_attn_a_bk"), g("L0_attn_a_bv"),
         g("L0_attn_a_bphi_q"), g("L0_attn_a_bphi_k")]).reshape(1, 5 * D)
    cm["a_lnq"] = interleave(inputs["L0_attn_a_lnq_g"], inputs["L0_attn_a_lnq_b"])
    cm["a_lnk"] = interleave(inputs["L0_attn_a_lnk_g"], inputs["L0_attn_a_lnk_b"])
    cm["a_lnv_bc"] = bc2(inputs["L0_attn_a_lnv_g"], inputs["L0_attn_a_lnv_b"])
    cm["a_lna_bc"] = bc2(inputs["L0_attn_a_lna_g"], inputs["L0_attn_a_lna_b"])
    cm["aW1"] = g("L0_attn_a_W1")
    cm["a_b1c"] = np.ascontiguousarray(
        g("L0_attn_a_b1").reshape(1, HC, 128).transpose(0, 2, 1))
    cm["aW2"] = g("L0_attn_a_W2")
    cm["a_b2r"] = g("L0_attn_a_b2").reshape(1, 1, D)
    cm["a_gam"] = np.ascontiguousarray(
        np.broadcast_to(np.asarray(inputs["L0_attn_a_gamma"], np.float32), (128, 1)))
    cm["a_cbc"] = cols(inputs["L0_attn_a_conv_b"])
    w4 = g("L0_attn_a_conv_w").reshape(H, DH, DH, 5)
    cw = np.zeros((KC * 5, 128, 128), np.float32)
    for c in range(KC):
        for t in range(5):
            for par in range(2):
                cw[c * 5 + t, par * 64:(par + 1) * 64, par * 64:(par + 1) * 64] = \
                    w4[2 * c + par, :, :, t].T
    cm["a_cw"] = cw
    cm["bWqT"] = np.ascontiguousarray(g("L0_attn_b_Wq").T)
    cm["bWkT"] = np.ascontiguousarray(g("L0_attn_b_Wk").T)
    cm["bWvT"] = np.ascontiguousarray(g("L0_attn_b_Wv").T)
    cm["bWoT"] = np.ascontiguousarray(g("L0_attn_b_Wo").T)
    cm["b_bqc"] = cols(inputs["L0_attn_b_bq"])
    cm["b_bkc"] = cols(inputs["L0_attn_b_bk"])
    cm["b_bvr"] = g("L0_attn_b_bv").reshape(1, D)
    cm["b_bor"] = g("L0_attn_b_bo").reshape(1, D)
    cm["bPt"] = np.ascontiguousarray(g("L0_attn_b_P").T)
    cm["fW1s"] = g("L0_ffn_W1_s"); cm["fW2s"] = g("L0_ffn_W2_s")
    cm["fb1s"] = np.ascontiguousarray(
        g("L0_ffn_b1_s").reshape(2, HC, 128).transpose(0, 2, 1))
    cm["fb2s"] = g("L0_ffn_b2_s").reshape(2, 1, D)
    cm["fW1r"] = g("L0_ffn_W1_r"); cm["fW2r"] = g("L0_ffn_W2_r")
    cm["fb1r"] = np.ascontiguousarray(
        g("L0_ffn_b1_r").reshape(8, HC, 128).transpose(0, 2, 1))
    cm["fb2r"] = g("L0_ffn_b2_r").reshape(8, 1, D)
    cm["f_gs"] = cols(inputs["L0_ffn_g_shared"])
    cm["f_gr"] = cols(inputs["L0_ffn_g_routed"])
    cm["f_cen"] = g("L0_ffn_centroids")
    cm["c_ident"] = np.eye(128, dtype=np.float32)
    cm["c_onesr"] = np.ones((1, 512), np.float32)
    x = g("L0_attn_a_x")
    xs = [np.ascontiguousarray(x[2 * c:2 * c + 2].reshape(T, D))
          for c in range(NCORES)]
    return cm, xs


def kernel(**inputs):
    cm, xs = prep_inputs(inputs)
    nc = build_nc(SEQ)
    in_maps = [dict(cm, x=xs[c]) for c in range(NCORES)]
    res = run_bass_kernel_spmd(nc, in_maps, core_ids=list(range(NCORES)))
    out = np.concatenate([np.asarray(r["y"]).reshape(2, L, D)
                          for r in res.results], axis=0)
    return out
